# version 08
# speedup vs baseline: 1.6691x; 1.0993x over previous
import math
from contextlib import ExitStack

import numpy as np
import ml_dtypes
import concourse.bass as bass
import concourse.mybir as mybir
from concourse.bass_utils import run_bass_kernel_spmd

F32 = mybir.dt.float32
BF16 = mybir.dt.bfloat16
I32 = mybir.dt.int32
AF = mybir.ActivationFunctionType
ALU = mybir.AluOpType
AX = mybir.AxisListType
BF = ml_dtypes.bfloat16

NCORE = 8
D = 2048
SEQ = 8192
TPC = SEQ // NCORE
NH = 16
DH = 128
NC = D // 128
INTER = 5504
OPC = INTER // NCORE
NEXP = 16
IEXP = 1024
NEG = -30000.0


class Buf:
    __slots__ = ("h", "w", "r")

    def __init__(self, h):
        self.h = h
        self.w = None
        self.r = []

    def __getitem__(self, k):
        return self.h[k]

    def rearrange(self, *a, **kw):
        return self.h.rearrange(*a, **kw)

    @property
    def ap(self):
        return self.h[:]


class Prog:
    ENGINES = ("sync", "tensor", "vector", "scalar", "gpsimd")

    NPOOL = 6

    def __init__(self, nc):
        self.nc = nc
        self.es = ExitStack()
        self.sem = {e: self.es.enter_context(nc.semaphore(f"S{e}"))
                    for e in self.ENGINES}
        self.cnt = {e: 0 for e in self.ENGINES}
        self.q = {e: [] for e in self.ENGINES}
        self.hw = {e: {} for e in self.ENGINES}
        self.dpool = {e: [self.es.enter_context(nc.semaphore(f"D{e}{i}"))
                          for i in range(self.NPOOL)]
                      for e in ("sync", "gpsimd", "scalar", "vector")}
        self.dcnt = {e: [0] * self.NPOOL for e in self.dpool}
        self.di = {e: 0 for e in self.dpool}

    def sbuf(self, name, shape, dtype):
        return Buf(self.es.enter_context(self.nc.sbuf_tensor(name, shape, dtype)))

    def psum(self, name, shape):
        return Buf(self.es.enter_context(self.nc.psum_tensor(name, shape, F32)))

    def _wait(self, e, tok):
        sem, val = tok
        key = sem.name
        if self.hw[e].get(key, -1) >= val:
            return
        self.hw[e][key] = val
        self.q[e].append(("w", sem, val))

    def op(self, e, fn, ins=(), outs=(), dma=False, tag=""):
        deps = []
        for b in ins:
            if b.w is not None:
                deps.append(b.w)
        for b in outs:
            if b.w is not None:
                deps.append(b.w)
            deps.extend(b.r)
        for t in deps:
            self._wait(e, t)
        if dma:
            slot = self.di[e] % self.NPOOL
            self.di[e] += 1
            psem = self.dpool[e][slot]
            if self.dcnt[e][slot]:
                self._wait(e, (psem, self.dcnt[e][slot]))
            n_dma = getattr(fn, "n_dma", 1)
            self.dcnt[e][slot] += 16 * n_dma
            tok = (psem, self.dcnt[e][slot])
            self.q[e].append(("om" if n_dma > 1 else "o", fn, (psem, 16), tag))
            for b in ins:
                b.r.append(tok)
            for b in outs:
                b.w = tok
                b.r = []
            return tok
        amt = 1
        self.cnt[e] += amt
        tok = (self.sem[e], self.cnt[e])
        self.q[e].append(("o", fn, (self.sem[e], amt), tag))
        for b in ins:
            b.r.append(tok)
        for b in outs:
            b.w = tok
            b.r = []
        return tok

    def barrier_token(self, e):
        return (self.sem[e], self.cnt[e])

    def barrier(self):
        toks = [(self.sem[e], self.cnt[e]) for e in self.ENGINES
                if self.cnt[e] > 0]
        for e in self.dpool:
            for s, c in zip(self.dpool[e], self.dcnt[e]):
                if c:
                    toks.append((s, c))
        for s, c in getattr(self, "ccs_used", []):
            toks.append((s, c))
        for e in self.ENGINES:
            for t in toks:
                self._wait(e, t)

    from contextlib import contextmanager as _cm

    @_cm
    def scope(self):
        es = ExitStack()
        outer = self.es
        self.es = es
        try:
            yield
        finally:
            self.barrier()
            es.close()
            self.es = outer

    def cc(self, fn, ins=(), outs=()):
        if not hasattr(self, "ccsems"):
            self.ccsems = []
            self.ccs_used = []
        sem = self.nc.semaphore(f"CC{len(self.ccsems)}").__enter__()
        self.ccsems.append(sem)
        deps = []
        for b in ins:
            if b.w is not None:
                deps.append(b.w)
        for b in outs:
            if b.w is not None:
                deps.append(b.w)
            deps.extend(b.r)
        for t in deps:
            self._wait("gpsimd", t)
        tok = (sem, 1)
        self.ccs_used.append(tok)
        self.q["gpsimd"].append(("o", fn, (sem, 1), "cc"))
        for b in ins:
            b.r.append(tok)
        for b in outs:
            b.w = tok
            b.r = []
        return tok

    def finish(self, out_tokens):
        for t in out_tokens:
            self._wait("sync", t)

    def build(self):
        with self.nc.Block() as block:
            for e in self.ENGINES:
                items = self.q[e]

                def run(eng, items=items):
                    for it in items:
                        if it[0] == "w":
                            eng.wait_ge(it[1], it[2])
                        elif it[0] == "om":
                            insts = it[1](eng)
                            sem, amt = it[2]
                            for inst in insts:
                                inst.then_inc(sem, amt)
                        else:
                            last = it[1](eng)
                            if isinstance(last, list):
                                last = last[-1]
                            sem, amt = it[2]
                            last.then_inc(sem, amt)
                getattr(block, e)(run)
        self.es.close()


def _deint_perm():
    p = np.empty(128, np.int64)
    p[:64] = np.arange(64) * 2
    p[64:] = np.arange(64) * 2 + 1
    return p


def host_prep(inputs):
    h = {}
    x = np.asarray(inputs["L0_attn_x"])
    h["xT"] = np.ascontiguousarray(x[0].T)

    perm = _deint_perm()
    wqkvT = np.ascontiguousarray(np.asarray(inputs["L0_attn_W_qkv"]).T)
    wq = wqkvT[:, :2048].reshape(2048, NH, DH)[:, :, perm].reshape(2048, 2048)
    wk = wqkvT[:, 2048:4096].reshape(2048, NH, DH)[:, :, perm].reshape(2048, 2048)
    wv = wqkvT[:, 4096:]
    h["wqkvTp"] = np.ascontiguousarray(np.concatenate([wq, wk, wv], axis=1))
    h["woT"] = np.ascontiguousarray(np.asarray(inputs["L0_attn_W_o"]).T)

    h["lutrep"] = np.ascontiguousarray(
        np.broadcast_to(np.asarray(inputs["L0_ffn_lut"]), (128, 16))).astype(np.float32)
    h["gidxT"] = np.ascontiguousarray(np.asarray(inputs["L0_ffn_gate_idx"]).T)
    h["uidxT"] = np.ascontiguousarray(np.asarray(inputs["L0_ffn_up_idx"]).T)
    h["didxT"] = np.ascontiguousarray(np.asarray(inputs["L0_ffn_down_idx"]).T)

    h["wq1"] = np.ascontiguousarray(np.asarray(inputs["L1_attn_wq"]).reshape(2048, 2048))
    h["wk1"] = np.ascontiguousarray(np.asarray(inputs["L1_attn_wk"]).reshape(2048, 2048))
    h["wv1"] = np.ascontiguousarray(np.asarray(inputs["L1_attn_wv"]).reshape(2048, 2048))
    h["wo1"] = np.ascontiguousarray(np.asarray(inputs["L1_attn_wo"]).reshape(2048, 2048))
    h["qscale"] = np.asarray(inputs["L1_attn_q_scale"]).reshape(128, 1).astype(np.float32)
    h["kscale"] = np.asarray(inputs["L1_attn_k_scale"]).reshape(128, 1).astype(np.float32)
    h["gwT"] = np.ascontiguousarray(np.asarray(inputs["L1_ffn_gate_w"]).T)
    h["wg1"] = np.asarray(inputs["L1_ffn_wg"])
    h["wu1"] = np.asarray(inputs["L1_ffn_wu"])
    h["wd1"] = np.asarray(inputs["L1_ffn_wd"])

    h["ones_row_bf"] = np.ones((1, 128), BF)
    h["ones_col_bf"] = np.ones((128, 1), BF)
    h["ones_col_f"] = np.ones((128, 1), np.float32)
    h["ones_row_f"] = np.ones((1, 128), np.float32)
    h["ident_f"] = np.eye(128, dtype=np.float32)
    h["ident_bf"] = np.eye(128, dtype=BF)

    inv_freq = (10000.0 ** (-np.arange(0, DH, 2) / DH)).astype(np.float64)
    pos = np.arange(SEQ, dtype=np.float64)
    ang = pos[:, None] * inv_freq[None, :]
    cos = np.cos(ang).T
    sin = np.sin(ang).T
    cos128 = np.concatenate([cos, cos], 0)
    sinx = np.concatenate([-sin, sin], 0)
    sc = 1.0 / math.sqrt(DH)
    h["cos_k"] = cos128.astype(np.float32)
    h["sinx_k"] = sinx.astype(np.float32)
    h["cos_q"] = (cos128 * sc).astype(np.float32)
    h["sinx_q"] = (sinx * sc).astype(np.float32)

    f = np.arange(512)
    p = np.arange(128)
    masks = np.zeros((NCORE, 2, 64, 128, 512), BF)
    for c in range(NCORE):
        for qb in range(2):
            QB = c * TPC + qb * 512
            for kt in range(64):
                Dd = 128 * kt - QB
                if Dd + 127 <= 0:
                    continue
                m = np.where(f[None, :] >= (Dd + p[:, None]), 0.0, NEG)
                masks[c, qb, kt] = m.astype(BF)
    h["masks"] = masks
    return h


def build_l0a():
    nc = bass.Bass("TRN2", target_bir_lowering=False)
    d_x = nc.dram_tensor("xT_c", [D, TPC], F32, kind="ExternalInput")
    d_w = nc.dram_tensor("wqkvTp", [D, 3 * D], F32, kind="ExternalInput")
    d_cq = nc.dram_tensor("cos_q", [128, TPC], F32, kind="ExternalInput")
    d_sq = nc.dram_tensor("sinx_q", [128, TPC], F32, kind="ExternalInput")
    d_ck = nc.dram_tensor("cos_k", [128, TPC], F32, kind="ExternalInput")
    d_sk = nc.dram_tensor("sinx_k", [128, TPC], F32, kind="ExternalInput")
    d_onecf = nc.dram_tensor("ones_col_f", [128, 1], F32, kind="ExternalInput")
    d_onerf = nc.dram_tensor("ones_row_f", [1, 128], F32, kind="ExternalInput")
    d_onecb = nc.dram_tensor("ones_col_bf", [128, 1], BF16, kind="ExternalInput")

    d_xn = nc.dram_tensor("xn", [D, TPC], BF16, kind="ExternalOutput")
    d_q = nc.dram_tensor("qf", [D, TPC], BF16, kind="ExternalOutput")
    d_k = nc.dram_tensor("kf", [D, TPC], BF16, kind="ExternalOutput")
    d_v = nc.dram_tensor("vt", [TPC, D], BF16, kind="ExternalOutput")
    d_k2 = nc.dram_tensor("k2m", [1, NH], F32, kind="ExternalOutput")

    p = Prog(nc)
    out_toks = []

    xc = [p.sbuf(f"x{c}", [128, TPC], F32) for c in range(NC)]
    xnc = [p.sbuf(f"xn{c}", [128, TPC], BF16) for c in range(NC)]
    sqc = p.sbuf("sqc", [128, TPC], F32)
    cq = p.sbuf("cq", [128, TPC], F32)
    sq_t = p.sbuf("sq_t", [128, TPC], F32)
    ck = p.sbuf("ck", [128, TPC], F32)
    sk = p.sbuf("sk", [128, TPC], F32)
    onecf = p.sbuf("onecf", [128, 1], F32)
    onerf = p.sbuf("onerf", [1, 128], F32)
    onecb = p.sbuf("onecb", [128, 1], BF16)
    rowA = p.sbuf("rowA", [1, TPC], F32)
    rowB = p.sbuf("rowB", [1, TPC], F32)
    rowC = p.sbuf("rowC", [1, TPC], F32)
    tmpc = p.sbuf("tmpc", [128, TPC], F32)
    wslab = [p.sbuf(f"wslab{i}", [128, NC * 128], BF16) for i in range(2)]
    vslab = [p.sbuf(f"vslab{i}", [128, NC * 512], BF16) for i in range(2)]
    qkh = [p.sbuf(f"qkh{i}", [128, TPC], BF16) for i in range(2)]
    ropetmp = p.sbuf("ropetmp", [128, 512], F32)
    ropecos = p.sbuf("ropecos", [128, 512], F32)
    vtt = [p.sbuf(f"vtt{i}", [128, 512], BF16) for i in range(2)]
    k2sq = p.sbuf("k2sq", [128, TPC], BF16)
    k2row = p.sbuf("k2row", [1, NH], F32)

    T1 = p.psum("T1", [128, TPC])
    T2 = p.psum("T2", [128, TPC])
    T3 = p.psum("T3", [128, 512])
    T4 = p.psum("T4", [128, 512])
    ps_sum, ps_sq = T1, T2
    ps_b, ps_b2 = T1, T2
    ps_qk = [T3, T4]
    ps_v = [T3, T4]
    ps_k2 = T2

    for c in range(NC):
        p.op("sync", lambda e, c=c: e.dma_start(xc[c].ap, d_x[c * 128:(c + 1) * 128, :]),
             outs=[xc[c]], dma=True)
    p.op("sync", lambda e: e.dma_start(cq.ap, d_cq[:]), outs=[cq], dma=True)
    p.op("sync", lambda e: e.dma_start(sq_t.ap, d_sq[:]), outs=[sq_t], dma=True)
    p.op("sync", lambda e: e.dma_start(ck.ap, d_ck[:]), outs=[ck], dma=True)
    p.op("sync", lambda e: e.dma_start(sk.ap, d_sk[:]), outs=[sk], dma=True)
    p.op("sync", lambda e: e.dma_start(onecf.ap, d_onecf[:]), outs=[onecf], dma=True)
    p.op("sync", lambda e: e.dma_start(onerf.ap, d_onerf[:]), outs=[onerf], dma=True)
    p.op("sync", lambda e: e.dma_start(onecb.ap, d_onecb[:]), outs=[onecb], dma=True)

    for c in range(NC):
        p.op("scalar", lambda e, c=c: e.activation(sqc.ap, xc[c].ap, AF.Square),
             ins=[xc[c]], outs=[sqc])
        def mm_sum(e, c=c):
            last = None
            for tb in range(2):
                last = e.matmul(ps_sum[0:1, tb * 512:(tb + 1) * 512], onecf.ap,
                                xc[c][:, tb * 512:(tb + 1) * 512],
                                start=(c == 0), stop=(c == NC - 1))
            return last
        p.op("tensor", mm_sum, ins=[onecf, xc[c]], outs=[ps_sum])
        def mm_sq(e, c=c):
            last = None
            for tb in range(2):
                last = e.matmul(ps_sq[0:1, tb * 512:(tb + 1) * 512], onecf.ap,
                                sqc[:, tb * 512:(tb + 1) * 512],
                                start=(c == 0), stop=(c == NC - 1))
            return last
        p.op("tensor", mm_sq, ins=[onecf, sqc], outs=[ps_sq])

    p.op("scalar", lambda e: e.activation(rowA.ap, ps_sum[0:1, :], AF.Copy, scale=1.0 / D),
         ins=[ps_sum], outs=[rowA])
    p.op("scalar", lambda e: e.activation(rowB.ap, ps_sq[0:1, :], AF.Copy, scale=1.0 / D),
         ins=[ps_sq], outs=[rowB])
    p.op("scalar", lambda e: e.activation(rowC.ap, rowA.ap, AF.Square),
         ins=[rowA], outs=[rowC])
    p.op("vector", lambda e: e.tensor_sub(rowB.ap, rowB.ap, rowC.ap),
         ins=[rowB, rowC], outs=[rowB])
    p.op("vector", lambda e: e.tensor_scalar_add(rowB.ap, rowB.ap, 1e-5),
         ins=[rowB], outs=[rowB])
    p.op("scalar", lambda e: e.activation(rowC.ap, rowB.ap, AF.Sqrt),
         ins=[rowB], outs=[rowC])
    p.op("vector", lambda e: e.reciprocal(rowB.ap, rowC.ap),
         ins=[rowC], outs=[rowB])
    p.op("vector", lambda e: e.tensor_mul(rowC.ap, rowA.ap, rowB.ap),
         ins=[rowA, rowB], outs=[rowC])
    p.op("scalar", lambda e: e.activation(rowC.ap, rowC.ap, AF.Copy, scale=-1.0),
         ins=[rowC], outs=[rowC])
    def bc_r(e):
        last = None
        for tb in range(2):
            last = e.matmul(ps_b[:, tb * 512:(tb + 1) * 512], onerf.ap,
                            rowB[0:1, tb * 512:(tb + 1) * 512], start=True, stop=True)
        return last
    p.op("tensor", bc_r, ins=[onerf, rowB], outs=[ps_b])
    def bc_m(e):
        last = None
        for tb in range(2):
            last = e.matmul(ps_b2[:, tb * 512:(tb + 1) * 512], onerf.ap,
                            rowC[0:1, tb * 512:(tb + 1) * 512], start=True, stop=True)
        return last
    p.op("tensor", bc_m, ins=[onerf, rowC], outs=[ps_b2])
    for c in range(NC):
        p.op("vector", lambda e, c=c: e.tensor_mul(tmpc.ap, xc[c].ap, ps_b.ap),
             ins=[xc[c], ps_b], outs=[tmpc])
        p.op("vector", lambda e, c=c: e.tensor_add(xnc[c].ap, tmpc.ap, ps_b2.ap),
             ins=[tmpc, ps_b2], outs=[xnc[c]])
        out_toks.append(p.op("gpsimd", lambda e, c=c: e.dma_start(
            d_xn[c * 128:(c + 1) * 128, :], xnc[c].ap), ins=[xnc[c]], dma=True))

    def emit_qk(kind, off_e, cos_t, sin_t, d_out, do_k2):
        for h in range(NH):
            sl = wslab[h % 2]
            hb = qkh[h % 2]
            p.op("gpsimd", lambda e, h=h, sl=sl: e.dma_start(
                sl.rearrange("p (dc e) -> p dc e", dc=NC),
                d_w.rearrange("(dc p) e -> p dc e", p=128)[:, :, off_e + h * 128: off_e + (h + 1) * 128]),
                outs=[sl], dma=True)
            for tb in range(2):
                ps = ps_qk[tb]
                def mm(e, sl=sl, tb=tb, ps=ps):
                    last = None
                    for dc in range(NC):
                        last = e.matmul(ps.ap, sl[:, dc * 128:(dc + 1) * 128],
                                        xnc[dc][:, tb * 512:(tb + 1) * 512],
                                        start=(dc == 0), stop=(dc == NC - 1))
                    return last
                p.op("tensor", mm, ins=[sl] + xnc, outs=[ps])
                ts = slice(tb * 512, (tb + 1) * 512)
                p.op("vector", lambda e, ps=ps, ts=ts: e.tensor_mul(
                    ropetmp[0:64, :], ps[64:128, :], sin_t[0:64, ts]),
                    ins=[ps, sin_t], outs=[ropetmp])
                p.op("vector", lambda e, ps=ps, ts=ts: e.tensor_mul(
                    ropetmp[64:128, :], ps[0:64, :], sin_t[64:128, ts]),
                    ins=[ps, sin_t], outs=[ropetmp])
                p.op("vector", lambda e, ps=ps, ts=ts: e.tensor_mul(
                    ropecos.ap, ps.ap, cos_t[:, ts]),
                    ins=[ps, cos_t], outs=[ropecos])
                p.op("vector", lambda e, hb=hb, ts=ts: e.tensor_add(
                    hb[:, ts], ropecos.ap, ropetmp.ap),
                    ins=[ropecos, ropetmp], outs=[hb])
            out_toks.append(p.op("gpsimd", lambda e, h=h, hb=hb: e.dma_start(
                d_out[h * 128:(h + 1) * 128, :], hb.ap), ins=[hb], dma=True))
            if do_k2:
                p.op("scalar", lambda e, hb=hb: e.activation(k2sq.ap, hb.ap, AF.Square),
                     ins=[hb], outs=[k2sq])
                def mmk2(e):
                    last = None
                    for tb in range(2):
                        last = e.matmul(ps_k2[0:1, tb * 512:(tb + 1) * 512], onecb.ap,
                                        k2sq[:, tb * 512:(tb + 1) * 512],
                                        start=True, stop=True)
                    return last
                p.op("tensor", mmk2, ins=[onecb, k2sq], outs=[ps_k2])
                p.op("vector", lambda e, h=h: e.reduce_max(
                    k2row[0:1, h:h + 1], ps_k2[0:1, :], AX.X), ins=[ps_k2], outs=[k2row])

    emit_qk("q", 0, cq, sq_t, d_q, False)
    emit_qk("k", D, ck, sk, d_k, True)
    out_toks.append(p.op("gpsimd", lambda e: e.dma_start(d_k2[:], k2row.ap),
                         ins=[k2row], dma=True))

    for eb in range(4):
        sl = vslab[eb % 2]
        p.op("gpsimd", lambda e, eb=eb, sl=sl: e.dma_start(
            sl.rearrange("p (dc e) -> p dc e", dc=NC),
            d_w.rearrange("(dc p) e -> p dc e", p=128)[
                :, :, 2 * D + eb * 512: 2 * D + (eb + 1) * 512]),
            outs=[sl], dma=True)
        for tt in range(TPC // 128):
            ps = ps_v[tt % 2]
            def mmv(e, tt=tt, sl=sl, ps=ps):
                last = None
                for dc in range(NC):
                    last = e.matmul(ps.ap, xnc[dc][:, tt * 128:(tt + 1) * 128],
                                    sl[:, dc * 512:(dc + 1) * 512],
                                    start=(dc == 0), stop=(dc == NC - 1))
                return last
            p.op("tensor", mmv, ins=[sl] + xnc, outs=[ps])
            vt_b = vtt[tt % 2]
            p.op("scalar", lambda e, ps=ps, vt_b=vt_b: e.activation(
                vt_b.ap, ps.ap, AF.Copy), ins=[ps], outs=[vt_b])
            out_toks.append(p.op("gpsimd", lambda e, tt=tt, eb=eb, vt_b=vt_b: e.dma_start(
                d_v[tt * 128:(tt + 1) * 128, eb * 512:(eb + 1) * 512], vt_b.ap),
                ins=[vt_b], dma=True))

    p.finish(out_toks)
    p.build()
    return nc


def run_l0a(h):
    nc = build_l0a()
    in_maps = []
    for c in range(NCORE):
        ts = slice(c * TPC, (c + 1) * TPC)
        in_maps.append(dict(
            xT_c=np.ascontiguousarray(h["xT"][:, ts]),
            wqkvTp=h["wqkvTp"],
            cos_q=np.ascontiguousarray(h["cos_q"][:, ts]),
            sinx_q=np.ascontiguousarray(h["sinx_q"][:, ts]),
            cos_k=np.ascontiguousarray(h["cos_k"][:, ts]),
            sinx_k=np.ascontiguousarray(h["sinx_k"][:, ts]),
            ones_col_f=h["ones_col_f"],
            ones_row_f=h["ones_row_f"],
            ones_col_bf=h["ones_col_bf"],
        ))
    res = run_bass_kernel_spmd(nc, in_maps, core_ids=list(range(NCORE)))
    return res.results




def build_l0c():
    nc = bass.Bass("TRN2", target_bir_lowering=False)
    d_q = nc.dram_tensor("qf", [D, TPC], BF16, kind="ExternalInput")
    d_kf = nc.dram_tensor("kfull", [D, SEQ], BF16, kind="ExternalInput")
    d_vf = nc.dram_tensor("vfull", [SEQ, D], BF16, kind="ExternalInput")
    d_k2 = nc.dram_tensor("k2all", [NH, NCORE], F32, kind="ExternalInput")
    d_wo = nc.dram_tensor("woT", [D, D], F32, kind="ExternalInput")
    d_msk = nc.dram_tensor("masks_c", [2, 64, 128, 512], BF16, kind="ExternalInput")
    d_onerb = nc.dram_tensor("ones_row_bf", [1, 128], BF16, kind="ExternalInput")
    d_onecb = nc.dram_tensor("ones_col_bf", [128, 1], BF16, kind="ExternalInput")
    d_idb = nc.dram_tensor("ident_bf", [128, 128], BF16, kind="ExternalInput")
    d_scr = nc.dram_tensor("scr_k2", [1, NH], F32, kind="Internal")
    d_s = nc.dram_tensor("s_out", [D, TPC], BF16, kind="ExternalOutput")

    p = Prog(nc)
    out_toks = []

    qsb = p.sbuf("qsb", [128, NH * TPC], BF16)
    ysb = p.sbuf("ysb", [128, NH * TPC], BF16)
    mskslab = p.sbuf("mskslab", [128, 64 * 512], BF16)
    ksb = p.sbuf("ksb", [128, SEQ], BF16)
    vsb = p.sbuf("vsb", [128, SEQ], BF16)
    onerb = p.sbuf("onerb", [1, 128], BF16)
    onecb = p.sbuf("onecb", [128, 1], BF16)
    idb = p.sbuf("idb", [128, 128], BF16)
    k2sb = p.sbuf("k2sb", [NH, NCORE], F32)
    k2col = p.sbuf("k2col", [NH, 1], F32)
    k2row = p.sbuf("k2row", [1, NH], F32)
    sqh = p.sbuf("sqh", [128, TPC], BF16)
    q2row = p.sbuf("q2row", [1, TPC], F32)
    brow = p.sbuf("brow", [1, TPC], F32)
    negb = p.sbuf("negb", [1, TPC], BF16)
    p_sb = [p.sbuf(f"p_sb{i}", [128, 512], BF16) for i in range(2)]
    lrow = p.sbuf("lrow", [1, 512], F32)
    linvb = p.sbuf("linvb", [1, 512], BF16)
    lb_sb = p.sbuf("lb_sb", [128, 512], BF16)
    woslab = [p.sbuf(f"woslab{i}", [128, NC * 128], BF16) for i in range(2)]
    s_sb = [p.sbuf(f"s_sb{i}", [128, TPC], BF16) for i in range(2)]

    ps_s = [p.psum(f"ps_s{i}", [128, 512]) for i in range(2)]
    ps_o = p.psum("ps_o", [128, 512])
    ps_l = p.psum("ps_l", [1, 512])
    ps_q2 = p.psum("ps_q2", [1, TPC])

    p.op("gpsimd", lambda e: e.dma_start(
        qsb.rearrange("p (h t) -> p h t", h=NH),
        d_q.rearrange("(h p) t -> p h t", p=128)), outs=[qsb], dma=True)
    for b, d in ((onerb, d_onerb), (onecb, d_onecb), (idb, d_idb), (k2sb, d_k2)):
        p.op("sync", lambda e, b=b, d=d: e.dma_start(b.ap, d[:]), outs=[b], dma=True)
    p.op("vector", lambda e: e.reduce_max(k2col.ap, k2sb.ap, AX.X),
         ins=[k2sb], outs=[k2col])
    dscr = Buf(d_scr)
    p.op("gpsimd", lambda e: e.dma_start(d_scr.rearrange("a b -> b a"), k2col.ap),
         ins=[k2col], outs=[dscr], dma=True)
    p.op("gpsimd", lambda e: e.dma_start(k2row.ap, d_scr[:]),
         ins=[dscr], outs=[k2row], dma=True)

    for qb in range(2):
        p.op("sync", lambda e, qb=qb: e.dma_start(
            mskslab.rearrange("p (kt f) -> p kt f", kt=64),
            d_msk[qb, :, :, :].rearrange("kt p f -> p kt f")),
            outs=[mskslab], dma=True)
        for h in range(NH):
            p.op("scalar", lambda e, h=h: e.activation(
                sqh.ap, qsb[:, h * TPC:(h + 1) * TPC], AF.Square),
                ins=[qsb], outs=[sqh])
            def mmq2(e):
                last = None
                for tb in range(2):
                    last = e.matmul(ps_q2[0:1, tb * 512:(tb + 1) * 512], onecb.ap,
                                    sqh[:, tb * 512:(tb + 1) * 512], start=True, stop=True)
                return last
            p.op("tensor", mmq2, ins=[onecb, sqh], outs=[ps_q2])
            p.op("vector", lambda e, h=h: e.tensor_scalar(
                q2row.ap, ps_q2[0:1, :], k2row[0:1, h:h + 1], 1.0404,
                ALU.mult, ALU.mult), ins=[ps_q2, k2row], outs=[q2row])
            p.op("scalar", lambda e: e.activation(brow.ap, q2row.ap, AF.Sqrt),
                 ins=[q2row], outs=[brow])
            p.op("scalar", lambda e: e.activation(negb.ap, brow.ap, AF.Copy, scale=-1.0),
                 ins=[brow], outs=[negb])
            p.op("sync", lambda e, h=h: e.dma_start(
                ksb.ap, d_kf[h * 128:(h + 1) * 128, :]), outs=[ksb], dma=True)
            p.op("sync", lambda e, h=h: e.dma_start(
                vsb.rearrange("p (kt d) -> p kt d", kt=64),
                d_vf.rearrange("(kt p) d -> p kt d", p=128)[:, :, h * 128:(h + 1) * 128]),
                outs=[vsb], dma=True)
            for kt in range(64):
                ps = ps_s[kt % 2]
                pb = p_sb[kt % 2]
                def mms(e, h=h, qb=qb, kt=kt, ps=ps):
                    e.matmul(ps.ap, onerb.ap, negb[0:1, qb * 512:(qb + 1) * 512],
                             start=True, stop=False)
                    e.matmul(ps.ap, idb.ap, mskslab[:, kt * 512:(kt + 1) * 512],
                             start=False, stop=False)
                    return e.matmul(ps.ap, ksb[:, kt * 128:(kt + 1) * 128],
                                    qsb[:, h * TPC + qb * 512: h * TPC + (qb + 1) * 512],
                                    start=False, stop=True)
                p.op("tensor", mms, ins=[onerb, negb, idb, mskslab, ksb, qsb], outs=[ps])
                p.op("scalar", lambda e, ps=ps, pb=pb: e.activation(pb.ap, ps.ap, AF.Exp),
                     ins=[ps], outs=[pb])
                p.op("tensor", lambda e, kt=kt, pb=pb: e.matmul(
                    ps_o.ap, vsb[:, kt * 128:(kt + 1) * 128], pb.ap,
                    start=(kt == 0), stop=(kt == 63)),
                    ins=[vsb, pb], outs=[ps_o] if kt == 63 else [ps_o])
                p.op("tensor", lambda e, kt=kt, pb=pb: e.matmul(
                    ps_l[0:1, :], onecb.ap, pb.ap,
                    start=(kt == 0), stop=(kt == 63)),
                    ins=[onecb, pb], outs=[ps_l])
            p.op("scalar", lambda e: e.activation(lrow.ap, ps_l[0:1, :], AF.Copy),
                 ins=[ps_l], outs=[lrow])
            p.op("vector", lambda e: e.reciprocal(lrow.ap, lrow.ap),
                 ins=[lrow], outs=[lrow])
            p.op("scalar", lambda e: e.activation(linvb.ap, lrow.ap, AF.Copy),
                 ins=[lrow], outs=[linvb])
            p.op("tensor", lambda e: e.matmul(ps_s[0].ap, onerb.ap, linvb.ap,
                                              start=True, stop=True),
                 ins=[onerb, linvb], outs=[ps_s[0]])
            p.op("scalar", lambda e: e.activation(lb_sb.ap, ps_s[0].ap, AF.Copy),
                 ins=[ps_s[0]], outs=[lb_sb])
            p.op("vector", lambda e, h=h, qb=qb: e.tensor_mul(
                ysb[:, h * TPC + qb * 512: h * TPC + (qb + 1) * 512],
                ps_o.ap, lb_sb.ap), ins=[ps_o, lb_sb], outs=[ysb])

    for eo in range(NC):
        sl = woslab[eo % 2]
        p.op("gpsimd", lambda e, eo=eo, sl=sl: e.dma_start(
            sl.rearrange("p (dc e) -> p dc e", dc=NC),
            d_wo.rearrange("(dc p) e -> p dc e", p=128)[:, :, eo * 128:(eo + 1) * 128]),
            outs=[sl], dma=True)
        sb = s_sb[eo % 2]
        for tb in range(2):
            ps = ps_s[tb]
            def mmo(e, sl=sl, tb=tb, ps=ps):
                last = None
                for dc in range(NC):
                    last = e.matmul(ps.ap, sl[:, dc * 128:(dc + 1) * 128],
                                    ysb[:, dc * TPC + tb * 512: dc * TPC + (tb + 1) * 512],
                                    start=(dc == 0), stop=(dc == NC - 1))
                return last
            p.op("tensor", mmo, ins=[sl, ysb], outs=[ps])
            p.op("scalar", lambda e, tb=tb, ps=ps, sb=sb: e.activation(
                sb[:, tb * 512:(tb + 1) * 512], ps.ap, AF.Copy), ins=[ps], outs=[sb])
        out_toks.append(p.op("gpsimd", lambda e, eo=eo, sb=sb: e.dma_start(
            d_s[eo * 128:(eo + 1) * 128, :], sb.ap), ins=[sb], dma=True))

    p.finish(out_toks)
    p.build()
    return nc


def run_l0c(h, la):
    nc = build_l0c()
    kfull = np.concatenate([np.asarray(r["kf"]) for r in la], axis=1)
    vfull = np.concatenate([np.asarray(r["vt"]) for r in la], axis=0)
    k2all = np.stack([np.asarray(r["k2m"])[0] for r in la], axis=1)
    in_maps = []
    for c in range(NCORE):
        in_maps.append(dict(
            qf=np.asarray(la[c]["qf"]),
            kfull=kfull, vfull=vfull, k2all=np.ascontiguousarray(k2all),
            woT=h["woT"], masks_c=h["masks"][c],
            ones_row_bf=h["ones_row_bf"], ones_col_bf=h["ones_col_bf"],
            ident_bf=h["ident_bf"],
        ))
    res = run_bass_kernel_spmd(nc, in_maps, core_ids=list(range(NCORE)))
    return res.results




def build_l0b():
    nc = bass.Bass("TRN2", target_bir_lowering=False)
    d_gi = nc.dram_tensor("gidx_sl", [D, OPC], I32, kind="ExternalInput")
    d_ui = nc.dram_tensor("uidx_sl", [D, OPC], I32, kind="ExternalInput")
    d_di = nc.dram_tensor("didx_sl", [INTER, 2 * 128], I32, kind="ExternalInput")
    d_lut = nc.dram_tensor("lutrep", [128, 16], F32, kind="ExternalInput")
    d_wg = nc.dram_tensor("wgT_sl", [D, OPC], BF16, kind="ExternalOutput")
    d_wu = nc.dram_tensor("wuT_sl", [D, OPC], BF16, kind="ExternalOutput")
    d_wd = nc.dram_tensor("wdT_sl", [INTER, 2 * 128], BF16, kind="ExternalOutput")

    p = Prog(nc)
    out_toks = []
    lut = p.sbuf("lut", [128, 16], F32)
    p.op("sync", lambda e: e.dma_start(lut.ap, d_lut[:]), outs=[lut], dma=True)

    idxb = [p.sbuf(f"idxb{i}", [128, OPC], I32) for i in range(2)]
    accb = [p.sbuf(f"accb{i}", [128, OPC], BF16) for i in range(2)]
    term = [p.sbuf(f"term{i}", [128, OPC], BF16) for i in range(2)]

    def construct(d_idx, d_out, nrow, ncol):
        nslab = (nrow + 127) // 128
        for s in range(nslab):
            r0, r1 = s * 128, min((s + 1) * 128, nrow)
            ib, ab, tb = idxb[s % 2], accb[s % 2], term[s % 2]
            p.op("sync", lambda e, d_idx=d_idx, r0=r0, r1=r1, ib=ib, ncol=ncol:
                 e.dma_start(ib[0:r1 - r0, 0:ncol], d_idx[r0:r1, :]),
                 outs=[ib], dma=True)
            np_ = r1 - r0
            for j in range(16):
                if j == 0:
                    p.op("vector", lambda e, ib=ib, ab=ab, np_=np_, ncol=ncol:
                         e.tensor_scalar(ab[0:np_, 0:ncol], ib[0:np_, 0:ncol], 0.0,
                                         lut[0:np_, 0:1], ALU.is_equal, ALU.mult),
                         ins=[ib, lut], outs=[ab])
                else:
                    p.op("vector", lambda e, j=j, ib=ib, tb=tb, np_=np_, ncol=ncol:
                         e.tensor_scalar(tb[0:np_, 0:ncol], ib[0:np_, 0:ncol], float(j),
                                         lut[0:np_, j:j + 1], ALU.is_equal, ALU.mult),
                         ins=[ib, lut], outs=[tb])
                    p.op("vector", lambda e, ab=ab, tb=tb, np_=np_, ncol=ncol:
                         e.tensor_add(ab[0:np_, 0:ncol], ab[0:np_, 0:ncol],
                                      tb[0:np_, 0:ncol]),
                         ins=[tb, ab], outs=[ab])
            out_toks.append(p.op("gpsimd", lambda e, d_out=d_out, r0=r0, r1=r1, ab=ab,
                                 ncol=ncol: e.dma_start(d_out[r0:r1, :],
                                                        ab[0:r1 - r0, 0:ncol]),
                                 ins=[ab], dma=True))

    construct(d_gi, d_wg, D, OPC)
    construct(d_ui, d_wu, D, OPC)
    construct(d_di, d_wd, INTER, 2 * 128)
    p.finish(out_toks)
    p.build()
    return nc


def run_l0b(h):
    nc = build_l0b()
    in_maps = []
    for c in range(NCORE):
        osl = slice(c * OPC, (c + 1) * OPC)
        dsl = slice(c * 2 * 128, (c + 1) * 2 * 128)
        in_maps.append(dict(
            gidx_sl=np.ascontiguousarray(h["gidxT"][:, osl]),
            uidx_sl=np.ascontiguousarray(h["uidxT"][:, osl]),
            didx_sl=np.ascontiguousarray(h["didxT"][:, dsl]),
            lutrep=h["lutrep"],
        ))
    res = run_bass_kernel_spmd(nc, in_maps, core_ids=list(range(NCORE)))
    return res.results


def build_l0d():
    nc = bass.Bass("TRN2", target_bir_lowering=False)
    d_xn = nc.dram_tensor("xn", [D, TPC], BF16, kind="ExternalInput")
    d_x = nc.dram_tensor("xT_c", [D, TPC], F32, kind="ExternalInput")
    d_s = nc.dram_tensor("s_out", [D, TPC], BF16, kind="ExternalInput")
    d_wg = nc.dram_tensor("wgT", [D, INTER], BF16, kind="ExternalInput")
    d_wu = nc.dram_tensor("wuT", [D, INTER], BF16, kind="ExternalInput")
    d_wd = nc.dram_tensor("wdT", [INTER, D], BF16, kind="ExternalInput")
    d_x1 = nc.dram_tensor("x1T_c", [D, TPC], F32, kind="ExternalOutput")

    NOC = INTER // 128
    p = Prog(nc)
    out_toks = []
    xnc = [p.sbuf(f"xn{c}", [128, TPC], BF16) for c in range(NC)]
    hbuf = p.sbuf("hbuf", [128, NOC * 512], BF16)
    gslab = [p.sbuf(f"gslab{i}", [128, NC * 128], BF16) for i in range(2)]
    uslab = [p.sbuf(f"uslab{i}", [128, NC * 128], BF16) for i in range(2)]
    dslab = [p.sbuf(f"dslab{i}", [128, NOC * 128], BF16) for i in range(2)]
    gs = p.sbuf("gs", [128, 512], BF16)
    xcb = [p.sbuf(f"xcb{i}", [128, 512], F32) for i in range(2)]
    scb = [p.sbuf(f"scb{i}", [128, 512], BF16) for i in range(2)]
    gcb = p.sbuf("gcb", [128, 512], BF16)
    tmpb = p.sbuf("tmpb", [128, 512], BF16)

    ps_g = [p.psum(f"ps_g{i}", [128, 512]) for i in range(2)]
    ps_u = [p.psum(f"ps_u{i}", [128, 512]) for i in range(2)]
    ps_d = [p.psum(f"ps_d{i}", [128, 512]) for i in range(2)]

    for c in range(NC):
        p.op("sync", lambda e, c=c: e.dma_start(xnc[c].ap, d_xn[c * 128:(c + 1) * 128, :]),
             outs=[xnc[c]], dma=True)

    for tb in range(2):
        ts = slice(tb * 512, (tb + 1) * 512)
        for oc in range(NOC):
            g_sl, u_sl = gslab[oc % 2], uslab[oc % 2]
            p.op("gpsimd", lambda e, oc=oc, g_sl=g_sl: e.dma_start(
                g_sl.rearrange("p (dc o) -> p dc o", dc=NC),
                d_wg.rearrange("(dc p) o -> p dc o", p=128)[:, :, oc * 128:(oc + 1) * 128]),
                outs=[g_sl], dma=True)
            p.op("gpsimd", lambda e, oc=oc, u_sl=u_sl: e.dma_start(
                u_sl.rearrange("p (dc o) -> p dc o", dc=NC),
                d_wu.rearrange("(dc p) o -> p dc o", p=128)[:, :, oc * 128:(oc + 1) * 128]),
                outs=[u_sl], dma=True)
            pg, pu = ps_g[oc % 2], ps_u[oc % 2]
            def mmg(e, g_sl=g_sl, ts=ts, pg=pg):
                last = None
                for dc in range(NC):
                    last = e.matmul(pg.ap, g_sl[:, dc * 128:(dc + 1) * 128],
                                    xnc[dc][:, ts], start=(dc == 0), stop=(dc == NC - 1))
                return last
            p.op("tensor", mmg, ins=[g_sl] + xnc, outs=[pg])
            def mmu(e, u_sl=u_sl, ts=ts, pu=pu):
                last = None
                for dc in range(NC):
                    last = e.matmul(pu.ap, u_sl[:, dc * 128:(dc + 1) * 128],
                                    xnc[dc][:, ts], start=(dc == 0), stop=(dc == NC - 1))
                return last
            p.op("tensor", mmu, ins=[u_sl] + xnc, outs=[pu])
            p.op("scalar", lambda e, pg=pg: e.activation(gs.ap, pg.ap, AF.Silu),
                 ins=[pg], outs=[gs])
            p.op("vector", lambda e, oc=oc, pu=pu: e.tensor_mul(
                hbuf[:, oc * 512:(oc + 1) * 512], gs.ap, pu.ap),
                ins=[gs, pu], outs=[hbuf])
        for hc in range(NC):
            d_sl = dslab[hc % 2]
            p.op("gpsimd", lambda e, hc=hc, d_sl=d_sl: e.dma_start(
                d_sl.rearrange("p (ic h) -> p ic h", ic=NOC),
                d_wd.rearrange("(ic p) h -> p ic h", p=128)[:, :, hc * 128:(hc + 1) * 128]),
                outs=[d_sl], dma=True)
            pd = ps_d[hc % 2]
            def mmd(e, d_sl=d_sl, pd=pd):
                last = None
                for ic in range(NOC):
                    last = e.matmul(pd.ap, d_sl[:, ic * 128:(ic + 1) * 128],
                                    hbuf[:, ic * 512:(ic + 1) * 512],
                                    start=(ic == 0), stop=(ic == NOC - 1))
                return last
            p.op("tensor", mmd, ins=[d_sl, hbuf], outs=[pd])
            p.op("scalar", lambda e, pd=pd: e.activation(gcb.ap, pd.ap, AF.Silu),
                 ins=[pd], outs=[gcb])
            hs = slice(hc * 128, (hc + 1) * 128)
            xb, sb = xcb[hc % 2], scb[hc % 2]
            p.op("sync", lambda e, hs=hs, ts=ts, xb=xb: e.dma_start(
                xb.ap, d_x[hs, ts]), outs=[xb], dma=True)
            p.op("sync", lambda e, hs=hs, ts=ts, sb=sb: e.dma_start(
                sb.ap, d_s[hs, ts]), outs=[sb], dma=True)
            p.op("vector", lambda e, sb=sb: e.tensor_mul(tmpb.ap, gcb.ap, sb.ap),
                 ins=[gcb, sb], outs=[tmpb])
            p.op("vector", lambda e, xb=xb: e.tensor_add(xb.ap, xb.ap, tmpb.ap),
                 ins=[tmpb, xb], outs=[xb])
            out_toks.append(p.op("gpsimd", lambda e, hs=hs, ts=ts, xb=xb: e.dma_start(
                d_x1[hs, ts], xb.ap), ins=[xb], dma=True))
    p.finish(out_toks)
    p.build()
    return nc


def run_l0bd(h, la, lc, lb):
    nc = build_l0d()
    wgT = np.concatenate([np.asarray(r["wgT_sl"]) for r in lb], axis=1)
    wuT = np.concatenate([np.asarray(r["wuT_sl"]) for r in lb], axis=1)
    wdT = np.concatenate([np.asarray(r["wdT_sl"]) for r in lb], axis=1)
    in_maps = []
    for c in range(NCORE):
        ts = slice(c * TPC, (c + 1) * TPC)
        in_maps.append(dict(
            xn=np.asarray(la[c]["xn"]),
            xT_c=np.ascontiguousarray(h["xT"][:, ts]),
            s_out=np.asarray(lc[c]["s_out"]),
            wgT=wgT, wuT=wuT, wdT=wdT,
        ))
    res = run_bass_kernel_spmd(nc, in_maps, core_ids=list(range(NCORE)))
    return res.results




def build_l1a():
    nc = bass.Bass("TRN2", target_bir_lowering=False)
    d_x = nc.dram_tensor("x1T_c", [D, TPC], F32, kind="ExternalInput")
    d_wq = nc.dram_tensor("wq1", [D, D], F32, kind="ExternalInput")
    d_wk = nc.dram_tensor("wk1", [D, D], F32, kind="ExternalInput")
    d_wv = nc.dram_tensor("wv1", [D, D], F32, kind="ExternalInput")
    d_qs = nc.dram_tensor("qscale", [128, 1], F32, kind="ExternalInput")
    d_ks = nc.dram_tensor("kscale", [128, 1], F32, kind="ExternalInput")
    d_onecf = nc.dram_tensor("ones_col_f", [128, 1], F32, kind="ExternalInput")
    d_onerf = nc.dram_tensor("ones_row_f", [1, 128], F32, kind="ExternalInput")
    d_onecb = nc.dram_tensor("ones_col_bf", [128, 1], BF16, kind="ExternalInput")
    d_onerb = nc.dram_tensor("ones_row_bf", [1, 128], BF16, kind="ExternalInput")

    d_xn = nc.dram_tensor("xn1", [D, TPC], BF16, kind="ExternalOutput")
    d_qn = nc.dram_tensor("qn", [D, TPC], BF16, kind="ExternalOutput")
    d_kn = nc.dram_tensor("kn", [D, TPC], BF16, kind="ExternalOutput")
    d_v = nc.dram_tensor("v1t", [TPC, D], BF16, kind="ExternalOutput")

    p = Prog(nc)
    out_toks = []
    xc = [p.sbuf(f"sx{c}", [128, TPC], F32) for c in range(NC)]
    xnc = [p.sbuf(f"sxn{c}", [128, TPC], BF16) for c in range(NC)]
    sqc = p.sbuf("sqc", [128, TPC], F32)
    onecf = p.sbuf("onecf", [128, 1], F32)
    onerf = p.sbuf("onerf", [1, 128], F32)
    onecb = p.sbuf("onecb", [128, 1], BF16)
    onerb = p.sbuf("onerb", [1, 128], BF16)
    qscol = p.sbuf("qscol", [128, 1], F32)
    kscol = p.sbuf("kscol", [128, 1], F32)
    rowA = p.sbuf("rowA", [1, TPC], F32)
    rowB = p.sbuf("rowB", [1, TPC], F32)
    rowC = p.sbuf("rowC", [1, TPC], F32)
    tmpc = p.sbuf("tmpc", [128, TPC], F32)
    wslab = [p.sbuf(f"wslab{i}", [128, NC * 128], BF16) for i in range(2)]
    vslab = [p.sbuf(f"vslab{i}", [128, NC * 512], BF16) for i in range(2)]
    qkh = [p.sbuf(f"qkh{i}", [128, TPC], BF16) for i in range(2)]
    sqh = p.sbuf("sqh", [128, 512], BF16)
    r512 = p.sbuf("r512", [1, 512], F32)
    rbf = p.sbuf("rbf", [1, 512], BF16)
    rb_sb = p.sbuf("rb_sb", [128, 512], BF16)
    vtt = [p.sbuf(f"vtt{i}", [128, 512], BF16) for i in range(2)]

    T1 = p.psum("T1", [128, TPC])
    T2 = p.psum("T2", [128, TPC])
    T3 = p.psum("T3", [128, 512])
    T4 = p.psum("T4", [128, 512])
    ps_sum, ps_sq = T1, T2
    ps_b, ps_b2 = T1, T2
    ps_qk = [T3, T4]
    ps_q2 = T2
    ps_rb = T1
    ps_v = [T3, T4]

    for c in range(NC):
        p.op("sync", lambda e, c=c: e.dma_start(xc[c].ap, d_x[c * 128:(c + 1) * 128, :]),
             outs=[xc[c]], dma=True)
    for b, d in ((onecf, d_onecf), (onerf, d_onerf), (onecb, d_onecb),
                 (onerb, d_onerb), (qscol, d_qs), (kscol, d_ks)):
        p.op("sync", lambda e, b=b, d=d: e.dma_start(b.ap, d[:]), outs=[b], dma=True)

    for c in range(NC):
        p.op("scalar", lambda e, c=c: e.activation(sqc.ap, xc[c].ap, AF.Square),
             ins=[xc[c]], outs=[sqc])
        def mm_sum(e, c=c):
            last = None
            for tb in range(2):
                last = e.matmul(ps_sum[0:1, tb * 512:(tb + 1) * 512], onecf.ap,
                                xc[c][:, tb * 512:(tb + 1) * 512],
                                start=(c == 0), stop=(c == NC - 1))
            return last
        p.op("tensor", mm_sum, ins=[onecf, xc[c]], outs=[ps_sum])
        def mm_sq(e, c=c):
            last = None
            for tb in range(2):
                last = e.matmul(ps_sq[0:1, tb * 512:(tb + 1) * 512], onecf.ap,
                                sqc[:, tb * 512:(tb + 1) * 512],
                                start=(c == 0), stop=(c == NC - 1))
            return last
        p.op("tensor", mm_sq, ins=[onecf, sqc], outs=[ps_sq])
    p.op("scalar", lambda e: e.activation(rowA.ap, ps_sum[0:1, :], AF.Copy, scale=1.0 / D),
         ins=[ps_sum], outs=[rowA])
    p.op("scalar", lambda e: e.activation(rowB.ap, ps_sq[0:1, :], AF.Copy, scale=1.0 / D),
         ins=[ps_sq], outs=[rowB])
    p.op("scalar", lambda e: e.activation(rowC.ap, rowA.ap, AF.Square),
         ins=[rowA], outs=[rowC])
    p.op("vector", lambda e: e.tensor_sub(rowB.ap, rowB.ap, rowC.ap),
         ins=[rowB, rowC], outs=[rowB])
    p.op("vector", lambda e: e.tensor_scalar_add(rowB.ap, rowB.ap, 1e-5),
         ins=[rowB], outs=[rowB])
    p.op("scalar", lambda e: e.activation(rowC.ap, rowB.ap, AF.Sqrt),
         ins=[rowB], outs=[rowC])
    p.op("vector", lambda e: e.reciprocal(rowB.ap, rowC.ap),
         ins=[rowC], outs=[rowB])
    p.op("vector", lambda e: e.tensor_mul(rowC.ap, rowA.ap, rowB.ap),
         ins=[rowA, rowB], outs=[rowC])
    p.op("scalar", lambda e: e.activation(rowC.ap, rowC.ap, AF.Copy, scale=-1.0),
         ins=[rowC], outs=[rowC])
    def bc_r(e):
        last = None
        for tb in range(2):
            last = e.matmul(ps_b[:, tb * 512:(tb + 1) * 512], onerf.ap,
                            rowB[0:1, tb * 512:(tb + 1) * 512], start=True, stop=True)
        return last
    p.op("tensor", bc_r, ins=[onerf, rowB], outs=[ps_b])
    def bc_m(e):
        last = None
        for tb in range(2):
            last = e.matmul(ps_b2[:, tb * 512:(tb + 1) * 512], onerf.ap,
                            rowC[0:1, tb * 512:(tb + 1) * 512], start=True, stop=True)
        return last
    p.op("tensor", bc_m, ins=[onerf, rowC], outs=[ps_b2])
    for c in range(NC):
        p.op("vector", lambda e, c=c: e.tensor_mul(tmpc.ap, xc[c].ap, ps_b.ap),
             ins=[xc[c], ps_b], outs=[tmpc])
        p.op("vector", lambda e, c=c: e.tensor_add(xnc[c].ap, tmpc.ap, ps_b2.ap),
             ins=[tmpc, ps_b2], outs=[xnc[c]])
        out_toks.append(p.op("gpsimd", lambda e, c=c: e.dma_start(
            d_xn[c * 128:(c + 1) * 128, :], xnc[c].ap), ins=[xnc[c]], dma=True))

    def emit_qk1(d_wgt, scol, d_out):
        for h in range(NH):
            sl = wslab[h % 2]
            hb = qkh[h % 2]
            p.op("gpsimd", lambda e, h=h, sl=sl, d_wgt=d_wgt: e.dma_start(
                sl.rearrange("p (dc e) -> p dc e", dc=NC),
                d_wgt.rearrange("(dc p) e -> p dc e", p=128)[:, :, h * 128:(h + 1) * 128]),
                outs=[sl], dma=True)
            for tb in range(2):
                ps = ps_qk[tb]
                def mm(e, sl=sl, tb=tb, ps=ps):
                    last = None
                    for dc in range(NC):
                        last = e.matmul(ps.ap, sl[:, dc * 128:(dc + 1) * 128],
                                        xnc[dc][:, tb * 512:(tb + 1) * 512],
                                        start=(dc == 0), stop=(dc == NC - 1))
                    return last
                p.op("tensor", mm, ins=[sl] + xnc, outs=[ps])
                p.op("scalar", lambda e, ps=ps: e.activation(sqh.ap, ps.ap, AF.Square),
                     ins=[ps], outs=[sqh])
                p.op("tensor", lambda e: e.matmul(ps_q2[0:1, 0:512], onecb.ap, sqh.ap,
                                                  start=True, stop=True),
                     ins=[onecb, sqh], outs=[ps_q2])
                p.op("scalar", lambda e: e.activation(r512.ap, ps_q2[0:1, 0:512],
                                                      AF.Copy, scale=1.0 / DH),
                     ins=[ps_q2], outs=[r512])
                p.op("vector", lambda e: e.tensor_scalar_add(r512.ap, r512.ap, 1e-6),
                     ins=[r512], outs=[r512])
                p.op("scalar", lambda e: e.activation(r512.ap, r512.ap, AF.Sqrt),
                     ins=[r512], outs=[r512])
                p.op("vector", lambda e: e.reciprocal(r512.ap, r512.ap),
                     ins=[r512], outs=[r512])
                p.op("scalar", lambda e: e.activation(rbf.ap, r512.ap, AF.Copy),
                     ins=[r512], outs=[rbf])
                p.op("tensor", lambda e: e.matmul(ps_rb[:, 0:512], onerb.ap, rbf.ap,
                                                  start=True, stop=True),
                     ins=[onerb, rbf], outs=[ps_rb])
                p.op("scalar", lambda e: e.activation(rb_sb.ap, ps_rb[:, 0:512], AF.Copy),
                     ins=[ps_rb], outs=[rb_sb])
                ts = slice(tb * 512, (tb + 1) * 512)
                p.op("vector", lambda e, hb=hb, ts=ts, ps=ps, scol=scol:
                     e.scalar_tensor_tensor(hb[:, ts], ps.ap, scol.ap, rb_sb.ap,
                                            ALU.mult, ALU.mult),
                     ins=[ps, scol, rb_sb], outs=[hb])
            out_toks.append(p.op("gpsimd", lambda e, h=h, hb=hb, d_out=d_out: e.dma_start(
                d_out[h * 128:(h + 1) * 128, :], hb.ap), ins=[hb], dma=True))

    emit_qk1(d_wq, qscol, d_qn)
    emit_qk1(d_wk, kscol, d_kn)

    for eb in range(4):
        sl = vslab[eb % 2]
        p.op("gpsimd", lambda e, eb=eb, sl=sl: e.dma_start(
            sl.rearrange("p (dc e) -> p dc e", dc=NC),
            d_wv.rearrange("(dc p) e -> p dc e", p=128)[:, :, eb * 512:(eb + 1) * 512]),
            outs=[sl], dma=True)
        for tt in range(TPC // 128):
            ps = ps_v[tt % 2]
            def mmv(e, tt=tt, sl=sl, ps=ps):
                last = None
                for dc in range(NC):
                    last = e.matmul(ps.ap, xnc[dc][:, tt * 128:(tt + 1) * 128],
                                    sl[:, dc * 512:(dc + 1) * 512],
                                    start=(dc == 0), stop=(dc == NC - 1))
                return last
            p.op("tensor", mmv, ins=[sl] + xnc, outs=[ps])
            vt_b = vtt[tt % 2]
            p.op("scalar", lambda e, ps=ps, vt_b=vt_b: e.activation(
                vt_b.ap, ps.ap, AF.Copy), ins=[ps], outs=[vt_b])
            out_toks.append(p.op("gpsimd", lambda e, tt=tt, eb=eb, vt_b=vt_b: e.dma_start(
                d_v[tt * 128:(tt + 1) * 128, eb * 512:(eb + 1) * 512], vt_b.ap),
                ins=[vt_b], dma=True))

    p.finish(out_toks)
    p.build()
    return nc


def run_l1a(h, ld):
    nc = build_l1a()
    in_maps = []
    for c in range(NCORE):
        in_maps.append(dict(
            x1T_c=np.asarray(ld[c]["x1T_c"]),
            wq1=h["wq1"], wk1=h["wk1"], wv1=h["wv1"],
            qscale=h["qscale"], kscale=h["kscale"],
            ones_col_f=h["ones_col_f"], ones_row_f=h["ones_row_f"],
            ones_col_bf=h["ones_col_bf"], ones_row_bf=h["ones_row_bf"],
        ))
    res = run_bass_kernel_spmd(nc, in_maps, core_ids=list(range(NCORE)))
    return res.results


def build_l1b():
    nc = bass.Bass("TRN2", target_bir_lowering=False)
    d_q = nc.dram_tensor("qn", [D, TPC], BF16, kind="ExternalInput")
    d_kf = nc.dram_tensor("knfull", [D, SEQ], BF16, kind="ExternalInput")
    d_vf = nc.dram_tensor("v1full", [SEQ, D], BF16, kind="ExternalInput")
    d_wo = nc.dram_tensor("wo1", [D, D], F32, kind="ExternalInput")
    d_onerb = nc.dram_tensor("ones_row_bf", [1, 128], BF16, kind="ExternalInput")
    d_onecb = nc.dram_tensor("ones_col_bf", [128, 1], BF16, kind="ExternalInput")
    d_idf = nc.dram_tensor("ident_f", [128, 128], F32, kind="ExternalInput")
    d_s = nc.dram_tensor("s1_out", [D, TPC], BF16, kind="ExternalOutput")

    p = Prog(nc)
    out_toks = []
    qsb = p.sbuf("qsb", [128, NH * TPC], BF16)
    ysb = p.sbuf("ysb", [128, NH * TPC], BF16)
    ksb = [p.sbuf(f"ksb{i}", [128, SEQ], BF16) for i in range(2)]
    vsb = [p.sbuf(f"vsb{i}", [128, SEQ], BF16) for i in range(2)]
    onerb = p.sbuf("onerb", [1, 128], BF16)
    onecb = p.sbuf("onecb", [128, 1], BF16)
    idf = p.sbuf("idf", [128, 128], F32)
    m_all = p.sbuf("m_all", [128, 16], F32)
    mcol = p.sbuf("mcol", [128, 1], F32)
    negm = p.sbuf("negm", [1, 512], BF16)
    p_sb = [p.sbuf(f"p_sb{i}", [128, 512], BF16) for i in range(2)]
    lrow = p.sbuf("lrow", [1, 512], F32)
    linvb = p.sbuf("linvb", [1, 512], BF16)
    lb_sb = p.sbuf("lb_sb", [128, 512], BF16)
    woslab = [p.sbuf(f"woslab{i}", [128, NC * 128], BF16) for i in range(2)]
    s_sb = [p.sbuf(f"s_sb{i}", [128, TPC], BF16) for i in range(2)]

    ps_s = [p.psum(f"ps_s{i}", [128, 512]) for i in range(2)]
    ps_o = p.psum("ps_o", [128, 512])
    ps_l = p.psum("ps_l", [1, 512])
    ps_t = p.psum("ps_t", [1, 128])

    p.op("gpsimd", lambda e: e.dma_start(
        qsb.rearrange("p (h t) -> p h t", h=NH),
        d_q.rearrange("(h p) t -> p h t", p=128)), outs=[qsb], dma=True)
    for b, d in ((onerb, d_onerb), (onecb, d_onecb), (idf, d_idf)):
        p.op("sync", lambda e, b=b, d=d: e.dma_start(b.ap, d[:]), outs=[b], dma=True)

    for h in range(NH):
        kb, vb = ksb[h % 2], vsb[h % 2]
        p.op("sync", lambda e, h=h, kb=kb: e.dma_start(
            kb.ap, d_kf[h * 128:(h + 1) * 128, :]), outs=[kb], dma=True)
        p.op("sync", lambda e, h=h, vb=vb: e.dma_start(
            vb.rearrange("p (kt d) -> p kt d", kt=64),
            d_vf.rearrange("(kt p) d -> p kt d", p=128)[:, :, h * 128:(h + 1) * 128]),
            outs=[vb], dma=True)
        for qb in range(2):
            for qt in range(4):
                q0 = h * TPC + qb * 512 + qt * 128
                for kc in range(16):
                    ps = ps_s[kc % 2]
                    p.op("tensor", lambda e, q0=q0, kc=kc, ps=ps, kb=kb: e.matmul(
                        ps.ap, qsb[:, q0:q0 + 128], kb[:, kc * 512:(kc + 1) * 512],
                        start=True, stop=True), ins=[qsb, kb], outs=[ps])
                    p.op("vector", lambda e, kc=kc, ps=ps: e.reduce_max(
                        m_all[:, kc:kc + 1], ps.ap, AX.X), ins=[ps], outs=[m_all])
                p.op("vector", lambda e: e.reduce_max(mcol.ap, m_all.ap, AX.X),
                     ins=[m_all], outs=[mcol])
                p.op("tensor", lambda e: e.transpose(ps_t.ap, mcol.ap, idf.ap),
                     ins=[mcol, idf], outs=[ps_t])
                p.op("scalar", lambda e, qt=qt: e.activation(
                    negm[0:1, qt * 128:(qt + 1) * 128], ps_t.ap, AF.Copy, scale=-1.0),
                    ins=[ps_t], outs=[negm])
            for kt in range(64):
                ps = ps_s[kt % 2]
                pb = p_sb[kt % 2]
                def mms(e, h=h, qb=qb, kt=kt, ps=ps, kb=kb):
                    e.matmul(ps.ap, onerb.ap, negm.ap, start=True, stop=False)
                    return e.matmul(ps.ap, kb[:, kt * 128:(kt + 1) * 128],
                                    qsb[:, h * TPC + qb * 512: h * TPC + (qb + 1) * 512],
                                    start=False, stop=True)
                p.op("tensor", mms, ins=[onerb, negm, kb, qsb], outs=[ps])
                p.op("scalar", lambda e, ps=ps, pb=pb: e.activation(pb.ap, ps.ap, AF.Exp),
                     ins=[ps], outs=[pb])
                p.op("tensor", lambda e, kt=kt, pb=pb, vb=vb: e.matmul(
                    ps_o.ap, vb[:, kt * 128:(kt + 1) * 128], pb.ap,
                    start=(kt == 0), stop=(kt == 63)), ins=[vb, pb], outs=[ps_o])
                p.op("tensor", lambda e, kt=kt, pb=pb: e.matmul(
                    ps_l[0:1, :], onecb.ap, pb.ap,
                    start=(kt == 0), stop=(kt == 63)), ins=[onecb, pb], outs=[ps_l])
            p.op("scalar", lambda e: e.activation(lrow.ap, ps_l[0:1, :], AF.Copy),
                 ins=[ps_l], outs=[lrow])
            p.op("vector", lambda e: e.reciprocal(lrow.ap, lrow.ap),
                 ins=[lrow], outs=[lrow])
            p.op("scalar", lambda e: e.activation(linvb.ap, lrow.ap, AF.Copy),
                 ins=[lrow], outs=[linvb])
            p.op("tensor", lambda e: e.matmul(ps_s[0].ap, onerb.ap, linvb.ap,
                                              start=True, stop=True),
                 ins=[onerb, linvb], outs=[ps_s[0]])
            p.op("scalar", lambda e: e.activation(lb_sb.ap, ps_s[0].ap, AF.Copy),
                 ins=[ps_s[0]], outs=[lb_sb])
            p.op("vector", lambda e, h=h, qb=qb: e.tensor_mul(
                ysb[:, h * TPC + qb * 512: h * TPC + (qb + 1) * 512],
                ps_o.ap, lb_sb.ap), ins=[ps_o, lb_sb], outs=[ysb])

    for eo in range(NC):
        sl = woslab[eo % 2]
        p.op("gpsimd", lambda e, eo=eo, sl=sl: e.dma_start(
            sl.rearrange("p (dc e) -> p dc e", dc=NC),
            d_wo.rearrange("(dc p) e -> p dc e", p=128)[:, :, eo * 128:(eo + 1) * 128]),
            outs=[sl], dma=True)
        sb = s_sb[eo % 2]
        for tb in range(2):
            ps = ps_s[tb]
            def mmo(e, sl=sl, tb=tb, ps=ps):
                last = None
                for dc in range(NC):
                    last = e.matmul(ps.ap, sl[:, dc * 128:(dc + 1) * 128],
                                    ysb[:, dc * TPC + tb * 512: dc * TPC + (tb + 1) * 512],
                                    start=(dc == 0), stop=(dc == NC - 1))
                return last
            p.op("tensor", mmo, ins=[sl, ysb], outs=[ps])
            p.op("scalar", lambda e, tb=tb, ps=ps, sb=sb: e.activation(
                sb[:, tb * 512:(tb + 1) * 512], ps.ap, AF.Copy), ins=[ps], outs=[sb])
        out_toks.append(p.op("gpsimd", lambda e, eo=eo, sb=sb: e.dma_start(
            d_s[eo * 128:(eo + 1) * 128, :], sb.ap), ins=[sb], dma=True))

    p.finish(out_toks)
    p.build()
    return nc


def run_l1b(h, le):
    nc = build_l1b()
    knfull = np.concatenate([np.asarray(r["kn"]) for r in le], axis=1)
    v1full = np.concatenate([np.asarray(r["v1t"]) for r in le], axis=0)
    in_maps = []
    for c in range(NCORE):
        in_maps.append(dict(
            qn=np.asarray(le[c]["qn"]), knfull=knfull, v1full=v1full,
            wo1=h["wo1"], ones_row_bf=h["ones_row_bf"],
            ones_col_bf=h["ones_col_bf"], ident_f=h["ident_f"],
        ))
    res = run_bass_kernel_spmd(nc, in_maps, core_ids=list(range(NCORE)))
    return res.results




def build_l1c():
    nc = bass.Bass("TRN2", target_bir_lowering=False)
    d_xn = nc.dram_tensor("xn1", [D, TPC], BF16, kind="ExternalInput")
    d_x = nc.dram_tensor("x1T_c", [D, TPC], F32, kind="ExternalInput")
    d_s = nc.dram_tensor("s1_out", [D, TPC], BF16, kind="ExternalInput")
    d_gw = nc.dram_tensor("gwT", [D, NEXP], F32, kind="ExternalInput")
    d_wg = nc.dram_tensor("wg1", [NEXP, D, IEXP], F32, kind="ExternalInput")
    d_wu = nc.dram_tensor("wu1", [NEXP, D, IEXP], F32, kind="ExternalInput")
    d_wd = nc.dram_tensor("wd1", [NEXP, IEXP, D], F32, kind="ExternalInput")
    d_onerb = nc.dram_tensor("ones_row_bf", [1, 128], BF16, kind="ExternalInput")
    d_idf = nc.dram_tensor("ident_f", [128, 128], F32, kind="ExternalInput")
    d_scr = nc.dram_tensor("scr_comb", [NEXP, TPC], F32, kind="Internal")
    d_xo = nc.dram_tensor("xoT_c", [D, TPC], F32, kind="ExternalOutput")

    p = Prog(nc)
    out_toks = []
    xnc = [p.sbuf(f"sxn{c}", [128, TPC], BF16) for c in range(NC)]
    y_acc = p.sbuf("y_acc", [128, NC * 512], F32)
    abuf = p.sbuf("abuf", [128, 8 * 512], BF16)
    gslab = [p.sbuf(f"gslab{i}", [128, NC * 128], BF16) for i in range(2)]
    uslab = [p.sbuf(f"uslab{i}", [128, NC * 128], BF16) for i in range(2)]
    dslab = [p.sbuf(f"dslab{i}", [128, 8 * 128], BF16) for i in range(2)]
    gw_sb = p.sbuf("gw_sb", [128, NC * NEXP], BF16)
    onerb = p.sbuf("onerb", [1, 128], BF16)
    idf = p.sbuf("idf", [128, 128], F32)
    prob = p.sbuf("prob", [128, NEXP], F32)
    mask1 = p.sbuf("mask1", [128, NEXP], F32)
    p2 = p.sbuf("p2", [128, NEXP], F32)
    comb = p.sbuf("comb", [128, NEXP], F32)
    mxc = p.sbuf("mxc", [128, 1], F32)
    sumc = p.sbuf("sumc", [128, 1], F32)
    m1c = p.sbuf("m1c", [128, 1], F32)
    m2c = p.sbuf("m2c", [128, 1], F32)
    ct_sb = p.sbuf("ct_sb", [NEXP, 128], F32)
    crow = p.sbuf("crow", [1, TPC], F32)
    crowb = p.sbuf("crowb", [1, TPC], BF16)
    cb_sb = p.sbuf("cb_sb", [128, 512], BF16)
    gs = p.sbuf("gs", [128, 512], BF16)
    gsc = p.sbuf("gsc", [128, 512], BF16)
    gfin = p.sbuf("gfin", [128, 512], BF16)
    tmpb = p.sbuf("tmpb", [128, 512], BF16)
    xcb = [p.sbuf(f"xcb{i}", [128, 512], F32) for i in range(2)]
    scb = [p.sbuf(f"scb{i}", [128, 512], BF16) for i in range(2)]

    ps_misc = p.psum("ps_misc", [128, 512])
    ps_cb = p.psum("ps_cb", [128, 512])
    ps_g = [p.psum(f"ps_g{i}", [128, 512]) for i in range(2)]
    ps_u = [p.psum(f"ps_u{i}", [128, 512]) for i in range(2)]
    ps_y = [p.psum(f"ps_y{i}", [128, 512]) for i in range(2)]

    for c in range(NC):
        p.op("sync", lambda e, c=c: e.dma_start(xnc[c].ap, d_xn[c * 128:(c + 1) * 128, :]),
             outs=[xnc[c]], dma=True)
    p.op("gpsimd", lambda e: e.dma_start(
        gw_sb.rearrange("p (dc e) -> p dc e", dc=NC),
        d_gw.rearrange("(dc p) e -> p dc e", p=128)), outs=[gw_sb], dma=True)
    for b, d in ((onerb, d_onerb), (idf, d_idf)):
        p.op("sync", lambda e, b=b, d=d: e.dma_start(b.ap, d[:]), outs=[b], dma=True)

    dscr = Buf(d_scr)
    for tt in range(TPC // 128):
        def mmsc(e, tt=tt):
            last = None
            for dc in range(NC):
                last = e.matmul(ps_misc[:, 0:NEXP], xnc[dc][:, tt * 128:(tt + 1) * 128],
                                gw_sb[:, dc * NEXP:(dc + 1) * NEXP],
                                start=(dc == 0), stop=(dc == NC - 1))
            return last
        p.op("tensor", mmsc, ins=[gw_sb] + xnc, outs=[ps_misc])
        p.op("vector", lambda e: e.reduce_max(mxc.ap, ps_misc[:, 0:NEXP], AX.X),
             ins=[ps_misc], outs=[mxc])
        p.op("vector", lambda e: e.tensor_scalar_mul(mxc.ap, mxc.ap, -1.0),
             ins=[mxc], outs=[mxc])
        p.op("scalar", lambda e: e.activation(prob.ap, ps_misc[:, 0:NEXP], AF.Exp,
                                              bias=mxc.ap, accum_out=sumc.ap),
             ins=[ps_misc, mxc], outs=[prob, sumc])
        p.op("vector", lambda e: e.reciprocal(sumc.ap, sumc.ap), ins=[sumc], outs=[sumc])
        p.op("vector", lambda e: e.tensor_scalar_mul(prob.ap, prob.ap, sumc.ap),
             ins=[prob, sumc], outs=[prob])
        p.op("vector", lambda e: e.reduce_max(m1c.ap, prob.ap, AX.X),
             ins=[prob], outs=[m1c])
        p.op("vector", lambda e: e.tensor_scalar(mask1.ap, prob.ap, m1c.ap, None,
                                                 ALU.is_equal), ins=[prob, m1c], outs=[mask1])
        p.op("vector", lambda e: e.tensor_mul(p2.ap, prob.ap, mask1.ap),
             ins=[prob, mask1], outs=[p2])
        p.op("vector", lambda e: e.tensor_sub(p2.ap, prob.ap, p2.ap),
             ins=[prob, p2], outs=[p2])
        p.op("vector", lambda e: e.reduce_max(m2c.ap, p2.ap, AX.X), ins=[p2], outs=[m2c])
        p.op("vector", lambda e: e.tensor_scalar(p2.ap, p2.ap, m2c.ap, None,
                                                 ALU.is_equal), ins=[p2, m2c], outs=[p2])
        p.op("vector", lambda e: e.tensor_add(mask1.ap, mask1.ap, p2.ap),
             ins=[mask1, p2], outs=[mask1])
        p.op("vector", lambda e: e.tensor_mul(comb.ap, prob.ap, mask1.ap),
             ins=[prob, mask1], outs=[comb])
        p.op("vector", lambda e: e.tensor_add(m1c.ap, m1c.ap, m2c.ap),
             ins=[m1c, m2c], outs=[m1c])
        p.op("vector", lambda e: e.tensor_scalar_add(m1c.ap, m1c.ap, 1e-20),
             ins=[m1c], outs=[m1c])
        p.op("vector", lambda e: e.reciprocal(m1c.ap, m1c.ap), ins=[m1c], outs=[m1c])
        p.op("vector", lambda e: e.tensor_scalar_mul(comb.ap, comb.ap, m1c.ap),
             ins=[comb, m1c], outs=[comb])
        p.op("tensor", lambda e: e.transpose(ps_misc[0:NEXP, 0:128], comb.ap, idf.ap),
             ins=[comb, idf], outs=[ps_misc])
        p.op("scalar", lambda e: e.activation(ct_sb.ap, ps_misc[0:NEXP, 0:128], AF.Copy),
             ins=[ps_misc], outs=[ct_sb])
        out_toks_dummy = p.op("gpsimd", lambda e, tt=tt: e.dma_start(
            d_scr[:, tt * 128:(tt + 1) * 128], ct_sb.ap), ins=[ct_sb], outs=[dscr], dma=True)

    for tb in range(2):
        ts = slice(tb * 512, (tb + 1) * 512)
        for ex in range(NEXP):
            p.op("sync", lambda e, ex=ex: e.dma_start(crow.ap, d_scr[ex:ex + 1, :]),
                 ins=[dscr], outs=[crow], dma=True)
            p.op("scalar", lambda e: e.activation(crowb.ap, crow.ap, AF.Copy),
                 ins=[crow], outs=[crowb])
            p.op("tensor", lambda e, ts=ts: e.matmul(ps_cb.ap, onerb.ap, crowb[0:1, ts],
                                                     start=True, stop=True),
                 ins=[onerb, crowb], outs=[ps_cb])
            p.op("scalar", lambda e: e.activation(cb_sb.ap, ps_cb.ap, AF.Copy),
                 ins=[ps_cb], outs=[cb_sb])
            for ic in range(8):
                g_sl, u_sl = gslab[ic % 2], uslab[ic % 2]
                p.op("gpsimd", lambda e, ex=ex, ic=ic, g_sl=g_sl: e.dma_start(
                    g_sl.rearrange("p (dc i) -> p dc i", dc=NC),
                    d_wg[ex, :, :].rearrange("(dc p) i -> p dc i", p=128)[
                        :, :, ic * 128:(ic + 1) * 128]), outs=[g_sl], dma=True)
                p.op("gpsimd", lambda e, ex=ex, ic=ic, u_sl=u_sl: e.dma_start(
                    u_sl.rearrange("p (dc i) -> p dc i", dc=NC),
                    d_wu[ex, :, :].rearrange("(dc p) i -> p dc i", p=128)[
                        :, :, ic * 128:(ic + 1) * 128]), outs=[u_sl], dma=True)
                pg, pu = ps_g[ic % 2], ps_u[ic % 2]
                def mmg(e, g_sl=g_sl, ts=ts, pg=pg):
                    last = None
                    for dc in range(NC):
                        last = e.matmul(pg.ap, g_sl[:, dc * 128:(dc + 1) * 128],
                                        xnc[dc][:, ts], start=(dc == 0), stop=(dc == NC - 1))
                    return last
                p.op("tensor", mmg, ins=[g_sl] + xnc, outs=[pg])
                def mmu(e, u_sl=u_sl, ts=ts, pu=pu):
                    last = None
                    for dc in range(NC):
                        last = e.matmul(pu.ap, u_sl[:, dc * 128:(dc + 1) * 128],
                                        xnc[dc][:, ts], start=(dc == 0), stop=(dc == NC - 1))
                    return last
                p.op("tensor", mmu, ins=[u_sl] + xnc, outs=[pu])
                p.op("scalar", lambda e, pg=pg: e.activation(gs.ap, pg.ap, AF.Silu),
                     ins=[pg], outs=[gs])
                p.op("vector", lambda e: e.tensor_mul(gsc.ap, gs.ap, cb_sb.ap),
                     ins=[gs, cb_sb], outs=[gsc])
                p.op("vector", lambda e, ic=ic, pu=pu: e.tensor_mul(
                    abuf[:, ic * 512:(ic + 1) * 512], gsc.ap, pu.ap),
                    ins=[gsc, pu], outs=[abuf])
            for hc in range(NC):
                d_sl = dslab[hc % 2]
                p.op("gpsimd", lambda e, ex=ex, hc=hc, d_sl=d_sl: e.dma_start(
                    d_sl.rearrange("p (ic h) -> p ic h", ic=8),
                    d_wd[ex, :, :].rearrange("(ic p) h -> p ic h", p=128)[
                        :, :, hc * 128:(hc + 1) * 128]), outs=[d_sl], dma=True)
                py = ps_y[hc % 2]
                def mmd(e, d_sl=d_sl, py=py):
                    last = None
                    for ic in range(8):
                        last = e.matmul(py.ap, d_sl[:, ic * 128:(ic + 1) * 128],
                                        abuf[:, ic * 512:(ic + 1) * 512],
                                        start=(ic == 0), stop=(ic == 7))
                    return last
                p.op("tensor", mmd, ins=[d_sl, abuf], outs=[py])
                ya = slice(hc * 512, (hc + 1) * 512)
                if ex == 0:
                    p.op("scalar", lambda e, ya=ya, py=py: e.activation(
                        y_acc[:, ya], py.ap, AF.Copy), ins=[py], outs=[y_acc])
                else:
                    p.op("vector", lambda e, ya=ya, py=py: e.tensor_add(
                        y_acc[:, ya], y_acc[:, ya], py.ap), ins=[py, y_acc], outs=[y_acc])
        for hc in range(NC):
            ya = slice(hc * 512, (hc + 1) * 512)
            hs = slice(hc * 128, (hc + 1) * 128)
            xb, sb = xcb[hc % 2], scb[hc % 2]
            p.op("scalar", lambda e, ya=ya: e.activation(gfin.ap, y_acc[:, ya], AF.Silu),
                 ins=[y_acc], outs=[gfin])
            p.op("sync", lambda e, hs=hs, ts=ts, xb=xb: e.dma_start(xb.ap, d_x[hs, ts]),
                 outs=[xb], dma=True)
            p.op("sync", lambda e, hs=hs, ts=ts, sb=sb: e.dma_start(sb.ap, d_s[hs, ts]),
                 outs=[sb], dma=True)
            p.op("vector", lambda e, sb=sb: e.tensor_mul(tmpb.ap, gfin.ap, sb.ap),
                 ins=[gfin, sb], outs=[tmpb])
            p.op("vector", lambda e, xb=xb: e.tensor_add(xb.ap, xb.ap, tmpb.ap),
                 ins=[tmpb, xb], outs=[xb])
            out_toks.append(p.op("gpsimd", lambda e, hs=hs, ts=ts, xb=xb: e.dma_start(
                d_xo[hs, ts], xb.ap), ins=[xb], dma=True))

    p.finish(out_toks)
    p.build()
    return nc


def run_l1c(h, ld, le, lf):
    nc = build_l1c()
    in_maps = []
    for c in range(NCORE):
        in_maps.append(dict(
            xn1=np.asarray(le[c]["xn1"]),
            x1T_c=np.asarray(ld[c]["x1T_c"]),
            s1_out=np.asarray(lf[c]["s1_out"]),
            gwT=h["gwT"], wg1=h["wg1"], wu1=h["wu1"], wd1=h["wd1"],
            ones_row_bf=h["ones_row_bf"], ident_f=h["ident_f"],
        ))
    res = run_bass_kernel_spmd(nc, in_maps, core_ids=list(range(NCORE)))
    return res.results




def build_fused():
    nc = bass.Bass("TRN2", target_bir_lowering=False)
    NOC = INTER // 128

    d_x = nc.dram_tensor("xT_c", [D, TPC], F32, kind="ExternalInput")
    d_w = nc.dram_tensor("wqkvTp", [D, 3 * D], F32, kind="ExternalInput")
    d_cq = nc.dram_tensor("cos_q", [128, TPC], F32, kind="ExternalInput")
    d_sq = nc.dram_tensor("sinx_q", [128, TPC], F32, kind="ExternalInput")
    d_ck = nc.dram_tensor("cos_k", [128, TPC], F32, kind="ExternalInput")
    d_sk = nc.dram_tensor("sinx_k", [128, TPC], F32, kind="ExternalInput")
    d_onecf = nc.dram_tensor("ones_col_f", [128, 1], F32, kind="ExternalInput")
    d_onerf = nc.dram_tensor("ones_row_f", [1, 128], F32, kind="ExternalInput")
    d_onecb = nc.dram_tensor("ones_col_bf", [128, 1], BF16, kind="ExternalInput")
    d_onerb = nc.dram_tensor("ones_row_bf", [1, 128], BF16, kind="ExternalInput")
    d_idb = nc.dram_tensor("ident_bf", [128, 128], BF16, kind="ExternalInput")
    d_idf = nc.dram_tensor("ident_f", [128, 128], F32, kind="ExternalInput")
    d_wo = nc.dram_tensor("woT", [D, D], F32, kind="ExternalInput")
    d_msk = nc.dram_tensor("masks_c", [2, 64, 128, 512], BF16, kind="ExternalInput")
    d_gi = nc.dram_tensor("gidx_sl", [D, OPC], I32, kind="ExternalInput")
    d_ui = nc.dram_tensor("uidx_sl", [D, OPC], I32, kind="ExternalInput")
    d_di = nc.dram_tensor("didx_sl", [INTER, 256], I32, kind="ExternalInput")
    d_lut = nc.dram_tensor("lutrep", [128, 16], F32, kind="ExternalInput")
    d_wq1 = nc.dram_tensor("wq1", [D, D], F32, kind="ExternalInput")
    d_wk1 = nc.dram_tensor("wk1", [D, D], F32, kind="ExternalInput")
    d_wv1 = nc.dram_tensor("wv1", [D, D], F32, kind="ExternalInput")
    d_wo1 = nc.dram_tensor("wo1", [D, D], F32, kind="ExternalInput")
    d_qs = nc.dram_tensor("qscale", [128, 1], F32, kind="ExternalInput")
    d_ks = nc.dram_tensor("kscale", [128, 1], F32, kind="ExternalInput")
    d_gw = nc.dram_tensor("gwT", [D, NEXP], F32, kind="ExternalInput")
    d_wg1 = nc.dram_tensor("wg1", [NEXP, D, IEXP], F32, kind="ExternalInput")
    d_wu1 = nc.dram_tensor("wu1", [NEXP, D, IEXP], F32, kind="ExternalInput")
    d_wd1 = nc.dram_tensor("wd1", [NEXP, IEXP, D], F32, kind="ExternalInput")

    d_qf = nc.dram_tensor("i_qf", [D, TPC], BF16)
    d_kfl = nc.dram_tensor("i_kfl", [D, TPC], BF16)
    d_kfa = nc.dram_tensor("i_kfa", [NCORE * D, TPC], BF16)
    d_vtl = nc.dram_tensor("i_vtl", [TPC, D], BF16)
    d_vta = nc.dram_tensor("i_vta", [SEQ, D], BF16)
    d_k2l = nc.dram_tensor("i_k2l", [NH, 1], F32)
    d_k2a = nc.dram_tensor("i_k2a", [NCORE * NH, 1], F32)
    d_scrk2 = nc.dram_tensor("i_scrk2", [1, NH], F32)
    d_wgl = nc.dram_tensor("i_wgl", [D, OPC], BF16)
    d_wga = nc.dram_tensor("i_wga", [NCORE * D, OPC], BF16)
    d_wul = nc.dram_tensor("i_wul", [D, OPC], BF16)
    d_wua = nc.dram_tensor("i_wua", [NCORE * D, OPC], BF16)
    d_wdl = nc.dram_tensor("i_wdl", [INTER, 256], BF16)
    d_wda = nc.dram_tensor("i_wda", [NCORE * INTER, 256], BF16)
    d_qn = nc.dram_tensor("i_qn", [D, TPC], BF16)
    d_knl = nc.dram_tensor("i_knl", [D, TPC], BF16)
    d_kna = nc.dram_tensor("i_kna", [NCORE * D, TPC], BF16)
    d_v1l = nc.dram_tensor("i_v1l", [TPC, D], BF16)
    d_v1a = nc.dram_tensor("i_v1a", [SEQ, D], BF16)
    d_scrc = nc.dram_tensor("i_scrc", [NEXP, TPC], F32)
    d_x1s = nc.dram_tensor("i_x1s", [D, TPC], F32)

    d_xo = nc.dram_tensor("xoT_c", [D, TPC], F32, kind="ExternalOutput")

    p = Prog(nc)
    out_toks = []

    xnc = [p.sbuf(f"P_xn{c}", [128, TPC], BF16) for c in range(NC)]
    s_res = p.sbuf("P_s", [128, NC * TPC], BF16)
    onecf = p.sbuf("P_onecf", [128, 1], F32)
    onerf = p.sbuf("P_onerf", [1, 128], F32)
    onecb = p.sbuf("P_onecb", [128, 1], BF16)
    onerb = p.sbuf("P_onerb", [1, 128], BF16)
    idb = p.sbuf("P_idb", [128, 128], BF16)
    idf = p.sbuf("P_idf", [128, 128], F32)
    qscol = p.sbuf("P_qscol", [128, 1], F32)
    kscol = p.sbuf("P_kscol", [128, 1], F32)
    lut = p.sbuf("P_lut", [128, 16], F32)
    for b, d in ((onecf, d_onecf), (onerf, d_onerf), (onecb, d_onecb),
                 (onerb, d_onerb), (idb, d_idb), (idf, d_idf),
                 (qscol, d_qs), (kscol, d_ks), (lut, d_lut)):
        p.op("sync", lambda e, b=b, d=d: e.dma_start(b.ap, d[:]), outs=[b], dma=True)

    bkfl, bvtl, bk2l = Buf(d_kfl), Buf(d_vtl), Buf(d_k2l)
    bkfa, bvta, bk2a = Buf(d_kfa), Buf(d_vta), Buf(d_k2a)
    bqf = Buf(d_qf)
    bwgl, bwul, bwdl = Buf(d_wgl), Buf(d_wul), Buf(d_wdl)
    bwga, bwua, bwda = Buf(d_wga), Buf(d_wua), Buf(d_wda)
    bqn, bknl, bv1l = Buf(d_qn), Buf(d_knl), Buf(d_v1l)
    bkna, bv1a = Buf(d_kna), Buf(d_v1a)
    bscrk2, bscrc = Buf(d_scrk2), Buf(d_scrc)
    bx1s = Buf(d_x1s)


    cops = []
    cbufs = {}

    def make_construct():
        idxb = [p.sbuf(f"C_idx{i}", [128, OPC], I32) for i in range(2)]
        accb = [p.sbuf(f"C_acc{i}", [128, OPC], BF16) for i in range(2)]
        term = [p.sbuf(f"C_term{i}", [128, OPC], BF16) for i in range(2)]
        cbufs.update(idxb=idxb, accb=accb, term=term)

        def construct(d_idx, bout, d_out, nrow, ncol):
            nslab = (nrow + 127) // 128
            for s in range(nslab):
                r0, r1 = s * 128, min((s + 1) * 128, nrow)
                ib, ab, tb = idxb[s % 2], accb[s % 2], term[s % 2]
                np_ = r1 - r0
                cops.append(lambda d_idx=d_idx, r0=r0, r1=r1, ib=ib, ncol=ncol:
                            p.op("gpsimd", lambda e: e.dma_start(
                                ib[0:r1 - r0, 0:ncol], d_idx[r0:r1, :]),
                                outs=[ib], dma=True))
                for j in range(16):
                    if j == 0:
                        cops.append(lambda ib=ib, ab=ab, np_=np_, ncol=ncol:
                                    p.op("vector", lambda e: e.tensor_scalar(
                                        ab[0:np_, 0:ncol], ib[0:np_, 0:ncol], 0.0,
                                        lut[0:np_, 0:1], ALU.is_equal, ALU.mult),
                                        ins=[ib, lut], outs=[ab]))
                    else:
                        cops.append(lambda j=j, ib=ib, tb=tb, np_=np_, ncol=ncol:
                                    p.op("vector", lambda e: e.tensor_scalar(
                                        tb[0:np_, 0:ncol], ib[0:np_, 0:ncol], float(j),
                                        lut[0:np_, j:j + 1], ALU.is_equal, ALU.mult),
                                        ins=[ib, lut], outs=[tb]))
                        cops.append(lambda ab=ab, tb=tb, np_=np_, ncol=ncol:
                                    p.op("vector", lambda e: e.tensor_add(
                                        ab[0:np_, 0:ncol], ab[0:np_, 0:ncol],
                                        tb[0:np_, 0:ncol]), ins=[tb, ab], outs=[ab]))
                cops.append(lambda d_out=d_out, bout=bout, r0=r0, r1=r1, ab=ab,
                            ncol=ncol: p.op("gpsimd", lambda e: e.dma_start(
                                d_out[r0:r1, :], ab[0:r1 - r0, 0:ncol]),
                                ins=[ab], outs=[bout], dma=True))

        construct(d_gi, bwgl, d_wgl, D, OPC)
        construct(d_ui, bwul, d_wul, D, OPC)
        construct(d_di, bwdl, d_wdl, INTER, 256)

    def drain_cops(n):
        k = 0
        while cops and k < n:
            cops.pop(0)()
            k += 1

    def _s1():
        with p.scope():
            xc = [p.sbuf(f"A_x{c}", [128, TPC], F32) for c in range(NC)]
            cq = p.sbuf("A_cq", [128, TPC], F32)
            sq_t = p.sbuf("A_sq", [128, TPC], F32)
            ck = p.sbuf("A_ck", [128, TPC], F32)
            sk = p.sbuf("A_sk", [128, TPC], F32)
            rowA = p.sbuf("A_rowA", [1, TPC], F32)
            rowB = p.sbuf("A_rowB", [1, TPC], F32)
            rowC = p.sbuf("A_rowC", [1, TPC], F32)
            tmpc = p.sbuf("A_tmpc", [128, TPC], F32)
            wslab = [p.sbuf(f"A_wslab{i}", [128, NC * 128], BF16) for i in range(2)]
            vslab = [p.sbuf(f"A_vslab{i}", [128, NC * 256], BF16) for i in range(2)]
            qkh = [p.sbuf(f"A_qkh{i}", [128, TPC], BF16) for i in range(2)]
            ropetmp = p.sbuf("A_ropetmp", [128, 512], F32)
            ropecos = p.sbuf("A_ropecos", [128, 512], F32)
            vtt = [p.sbuf(f"A_vtt{i}", [128, 512], BF16) for i in range(2)]
            k2sq = p.sbuf("A_k2sq", [128, TPC], BF16)
            k2rowA = p.sbuf("A_k2row", [1, NH], F32)

            T1 = p.psum("A_T1", [128, TPC])
            T2 = p.psum("A_T2", [128, TPC])
            T3 = p.psum("A_T3", [128, 512])
            T4 = p.psum("A_T4", [128, 512])
            ps_qk = [T3, T4]
            ps_v = [T3, T4]

            for c in range(NC):
                p.op("sync", lambda e, c=c: e.dma_start(xc[c].ap, d_x[c * 128:(c + 1) * 128, :]),
                     outs=[xc[c]], dma=True)
            for b, d in ((cq, d_cq), (sq_t, d_sq), (ck, d_ck), (sk, d_sk)):
                p.op("sync", lambda e, b=b, d=d: e.dma_start(b.ap, d[:]), outs=[b], dma=True)

            sqc = tmpc
            for c in range(NC):
                p.op("scalar", lambda e, c=c: e.activation(sqc.ap, xc[c].ap, AF.Square),
                     ins=[xc[c]], outs=[sqc])
                def mm_sum(e, c=c):
                    last = None
                    for tb in range(2):
                        last = e.matmul(T1[0:1, tb * 512:(tb + 1) * 512], onecf.ap,
                                        xc[c][:, tb * 512:(tb + 1) * 512],
                                        start=(c == 0), stop=(c == NC - 1))
                    return last
                p.op("tensor", mm_sum, ins=[onecf, xc[c]], outs=[T1])
                def mm_sq(e, c=c):
                    last = None
                    for tb in range(2):
                        last = e.matmul(T2[0:1, tb * 512:(tb + 1) * 512], onecf.ap,
                                        sqc[:, tb * 512:(tb + 1) * 512],
                                        start=(c == 0), stop=(c == NC - 1))
                    return last
                p.op("tensor", mm_sq, ins=[onecf, sqc], outs=[T2])
            p.op("scalar", lambda e: e.activation(rowA.ap, T1[0:1, :], AF.Copy, scale=1.0 / D),
                 ins=[T1], outs=[rowA])
            p.op("scalar", lambda e: e.activation(rowB.ap, T2[0:1, :], AF.Copy, scale=1.0 / D),
                 ins=[T2], outs=[rowB])
            p.op("scalar", lambda e: e.activation(rowC.ap, rowA.ap, AF.Square),
                 ins=[rowA], outs=[rowC])
            p.op("vector", lambda e: e.tensor_sub(rowB.ap, rowB.ap, rowC.ap),
                 ins=[rowB, rowC], outs=[rowB])
            p.op("vector", lambda e: e.tensor_scalar_add(rowB.ap, rowB.ap, 1e-5),
                 ins=[rowB], outs=[rowB])
            p.op("scalar", lambda e: e.activation(rowC.ap, rowB.ap, AF.Sqrt),
                 ins=[rowB], outs=[rowC])
            p.op("vector", lambda e: e.reciprocal(rowB.ap, rowC.ap), ins=[rowC], outs=[rowB])
            p.op("vector", lambda e: e.tensor_mul(rowC.ap, rowA.ap, rowB.ap),
                 ins=[rowA, rowB], outs=[rowC])
            p.op("scalar", lambda e: e.activation(rowC.ap, rowC.ap, AF.Copy, scale=-1.0),
                 ins=[rowC], outs=[rowC])
            def bc_r(e):
                last = None
                for tb in range(2):
                    last = e.matmul(T1[:, tb * 512:(tb + 1) * 512], onerf.ap,
                                    rowB[0:1, tb * 512:(tb + 1) * 512], start=True, stop=True)
                return last
            p.op("tensor", bc_r, ins=[onerf, rowB], outs=[T1])
            def bc_m(e):
                last = None
                for tb in range(2):
                    last = e.matmul(T2[:, tb * 512:(tb + 1) * 512], onerf.ap,
                                    rowC[0:1, tb * 512:(tb + 1) * 512], start=True, stop=True)
                return last
            p.op("tensor", bc_m, ins=[onerf, rowC], outs=[T2])
            for c in range(NC):
                p.op("vector", lambda e, c=c: e.tensor_mul(tmpc.ap, xc[c].ap, T1.ap),
                     ins=[xc[c], T1], outs=[tmpc])
                p.op("vector", lambda e, c=c: e.tensor_add(xnc[c].ap, tmpc.ap, T2.ap),
                     ins=[tmpc, T2], outs=[xnc[c]])

            def emit_qk(off_e, cos_t, sin_t, d_out, bout, do_k2):
                for h in range(NH):
                    sl = wslab[h % 2]
                    hb = qkh[h % 2]
                    p.op("gpsimd", lambda e, h=h, sl=sl: e.dma_start(
                        sl.rearrange("p (dc e) -> p dc e", dc=NC),
                        d_w.rearrange("(dc p) e -> p dc e", p=128)[
                            :, :, off_e + h * 128: off_e + (h + 1) * 128]),
                        outs=[sl], dma=True)
                    for tb in range(2):
                        ps = ps_qk[tb]
                        def mm(e, sl=sl, tb=tb, ps=ps):
                            last = None
                            for dc in range(NC):
                                last = e.matmul(ps.ap, sl[:, dc * 128:(dc + 1) * 128],
                                                xnc[dc][:, tb * 512:(tb + 1) * 512],
                                                start=(dc == 0), stop=(dc == NC - 1))
                            return last
                        p.op("tensor", mm, ins=[sl] + xnc, outs=[ps])
                        ts = slice(tb * 512, (tb + 1) * 512)
                        p.op("vector", lambda e, ps=ps, ts=ts: e.tensor_mul(
                            ropetmp[0:64, :], ps[64:128, :], sin_t[0:64, ts]),
                            ins=[ps, sin_t], outs=[ropetmp])
                        p.op("vector", lambda e, ps=ps, ts=ts: e.tensor_mul(
                            ropetmp[64:128, :], ps[0:64, :], sin_t[64:128, ts]),
                            ins=[ps, sin_t], outs=[ropetmp])
                        p.op("vector", lambda e, ps=ps, ts=ts: e.tensor_mul(
                            ropecos.ap, ps.ap, cos_t[:, ts]),
                            ins=[ps, cos_t], outs=[ropecos])
                        p.op("vector", lambda e, hb=hb, ts=ts: e.tensor_add(
                            hb[:, ts], ropecos.ap, ropetmp.ap),
                            ins=[ropecos, ropetmp], outs=[hb])
                    p.op("gpsimd", lambda e, h=h, hb=hb, d_out=d_out: e.dma_start(
                        d_out[h * 128:(h + 1) * 128, :], hb.ap),
                        ins=[hb], outs=[bout], dma=True)
                    if do_k2:
                        p.op("scalar", lambda e, hb=hb: e.activation(k2sq.ap, hb.ap, AF.Square),
                             ins=[hb], outs=[k2sq])
                        def mmk2(e):
                            last = None
                            for tb in range(2):
                                last = e.matmul(T2[0:1, tb * 512:(tb + 1) * 512], onecb.ap,
                                                k2sq[:, tb * 512:(tb + 1) * 512],
                                                start=True, stop=True)
                            return last
                        p.op("tensor", mmk2, ins=[onecb, k2sq], outs=[T2])
                        p.op("vector", lambda e, h=h: e.reduce_max(
                            k2rowA[0:1, h:h + 1], T2[0:1, :], AX.X),
                            ins=[T2], outs=[k2rowA])

            emit_qk(0, cq, sq_t, d_qf, bqf, False)
            emit_qk(D, ck, sk, d_kfl, bkfl, True)
            p.op("gpsimd", lambda e: e.dma_start(d_k2l.rearrange("a b -> b a"), k2rowA.ap),
                 ins=[k2rowA], outs=[bk2l], dma=True)

            for eb in range(8):
                sl = vslab[eb % 2]
                p.op("gpsimd", lambda e, eb=eb, sl=sl: e.dma_start(
                    sl.rearrange("p (dc e) -> p dc e", dc=NC),
                    d_w.rearrange("(dc p) e -> p dc e", p=128)[
                        :, :, 2 * D + eb * 256: 2 * D + (eb + 1) * 256]),
                    outs=[sl], dma=True)
                for tt in range(TPC // 128):
                    ps = ps_v[tt % 2]
                    def mmv(e, tt=tt, sl=sl, ps=ps):
                        last = None
                        for dc in range(NC):
                            last = e.matmul(ps[:, 0:256], xnc[dc][:, tt * 128:(tt + 1) * 128],
                                            sl[:, dc * 256:(dc + 1) * 256],
                                            start=(dc == 0), stop=(dc == NC - 1))
                        return last
                    p.op("tensor", mmv, ins=[sl] + xnc, outs=[ps])
                    vt_b = vtt[tt % 2]
                    p.op("scalar", lambda e, ps=ps, vt_b=vt_b: e.activation(
                        vt_b[:, 0:256], ps[:, 0:256], AF.Copy), ins=[ps], outs=[vt_b])
                    p.op("gpsimd", lambda e, tt=tt, eb=eb, vt_b=vt_b: e.dma_start(
                        d_vtl[tt * 128:(tt + 1) * 128, eb * 256:(eb + 1) * 256],
                        vt_b[:, 0:256]), ins=[vt_b], outs=[bvtl], dma=True)


    _s1()

    def ag(din, dout, bin_, bout):
        p.cc(lambda e, din=din, dout=dout: e.collective_compute(
            "AllGather", ALU.bypass, replica_groups=[list(range(NCORE))],
            ins=[din.ap()], outs=[dout.ap()]), ins=[bin_], outs=[bout])

    ag(d_kfl, d_kfa, bkfl, bkfa)
    ag(d_vtl, d_vta, bvtl, bvta)
    ag(d_k2l, d_k2a, bk2l, bk2a)

    def _s2():
        with p.scope():
            make_construct()
            mskslab = p.sbuf("B_msk", [128, 64 * 512], BF16)
            ksb = p.sbuf("B_ksb", [128, SEQ], BF16)
            vsb = p.sbuf("B_vsb", [128, SEQ], BF16)
            qh_sb = [p.sbuf(f"B_qh{i}", [128, TPC], BF16) for i in range(2)]
            ysb = p.sbuf("B_ysb", [128, NH * 512], BF16)
            k2sb = p.sbuf("B_k2sb", [NH, NCORE], F32)
            k2colB = p.sbuf("B_k2col", [NH, 1], F32)
            k2row = p.sbuf("B_k2row", [1, NH], F32)
            sqh = p.sbuf("B_sqh", [128, 512], BF16)
            b2sc = p.sbuf("B_b2sc", [1, 1], F32)
            bscb = p.sbuf("B_bscb", [1, 1], BF16)
            negb = p.sbuf("B_negbc", [128, 1], F32)
            p_sb = [p.sbuf(f"B_p{i}", [128, 1024], BF16) for i in range(2)]
            l_accW = p.sbuf("B_laccW", [128, 1024], F32)
            lrow = p.sbuf("B_lrow", [1, 512], F32)
            linvb = p.sbuf("B_linvb", [1, 512], BF16)
            lb_sb = p.sbuf("B_lb", [128, 512], BF16)

            ps_s = [p.psum(f"B_ps{i}", [128, 1024]) for i in range(2)]
            ps_nb = p.psum("B_pnb", [128, 1])
            ps_o = p.psum("B_po", [128, 512])
            ps_q2 = p.psum("B_pq2", [1, TPC])
            ps_l = ps_q2

            p.op("sync", lambda e: e.dma_start(
                k2sb.ap, d_k2a.rearrange("(r h) c -> h (r c)", r=NCORE),
                allow_slow_non_contiguous=True), ins=[bk2a], outs=[k2sb], dma=True)
            p.op("vector", lambda e: e.reduce_max(k2colB.ap, k2sb.ap, AX.X),
                 ins=[k2sb], outs=[k2colB])
            p.op("gpsimd", lambda e: e.dma_start(d_scrk2.rearrange("a b -> b a"), k2colB.ap),
                 ins=[k2colB], outs=[bscrk2], dma=True)
            p.op("gpsimd", lambda e: e.dma_start(k2row.ap, d_scrk2[:]),
                 ins=[bscrk2], outs=[k2row], dma=True)

            kfa_v = d_kfa.rearrange("(r c) t -> c r t", r=NCORE)
            for qb in range(2):
                p.op("scalar", lambda e, qb=qb: e.dma_start(
                    mskslab.rearrange("p (kt f) -> p kt f", kt=64),
                    d_msk[qb, :, :, :].rearrange("kt p f -> p kt f")),
                    outs=[mskslab], dma=True)
                for h in range(NH):
                    qh = qh_sb[h % 2]
                    p.op("gpsimd", lambda e, h=h, qh=qh: e.dma_start(
                        qh.ap, d_qf[h * 128:(h + 1) * 128, :]), ins=[bqf], outs=[qh], dma=True)
                    for tb2 in range(2):
                        p.op("scalar", lambda e, qh=qh, tb2=tb2: e.activation(
                            sqh.ap, qh[:, tb2 * 512:(tb2 + 1) * 512], AF.Square),
                            ins=[qh], outs=[sqh])
                        p.op("tensor", lambda e, tb2=tb2: e.matmul(
                            ps_q2[0:1, tb2 * 512:(tb2 + 1) * 512], onecb.ap, sqh.ap,
                            start=True, stop=True), ins=[onecb, sqh], outs=[ps_q2])
                    p.op("vector", lambda e: e.reduce_max(b2sc.ap, ps_q2[0:1, :], AX.X),
                         ins=[ps_q2], outs=[b2sc])
                    p.op("vector", lambda e, h=h: e.tensor_scalar(
                        b2sc.ap, b2sc.ap, k2row[0:1, h:h + 1], 1.0404,
                        ALU.mult, ALU.mult), ins=[b2sc, k2row], outs=[b2sc])
                    p.op("scalar", lambda e: e.activation(bscb.ap, b2sc.ap, AF.Sqrt),
                         ins=[b2sc], outs=[bscb])
                    p.op("tensor", lambda e: e.matmul(ps_nb.ap, onerb.ap, bscb.ap,
                                                      start=True, stop=True),
                         ins=[onerb, bscb], outs=[ps_nb])
                    p.op("scalar", lambda e: e.activation(negb.ap, ps_nb.ap, AF.Copy,
                                                          scale=-1.0), ins=[ps_nb], outs=[negb])
                    p.op("sync", lambda e, h=h: e.dma_start(
                        ksb.rearrange("p (r t) -> p r t", r=NCORE),
                        kfa_v[h * 128:(h + 1) * 128, :, :]), ins=[bkfa], outs=[ksb], dma=True)
                    p.op("sync", lambda e, h=h: e.dma_start(
                        vsb.rearrange("p (kt d) -> p kt d", kt=64),
                        d_vta.rearrange("(kt p) d -> p kt d", p=128)[
                            :, :, h * 128:(h + 1) * 128]), ins=[bvta], outs=[vsb], dma=True)
                    def emit_pv(ktp, pb):
                        for half in range(2):
                            kt = 2 * ktp + half
                            p.op("tensor", lambda e, kt=kt, pb=pb, half=half: e.matmul(
                                ps_o.ap, vsb[:, kt * 128:(kt + 1) * 128],
                                pb[:, half * 512:(half + 1) * 512],
                                start=(kt == 0), stop=(kt == 63)), ins=[vsb, pb],
                                outs=[ps_o])
                        if ktp == 0:
                            p.op("vector", lambda e, pb=pb: e.tensor_copy(l_accW.ap, pb.ap),
                                 ins=[pb], outs=[l_accW])
                        else:
                            p.op("vector", lambda e, pb=pb: e.tensor_add(
                                l_accW.ap, l_accW.ap, pb.ap), ins=[pb], outs=[l_accW])
                    prev = None
                    for ktp in range(32):
                        ps = ps_s[ktp % 2]
                        pb = p_sb[ktp % 2]
                        def mms(e, qb=qb, ktp=ktp, ps=ps, qh=qh):
                            for half in range(2):
                                kt = 2 * ktp + half
                                e.matmul(ps[:, half * 512:(half + 1) * 512], idb.ap,
                                         mskslab[:, kt * 512:(kt + 1) * 512],
                                         start=True, stop=False)
                                last = e.matmul(ps[:, half * 512:(half + 1) * 512],
                                                ksb[:, kt * 128:(kt + 1) * 128],
                                                qh[:, qb * 512:(qb + 1) * 512],
                                                start=False, stop=True)
                            return last
                        p.op("tensor", mms, ins=[idb, mskslab, ksb, qh],
                             outs=[ps])
                        p.op("scalar", lambda e, ps=ps, pb=pb: e.activation(
                            pb.ap, ps.ap, AF.Exp, bias=negb.ap), ins=[ps, negb], outs=[pb])
                        if prev is not None:
                            emit_pv(*prev)
                        prev = (ktp, pb)
                    emit_pv(*prev)
                    p.op("vector", lambda e: e.tensor_add(
                        l_accW[:, 0:512], l_accW[:, 0:512], l_accW[:, 512:1024]),
                        ins=[l_accW], outs=[l_accW])
                    p.op("tensor", lambda e: e.matmul(ps_l[0:1, 0:512], onecf.ap,
                                                      l_accW[:, 0:512],
                                                      start=True, stop=True),
                         ins=[onecf, l_accW], outs=[ps_l])
                    p.op("scalar", lambda e: e.activation(lrow.ap, ps_l[0:1, 0:512], AF.Copy),
                         ins=[ps_l], outs=[lrow])
                    p.op("vector", lambda e: e.reciprocal(lrow.ap, lrow.ap),
                         ins=[lrow], outs=[lrow])
                    p.op("scalar", lambda e: e.activation(linvb.ap, lrow.ap, AF.Copy),
                         ins=[lrow], outs=[linvb])
                    p.op("tensor", lambda e: e.matmul(ps_s[0][:, 0:512], onerb.ap, linvb.ap,
                                                      start=True, stop=True),
                         ins=[onerb, linvb], outs=[ps_s[0]])
                    p.op("scalar", lambda e: e.activation(lb_sb.ap, ps_s[0][:, 0:512], AF.Copy),
                         ins=[ps_s[0]], outs=[lb_sb])
                    p.op("vector", lambda e, h=h: e.tensor_mul(
                        ysb[:, h * 512:(h + 1) * 512], ps_o.ap, lb_sb.ap),
                        ins=[ps_o, lb_sb], outs=[ysb])
                    drain_cops(75)
                for eo in range(NC):
                    off = (eo % 2) * 2048
                    p.op("gpsimd", lambda e, eo=eo, off=off: e.dma_start(
                        ksb[:, off:off + 2048].rearrange("p (dc e) -> p dc e", dc=NC),
                        d_wo.rearrange("(dc p) e -> p dc e", p=128)[
                            :, :, eo * 128:(eo + 1) * 128]), outs=[ksb], dma=True)
                    ps = ps_s[eo % 2]
                    def mmo(e, off=off, ps=ps):
                        last = None
                        for dc in range(NC):
                            last = e.matmul(ps[:, 0:512],
                                            ksb[:, off + dc * 128:off + (dc + 1) * 128],
                                            ysb[:, dc * 512:(dc + 1) * 512],
                                            start=(dc == 0), stop=(dc == NC - 1))
                        return last
                    p.op("tensor", mmo, ins=[ksb, ysb], outs=[ps])
                    p.op("scalar", lambda e, eo=eo, qb=qb, ps=ps: e.activation(
                        s_res[:, eo * TPC + qb * 512: eo * TPC + (qb + 1) * 512],
                        ps[:, 0:512], AF.Copy), ins=[ps], outs=[s_res])
            drain_cops(10 ** 9)


    _s2()

    ag(d_wgl, d_wga, bwgl, bwga)
    ag(d_wul, d_wua, bwul, bwua)
    ag(d_wdl, d_wda, bwdl, bwda)

    x1_scope = p.scope()
    x1_scope.__enter__()
    x1 = p.sbuf("P_x1", [128, NC * TPC], F32)

    def _s3():
        with p.scope():
            hbuf = p.sbuf("G_h", [128, NOC * 512], BF16)
            gslab = [p.sbuf(f"G_gs{i}", [128, NC * 128], BF16) for i in range(2)]
            uslab = [p.sbuf(f"G_us{i}", [128, NC * 128], BF16) for i in range(2)]
            dslab = [p.sbuf("G_ds0", [128, NOC * 128], BF16)] * 2
            gsil = p.sbuf("G_gsil", [128, 512], BF16)
            gcb = p.sbuf("G_gcb", [128, 512], BF16)
            tmpb = p.sbuf("G_tmpb", [128, 512], BF16)
            xcb = [p.sbuf(f"G_xcb{i}", [128, 512], F32) for i in range(2)]
            ps_g = [p.psum(f"G_pg{i}", [128, 512]) for i in range(2)]
            ps_u = [p.psum(f"G_pu{i}", [128, 512]) for i in range(2)]
            ps_d = [p.psum(f"G_pd{i}", [128, 512]) for i in range(2)]

            def ochunk_pieces(oc):
                lo, hi = oc * 128, (oc + 1) * 128
                out = []
                r0, r1 = lo // OPC, (hi - 1) // OPC
                for r in range(r0, r1 + 1):
                    a = max(lo, r * OPC) - r * OPC
                    b = min(hi, (r + 1) * OPC) - r * OPC
                    out.append((r, a, b, max(lo, r * OPC) - lo))
                return out

            wga_v = d_wga.rearrange("(r dc p) o -> p r dc o", r=NCORE, p=128)
            wua_v = d_wua.rearrange("(r dc p) o -> p r dc o", r=NCORE, p=128)
            wda_v = d_wda.rearrange("(r ic p) h -> p r ic h", r=NCORE, p=128)

            for tb in range(2):
                ts = slice(tb * 512, (tb + 1) * 512)
                for oc in range(NOC):
                    g_sl, u_sl = gslab[oc % 2], uslab[oc % 2]
                    pcs = ochunk_pieces(oc)

                    def mk_ld(sl, src_v, pcs=pcs):
                        def f(e):
                            out = []
                            for (r, a, b, off) in pcs:
                                out.append(e.dma_start(
                                    sl.rearrange("p (dc o) -> p dc o", dc=NC)[
                                        :, :, off:off + b - a],
                                    src_v[:, r, :, a:b]))
                            return out
                        f.n_dma = len(pcs)
                        return f
                    p.op("sync", mk_ld(g_sl, wga_v), ins=[bwga], outs=[g_sl], dma=True)
                    p.op("gpsimd", mk_ld(u_sl, wua_v), ins=[bwua], outs=[u_sl], dma=True)
                    pg, pu = ps_g[oc % 2], ps_u[oc % 2]
                    def mmg(e, g_sl=g_sl, ts=ts, pg=pg):
                        last = None
                        for dc in range(NC):
                            last = e.matmul(pg.ap, g_sl[:, dc * 128:(dc + 1) * 128],
                                            xnc[dc][:, ts], start=(dc == 0), stop=(dc == NC - 1))
                        return last
                    p.op("tensor", mmg, ins=[g_sl] + xnc, outs=[pg])
                    def mmu(e, u_sl=u_sl, ts=ts, pu=pu):
                        last = None
                        for dc in range(NC):
                            last = e.matmul(pu.ap, u_sl[:, dc * 128:(dc + 1) * 128],
                                            xnc[dc][:, ts], start=(dc == 0), stop=(dc == NC - 1))
                        return last
                    p.op("tensor", mmu, ins=[u_sl] + xnc, outs=[pu])
                    p.op("scalar", lambda e, pg=pg: e.activation(gsil.ap, pg.ap, AF.Silu),
                         ins=[pg], outs=[gsil])
                    p.op("vector", lambda e, oc=oc, pu=pu: e.tensor_mul(
                        hbuf[:, oc * 512:(oc + 1) * 512], gsil.ap, pu.ap),
                        ins=[gsil, pu], outs=[hbuf])
                for hc in range(NC):
                    d_sl = dslab[hc % 2]
                    rr, h0 = hc // 2, (hc % 2) * 128
                    p.op("sync", lambda e, d_sl=d_sl, rr=rr, h0=h0: e.dma_start(
                        d_sl.rearrange("p (ic h) -> p ic h", ic=NOC),
                        wda_v[:, rr, :, h0:h0 + 128]), ins=[bwda], outs=[d_sl], dma=True)
                    pd = ps_d[hc % 2]
                    def mmd(e, d_sl=d_sl, pd=pd):
                        last = None
                        for ic in range(NOC):
                            last = e.matmul(pd.ap, d_sl[:, ic * 128:(ic + 1) * 128],
                                            hbuf[:, ic * 512:(ic + 1) * 512],
                                            start=(ic == 0), stop=(ic == NOC - 1))
                        return last
                    p.op("tensor", mmd, ins=[d_sl, hbuf], outs=[pd])
                    p.op("scalar", lambda e, pd=pd: e.activation(gcb.ap, pd.ap, AF.Silu),
                         ins=[pd], outs=[gcb])
                    hs = slice(hc * 128, (hc + 1) * 128)
                    xb = xcb[hc % 2]
                    p.op("sync", lambda e, hs=hs, ts=ts, xb=xb: e.dma_start(
                        xb.ap, d_x[hs, ts]), outs=[xb], dma=True)
                    p.op("vector", lambda e, hc=hc, ts=ts: e.tensor_mul(
                        tmpb.ap, gcb.ap, s_res[:, hc * TPC + tb * 512:hc * TPC + (tb + 1) * 512]),
                        ins=[gcb, s_res], outs=[tmpb])
                    p.op("vector", lambda e, hc=hc, tb=tb, xb=xb: e.tensor_add(
                        x1[:, hc * TPC + tb * 512: hc * TPC + (tb + 1) * 512],
                        xb.ap, tmpb.ap), ins=[tmpb, xb], outs=[x1])
                    p.op("sync", lambda e, hc=hc, tb=tb, hs=hs, ts=ts: e.dma_start(
                        d_x1s[hs, ts],
                        x1[:, hc * TPC + tb * 512: hc * TPC + (tb + 1) * 512]),
                        ins=[x1], outs=[bx1s], dma=True)


    _s3()

    def _s4():
        with p.scope():
            rowA = p.sbuf("E_rowA", [1, TPC], F32)
            rowB = p.sbuf("E_rowB", [1, TPC], F32)
            rowC = p.sbuf("E_rowC", [1, TPC], F32)
            tmpc = p.sbuf("E_tmpc", [128, TPC], F32)
            wslab = [p.sbuf(f"E_wslab{i}", [128, NC * 128], BF16) for i in range(2)]
            vslab = [p.sbuf(f"E_vslab{i}", [128, NC * 512], BF16) for i in range(2)]
            qkh = [p.sbuf(f"E_qkh{i}", [128, TPC], BF16) for i in range(2)]
            sqh = p.sbuf("E_sqh", [128, 512], BF16)
            r512 = p.sbuf("E_r512", [1, 512], F32)
            rbf = p.sbuf("E_rbf", [1, 512], BF16)
            rb_sb = p.sbuf("E_rb", [128, 512], BF16)
            vtt = [p.sbuf(f"E_vtt{i}", [128, 512], BF16) for i in range(2)]
            T1 = p.psum("E_T1", [128, TPC])
            T2 = p.psum("E_T2", [128, TPC])
            T3 = p.psum("E_T3", [128, 512])
            T4 = p.psum("E_T4", [128, 512])
            ps_qk = [T3, T4]
            ps_v = [T3, T4]

            def x1s(c):
                return x1[:, c * TPC:(c + 1) * TPC]

            sqc = tmpc
            for c in range(NC):
                p.op("scalar", lambda e, c=c: e.activation(sqc.ap, x1s(c), AF.Square),
                     ins=[x1], outs=[sqc])
                def mm_sum(e, c=c):
                    last = None
                    for tb in range(2):
                        last = e.matmul(T1[0:1, tb * 512:(tb + 1) * 512], onecf.ap,
                                        x1[:, c * TPC + tb * 512: c * TPC + (tb + 1) * 512],
                                        start=(c == 0), stop=(c == NC - 1))
                    return last
                p.op("tensor", mm_sum, ins=[onecf, x1], outs=[T1])
                def mm_sq(e, c=c):
                    last = None
                    for tb in range(2):
                        last = e.matmul(T2[0:1, tb * 512:(tb + 1) * 512], onecf.ap,
                                        sqc[:, tb * 512:(tb + 1) * 512],
                                        start=(c == 0), stop=(c == NC - 1))
                    return last
                p.op("tensor", mm_sq, ins=[onecf, sqc], outs=[T2])
            p.op("scalar", lambda e: e.activation(rowA.ap, T1[0:1, :], AF.Copy, scale=1.0 / D),
                 ins=[T1], outs=[rowA])
            p.op("scalar", lambda e: e.activation(rowB.ap, T2[0:1, :], AF.Copy, scale=1.0 / D),
                 ins=[T2], outs=[rowB])
            p.op("scalar", lambda e: e.activation(rowC.ap, rowA.ap, AF.Square),
                 ins=[rowA], outs=[rowC])
            p.op("vector", lambda e: e.tensor_sub(rowB.ap, rowB.ap, rowC.ap),
                 ins=[rowB, rowC], outs=[rowB])
            p.op("vector", lambda e: e.tensor_scalar_add(rowB.ap, rowB.ap, 1e-5),
                 ins=[rowB], outs=[rowB])
            p.op("scalar", lambda e: e.activation(rowC.ap, rowB.ap, AF.Sqrt),
                 ins=[rowB], outs=[rowC])
            p.op("vector", lambda e: e.reciprocal(rowB.ap, rowC.ap), ins=[rowC], outs=[rowB])
            p.op("vector", lambda e: e.tensor_mul(rowC.ap, rowA.ap, rowB.ap),
                 ins=[rowA, rowB], outs=[rowC])
            p.op("scalar", lambda e: e.activation(rowC.ap, rowC.ap, AF.Copy, scale=-1.0),
                 ins=[rowC], outs=[rowC])
            def bc_r(e):
                last = None
                for tb in range(2):
                    last = e.matmul(T1[:, tb * 512:(tb + 1) * 512], onerf.ap,
                                    rowB[0:1, tb * 512:(tb + 1) * 512], start=True, stop=True)
                return last
            p.op("tensor", bc_r, ins=[onerf, rowB], outs=[T1])
            def bc_m(e):
                last = None
                for tb in range(2):
                    last = e.matmul(T2[:, tb * 512:(tb + 1) * 512], onerf.ap,
                                    rowC[0:1, tb * 512:(tb + 1) * 512], start=True, stop=True)
                return last
            p.op("tensor", bc_m, ins=[onerf, rowC], outs=[T2])
            for c in range(NC):
                p.op("vector", lambda e, c=c: e.tensor_mul(tmpc.ap, x1s(c), T1.ap),
                     ins=[x1, T1], outs=[tmpc])
                p.op("vector", lambda e, c=c: e.tensor_add(xnc[c].ap, tmpc.ap, T2.ap),
                     ins=[tmpc, T2], outs=[xnc[c]])

            def emit_qk1(d_wgt, scol, d_out, bout):
                for h in range(NH):
                    sl = wslab[h % 2]
                    hb = qkh[h % 2]
                    p.op("gpsimd", lambda e, h=h, sl=sl, d_wgt=d_wgt: e.dma_start(
                        sl.rearrange("p (dc e) -> p dc e", dc=NC),
                        d_wgt.rearrange("(dc p) e -> p dc e", p=128)[
                            :, :, h * 128:(h + 1) * 128]), outs=[sl], dma=True)
                    for tb in range(2):
                        ps = ps_qk[tb]
                        def mm(e, sl=sl, tb=tb, ps=ps):
                            last = None
                            for dc in range(NC):
                                last = e.matmul(ps.ap, sl[:, dc * 128:(dc + 1) * 128],
                                                xnc[dc][:, tb * 512:(tb + 1) * 512],
                                                start=(dc == 0), stop=(dc == NC - 1))
                            return last
                        p.op("tensor", mm, ins=[sl] + xnc, outs=[ps])
                        p.op("scalar", lambda e, ps=ps: e.activation(sqh.ap, ps.ap, AF.Square),
                             ins=[ps], outs=[sqh])
                        p.op("tensor", lambda e: e.matmul(T2[0:1, 0:512], onecb.ap, sqh.ap,
                                                          start=True, stop=True),
                             ins=[onecb, sqh], outs=[T2])
                        p.op("scalar", lambda e: e.activation(r512.ap, T2[0:1, 0:512],
                                                              AF.Copy, scale=1.0 / DH),
                             ins=[T2], outs=[r512])
                        p.op("vector", lambda e: e.tensor_scalar_add(r512.ap, r512.ap, 1e-6),
                             ins=[r512], outs=[r512])
                        p.op("scalar", lambda e: e.activation(r512.ap, r512.ap, AF.Sqrt),
                             ins=[r512], outs=[r512])
                        p.op("vector", lambda e: e.reciprocal(r512.ap, r512.ap),
                             ins=[r512], outs=[r512])
                        p.op("scalar", lambda e: e.activation(rbf.ap, r512.ap, AF.Copy),
                             ins=[r512], outs=[rbf])
                        p.op("tensor", lambda e: e.matmul(T1[:, 0:512], onerb.ap, rbf.ap,
                                                          start=True, stop=True),
                             ins=[onerb, rbf], outs=[T1])
                        p.op("scalar", lambda e: e.activation(rb_sb.ap, T1[:, 0:512], AF.Copy),
                             ins=[T1], outs=[rb_sb])
                        ts = slice(tb * 512, (tb + 1) * 512)
                        p.op("vector", lambda e, hb=hb, ts=ts, ps=ps, scol=scol:
                             e.scalar_tensor_tensor(hb[:, ts], ps.ap, scol.ap, rb_sb.ap,
                                                    ALU.mult, ALU.mult),
                             ins=[ps, scol, rb_sb], outs=[hb])
                    p.op("gpsimd", lambda e, h=h, hb=hb, d_out=d_out, bout=bout: e.dma_start(
                        d_out[h * 128:(h + 1) * 128, :], hb.ap), ins=[hb], outs=[bout], dma=True)

            emit_qk1(d_wq1, qscol, d_qn, bqn)
            emit_qk1(d_wk1, kscol, d_knl, bknl)

            for eb in range(4):
                sl = vslab[eb % 2]
                p.op("gpsimd", lambda e, eb=eb, sl=sl: e.dma_start(
                    sl.rearrange("p (dc e) -> p dc e", dc=NC),
                    d_wv1.rearrange("(dc p) e -> p dc e", p=128)[
                        :, :, eb * 512:(eb + 1) * 512]), outs=[sl], dma=True)
                for tt in range(TPC // 128):
                    ps = ps_v[tt % 2]
                    def mmv(e, tt=tt, sl=sl, ps=ps):
                        last = None
                        for dc in range(NC):
                            last = e.matmul(ps.ap, xnc[dc][:, tt * 128:(tt + 1) * 128],
                                            sl[:, dc * 512:(dc + 1) * 512],
                                            start=(dc == 0), stop=(dc == NC - 1))
                        return last
                    p.op("tensor", mmv, ins=[sl] + xnc, outs=[ps])
                    vt_b = vtt[tt % 2]
                    p.op("scalar", lambda e, ps=ps, vt_b=vt_b: e.activation(
                        vt_b.ap, ps.ap, AF.Copy), ins=[ps], outs=[vt_b])
                    p.op("gpsimd", lambda e, tt=tt, eb=eb, vt_b=vt_b: e.dma_start(
                        d_v1l[tt * 128:(tt + 1) * 128, eb * 512:(eb + 1) * 512], vt_b.ap),
                        ins=[vt_b], outs=[bv1l], dma=True)


    _s4()

    ag(d_knl, d_kna, bknl, bkna)
    ag(d_v1l, d_v1a, bv1l, bv1a)

    def _s5():
        with p.scope():
            ksb = p.sbuf("F_ksb", [128, SEQ], BF16)
            vsb = p.sbuf("F_vsb", [128, SEQ], BF16)
            qh_sb = [p.sbuf("F_qh0", [128, TPC], BF16)] * 2
            ysb = p.sbuf("F_ysb", [128, NH * TPC], BF16)
            m_all = p.sbuf("F_mall", [128, 16], F32)
            mcol = p.sbuf("F_mcol", [128, 1], F32)
            mrun = p.sbuf("F_mrun", [128, 1], F32)
            msc = p.sbuf("F_msc", [1, 1], F32)
            mscb = p.sbuf("F_mscb", [1, 1], BF16)
            negm = p.sbuf("F_negmc", [128, 1], F32)
            p_sb = [p.sbuf(f"F_p{i}", [128, 1024], BF16) for i in range(2)]
            l_accW = p.sbuf("F_laccW", [128, 1024], F32)
            l_acc = [Buf(l_accW.h), Buf(l_accW.h)]
            lrow = p.sbuf("F_lrow", [1, 512], F32)
            linvb = p.sbuf("F_linvb", [1, 512], BF16)
            ps_s = [p.psum(f"F_ps{i}", [128, 1024]) for i in range(2)]
            ps_o = [p.psum(f"F_po{i}", [128, 512]) for i in range(2)]
            ps_nb = p.psum("F_pnb", [128, 1])
            ps_lt = p.psum("F_plt", [1, 512])

            kna_v = d_kna.rearrange("(r c) t -> c r t", r=NCORE)
            for h in range(NH):
                qh = qh_sb[h % 2]
                p.op("gpsimd", lambda e, h=h, qh=qh: e.dma_start(
                    qh.ap, d_qn[h * 128:(h + 1) * 128, :]), ins=[bqn], outs=[qh], dma=True)
                p.op("sync", lambda e, h=h: e.dma_start(
                    ksb.rearrange("p (r t) -> p r t", r=NCORE),
                    kna_v[h * 128:(h + 1) * 128, :, :]), ins=[bkna], outs=[ksb], dma=True)
                p.op("sync", lambda e, h=h: e.dma_start(
                    vsb.rearrange("p (kt d) -> p kt d", kt=64),
                    d_v1a.rearrange("(kt p) d -> p kt d", p=128)[
                        :, :, h * 128:(h + 1) * 128]), ins=[bv1a], outs=[vsb], dma=True)
                for qt in range(8):
                    q0 = qt * 128
                    for kc in range(16):
                        ps = ps_s[kc % 2]
                        p.op("tensor", lambda e, q0=q0, kc=kc, ps=ps, qh=qh:
                             e.matmul(ps[:, 0:512], qh[:, q0:q0 + 128],
                                      ksb[:, kc * 512:(kc + 1) * 512],
                                      start=True, stop=True), ins=[qh, ksb], outs=[ps])
                        p.op("vector", lambda e, kc=kc, ps=ps: e.reduce_max(
                            m_all[:, kc:kc + 1], ps[:, 0:512], AX.X), ins=[ps], outs=[m_all])
                    p.op("vector", lambda e: e.reduce_max(mcol.ap, m_all.ap, AX.X),
                         ins=[m_all], outs=[mcol])
                    if qt == 0:
                        p.op("vector", lambda e: e.tensor_copy(mrun.ap, mcol.ap),
                             ins=[mcol], outs=[mrun])
                    else:
                        p.op("vector", lambda e: e.tensor_max(mrun.ap, mrun.ap, mcol.ap),
                             ins=[mcol, mrun], outs=[mrun])
                p.op("tensor", lambda e: e.transpose(ps_lt[0:1, 0:128], mrun.ap, idf.ap),
                     ins=[mrun, idf], outs=[ps_lt])
                p.op("scalar", lambda e: e.activation(lrow[0:1, 0:128], ps_lt[0:1, 0:128],
                                                      AF.Copy), ins=[ps_lt], outs=[lrow])
                p.op("vector", lambda e: e.reduce_max(msc.ap, lrow[0:1, 0:128], AX.X),
                     ins=[lrow], outs=[msc])
                p.op("scalar", lambda e: e.activation(mscb.ap, msc.ap, AF.Copy),
                     ins=[msc], outs=[mscb])
                p.op("tensor", lambda e: e.matmul(ps_nb.ap, onerb.ap, mscb.ap,
                                                  start=True, stop=True),
                     ins=[onerb, mscb], outs=[ps_nb])
                p.op("scalar", lambda e: e.activation(negm.ap, ps_nb.ap, AF.Copy,
                                                      scale=-1.0), ins=[ps_nb], outs=[negm])
                def emit_pv(kt, pb):
                    for qb in range(2):
                        p.op("tensor", lambda e, kt=kt, pb=pb, qb=qb: e.matmul(
                            ps_o[qb].ap, vsb[:, kt * 128:(kt + 1) * 128],
                            pb[:, qb * 512:(qb + 1) * 512],
                            start=(kt == 0), stop=(kt == 63)), ins=[vsb, pb],
                            outs=[ps_o[qb]])
                    if kt == 0:
                        p.op("vector", lambda e, pb=pb: e.tensor_copy(l_accW.ap, pb.ap),
                             ins=[pb], outs=[l_accW])
                    else:
                        p.op("vector", lambda e, pb=pb: e.tensor_add(
                            l_accW.ap, l_accW.ap, pb.ap), ins=[pb], outs=[l_accW])
                prev = None
                for kt in range(64):
                    ps = ps_s[kt % 2]
                    pb = p_sb[kt % 2]
                    def mms(e, kt=kt, ps=ps, qh=qh):
                        e.matmul(ps[:, 0:512], ksb[:, kt * 128:(kt + 1) * 128],
                                 qh[:, 0:512], start=True, stop=True)
                        return e.matmul(ps[:, 512:1024], ksb[:, kt * 128:(kt + 1) * 128],
                                        qh[:, 512:1024], start=True, stop=True)
                    p.op("tensor", mms, ins=[ksb, qh], outs=[ps])
                    p.op("scalar", lambda e, ps=ps, pb=pb: e.activation(
                        pb.ap, ps.ap, AF.Exp, bias=negm.ap), ins=[ps, negm], outs=[pb])
                    if prev is not None:
                        emit_pv(*prev)
                    prev = (kt, pb)
                emit_pv(*prev)
                for qb in range(2):
                    p.op("tensor", lambda e, qb=qb: e.matmul(
                        ps_lt[0:1, :], onecf.ap, l_accW[:, qb * 512:(qb + 1) * 512],
                        start=True, stop=True), ins=[onecf, l_accW], outs=[ps_lt])
                    p.op("scalar", lambda e: e.activation(lrow.ap, ps_lt[0:1, :], AF.Copy),
                         ins=[ps_lt], outs=[lrow])
                    p.op("vector", lambda e: e.reciprocal(lrow.ap, lrow.ap),
                         ins=[lrow], outs=[lrow])
                    p.op("scalar", lambda e: e.activation(linvb.ap, lrow.ap, AF.Copy),
                         ins=[lrow], outs=[linvb])
                    p.op("tensor", lambda e, qb=qb: e.matmul(
                        ps_s[0][:, 0:512], onerb.ap, linvb.ap, start=True, stop=True),
                         ins=[onerb, linvb], outs=[ps_s[0]])
                    p.op("scalar", lambda e, qb=qb: e.activation(
                        l_accW[:, qb * 512:(qb + 1) * 512], ps_s[0][:, 0:512], AF.Copy),
                        ins=[ps_s[0]], outs=[l_accW])
                    p.op("vector", lambda e, h=h, qb=qb: e.tensor_mul(
                        ysb[:, h * TPC + qb * 512: h * TPC + (qb + 1) * 512],
                        ps_o[qb].ap, l_accW[:, qb * 512:(qb + 1) * 512]),
                        ins=[ps_o[qb], l_accW], outs=[ysb])
            for qb in range(2):
                for eo in range(NC):
                    off = (eo % 2) * 2048
                    p.op("gpsimd", lambda e, eo=eo, off=off: e.dma_start(
                        ksb[:, off:off + 2048].rearrange("p (dc e) -> p dc e", dc=NC),
                        d_wo1.rearrange("(dc p) e -> p dc e", p=128)[
                            :, :, eo * 128:(eo + 1) * 128]), outs=[ksb], dma=True)
                    ps = ps_s[eo % 2]
                    def mmo(e, off=off, ps=ps, qb=qb):
                        last = None
                        for dc in range(NC):
                            last = e.matmul(ps[:, 0:512],
                                            ksb[:, off + dc * 128:off + (dc + 1) * 128],
                                            ysb[:, dc * TPC + qb * 512: dc * TPC + (qb + 1) * 512],
                                            start=(dc == 0), stop=(dc == NC - 1))
                        return last
                    p.op("tensor", mmo, ins=[ksb, ysb], outs=[ps])
                    p.op("scalar", lambda e, eo=eo, qb=qb, ps=ps: e.activation(
                        s_res[:, eo * TPC + qb * 512: eo * TPC + (qb + 1) * 512],
                        ps[:, 0:512], AF.Copy), ins=[ps], outs=[s_res])
    _s5()

    x1_scope.__exit__(None, None, None)

    def _s6():
        with p.scope():
            y_acc2 = [p.sbuf(f"M_yacc{t}", [128, NC * 512], F32) for t in range(2)]
            abuf2 = [p.sbuf(f"M_abuf{t}", [128, 8 * 512], BF16) for t in range(2)]
            gslab = [p.sbuf(f"M_gs{i}", [128, NC * 128], BF16) for i in range(2)]
            uslab = [p.sbuf(f"M_us{i}", [128, NC * 128], BF16) for i in range(2)]
            dslab = [p.sbuf(f"M_ds{i}", [128, 8 * 128], BF16) for i in range(2)]
            gw_sb = p.sbuf("M_gw", [128, NC * NEXP], BF16)
            prob = p.sbuf("M_prob", [128, NEXP], F32)
            mask1 = p.sbuf("M_mask1", [128, NEXP], F32)
            p2 = p.sbuf("M_p2", [128, NEXP], F32)
            comb = p.sbuf("M_comb", [128, NEXP], F32)
            mxc = p.sbuf("M_mxc", [128, 1], F32)
            sumc = p.sbuf("M_sumc", [128, 1], F32)
            m1c = p.sbuf("M_m1c", [128, 1], F32)
            m2c = p.sbuf("M_m2c", [128, 1], F32)
            ct_sb = p.sbuf("M_ct", [NEXP, 128], F32)
            crow = p.sbuf("M_crow", [1, TPC], F32)
            crowb = p.sbuf("M_crowb", [1, TPC], BF16)
            cb_sb2 = [p.sbuf(f"M_cb{t}", [128, 512], BF16) for t in range(2)]
            gsl_ = p.sbuf("M_gsl", [128, 512], BF16)
            gsc = p.sbuf("M_gsc", [128, 512], BF16)
            gfin = p.sbuf("M_gfin", [128, 512], BF16)
            tmpb = p.sbuf("M_tmpb", [128, 512], BF16)
            xcb = [p.sbuf(f"M_xcb{i}", [128, 512], F32) for i in range(2)]
            ps_misc = p.psum("M_pm", [128, 512])
            ps_cb = p.psum("M_pcb", [128, 512])
            ps_g = [p.psum(f"M_pg{i}", [128, 512]) for i in range(2)]
            ps_u = [p.psum(f"M_pu{i}", [128, 512]) for i in range(2)]
            ps_y = [p.psum(f"M_py{i}", [128, 512]) for i in range(2)]

            p.op("gpsimd", lambda e: e.dma_start(
                gw_sb.rearrange("p (dc e) -> p dc e", dc=NC),
                d_gw.rearrange("(dc p) e -> p dc e", p=128)), outs=[gw_sb], dma=True)

            for tt in range(TPC // 128):
                def mmsc(e, tt=tt):
                    last = None
                    for dc in range(NC):
                        last = e.matmul(ps_misc[:, 0:NEXP],
                                        xnc[dc][:, tt * 128:(tt + 1) * 128],
                                        gw_sb[:, dc * NEXP:(dc + 1) * NEXP],
                                        start=(dc == 0), stop=(dc == NC - 1))
                    return last
                p.op("tensor", mmsc, ins=[gw_sb] + xnc, outs=[ps_misc])
                p.op("vector", lambda e: e.reduce_max(mxc.ap, ps_misc[:, 0:NEXP], AX.X),
                     ins=[ps_misc], outs=[mxc])
                p.op("vector", lambda e: e.tensor_scalar_mul(mxc.ap, mxc.ap, -1.0),
                     ins=[mxc], outs=[mxc])
                p.op("scalar", lambda e: e.activation(prob.ap, ps_misc[:, 0:NEXP], AF.Exp,
                                                      bias=mxc.ap, accum_out=sumc.ap),
                     ins=[ps_misc, mxc], outs=[prob, sumc])
                p.op("vector", lambda e: e.reciprocal(sumc.ap, sumc.ap), ins=[sumc], outs=[sumc])
                p.op("vector", lambda e: e.tensor_scalar_mul(prob.ap, prob.ap, sumc.ap),
                     ins=[prob, sumc], outs=[prob])
                p.op("vector", lambda e: e.reduce_max(m1c.ap, prob.ap, AX.X),
                     ins=[prob], outs=[m1c])
                p.op("vector", lambda e: e.tensor_scalar(mask1.ap, prob.ap, m1c.ap, None,
                                                         ALU.is_equal),
                     ins=[prob, m1c], outs=[mask1])
                p.op("vector", lambda e: e.tensor_mul(p2.ap, prob.ap, mask1.ap),
                     ins=[prob, mask1], outs=[p2])
                p.op("vector", lambda e: e.tensor_sub(p2.ap, prob.ap, p2.ap),
                     ins=[prob, p2], outs=[p2])
                p.op("vector", lambda e: e.reduce_max(m2c.ap, p2.ap, AX.X), ins=[p2], outs=[m2c])
                p.op("vector", lambda e: e.tensor_scalar(p2.ap, p2.ap, m2c.ap, None,
                                                         ALU.is_equal), ins=[p2, m2c], outs=[p2])
                p.op("vector", lambda e: e.tensor_add(mask1.ap, mask1.ap, p2.ap),
                     ins=[mask1, p2], outs=[mask1])
                p.op("vector", lambda e: e.tensor_mul(comb.ap, prob.ap, mask1.ap),
                     ins=[prob, mask1], outs=[comb])
                p.op("vector", lambda e: e.tensor_add(m1c.ap, m1c.ap, m2c.ap),
                     ins=[m1c, m2c], outs=[m1c])
                p.op("vector", lambda e: e.tensor_scalar_add(m1c.ap, m1c.ap, 1e-20),
                     ins=[m1c], outs=[m1c])
                p.op("vector", lambda e: e.reciprocal(m1c.ap, m1c.ap), ins=[m1c], outs=[m1c])
                p.op("vector", lambda e: e.tensor_scalar_mul(comb.ap, comb.ap, m1c.ap),
                     ins=[comb, m1c], outs=[comb])
                p.op("tensor", lambda e: e.transpose(ps_misc[0:NEXP, 0:128], comb.ap, idf.ap),
                     ins=[comb, idf], outs=[ps_misc])
                p.op("scalar", lambda e: e.activation(ct_sb.ap, ps_misc[0:NEXP, 0:128],
                                                      AF.Copy), ins=[ps_misc], outs=[ct_sb])
                p.op("gpsimd", lambda e, tt=tt: e.dma_start(
                    d_scrc[:, tt * 128:(tt + 1) * 128], ct_sb.ap),
                    ins=[ct_sb], outs=[bscrc], dma=True)

            for ex in range(NEXP):
                p.op("sync", lambda e, ex=ex: e.dma_start(crow.ap, d_scrc[ex:ex + 1, :]),
                     ins=[bscrc], outs=[crow], dma=True)
                p.op("scalar", lambda e: e.activation(crowb.ap, crow.ap, AF.Copy),
                     ins=[crow], outs=[crowb])
                for tb in range(2):
                    ts = slice(tb * 512, (tb + 1) * 512)
                    p.op("tensor", lambda e, ts=ts: e.matmul(ps_cb.ap, onerb.ap,
                                                             crowb[0:1, ts],
                                                             start=True, stop=True),
                         ins=[onerb, crowb], outs=[ps_cb])
                    p.op("scalar", lambda e, tb=tb: e.activation(cb_sb2[tb].ap, ps_cb.ap,
                                                                 AF.Copy),
                         ins=[ps_cb], outs=[cb_sb2[tb]])
                for ic in range(8):
                    g_sl, u_sl = gslab[ic % 2], uslab[ic % 2]
                    p.op("gpsimd", lambda e, ex=ex, ic=ic, g_sl=g_sl: e.dma_start(
                        g_sl.rearrange("p (dc i) -> p dc i", dc=NC),
                        d_wg1[ex, :, :].rearrange("(dc p) i -> p dc i", p=128)[
                            :, :, ic * 128:(ic + 1) * 128]), outs=[g_sl], dma=True)
                    p.op("gpsimd", lambda e, ex=ex, ic=ic, u_sl=u_sl: e.dma_start(
                        u_sl.rearrange("p (dc i) -> p dc i", dc=NC),
                        d_wu1[ex, :, :].rearrange("(dc p) i -> p dc i", p=128)[
                            :, :, ic * 128:(ic + 1) * 128]), outs=[u_sl], dma=True)
                    for tb in range(2):
                        ts = slice(tb * 512, (tb + 1) * 512)
                        pg, pu = ps_g[tb], ps_u[tb]
                        def mmg(e, g_sl=g_sl, ts=ts, pg=pg):
                            last = None
                            for dc in range(NC):
                                last = e.matmul(pg.ap, g_sl[:, dc * 128:(dc + 1) * 128],
                                                xnc[dc][:, ts], start=(dc == 0),
                                                stop=(dc == NC - 1))
                            return last
                        p.op("tensor", mmg, ins=[g_sl] + xnc, outs=[pg])
                        def mmu(e, u_sl=u_sl, ts=ts, pu=pu):
                            last = None
                            for dc in range(NC):
                                last = e.matmul(pu.ap, u_sl[:, dc * 128:(dc + 1) * 128],
                                                xnc[dc][:, ts], start=(dc == 0),
                                                stop=(dc == NC - 1))
                            return last
                        p.op("tensor", mmu, ins=[u_sl] + xnc, outs=[pu])
                        p.op("scalar", lambda e, pg=pg: e.activation(gsl_.ap, pg.ap, AF.Silu),
                             ins=[pg], outs=[gsl_])
                        p.op("vector", lambda e, tb=tb: e.tensor_mul(gsc.ap, gsl_.ap,
                                                                     cb_sb2[tb].ap),
                             ins=[gsl_, cb_sb2[tb]], outs=[gsc])
                        p.op("vector", lambda e, ic=ic, pu=pu, tb=tb: e.tensor_mul(
                            abuf2[tb][:, ic * 512:(ic + 1) * 512], gsc.ap, pu.ap),
                            ins=[gsc, pu], outs=[abuf2[tb]])
                for hc in range(NC):
                    d_sl = dslab[hc % 2]
                    p.op("gpsimd", lambda e, ex=ex, hc=hc, d_sl=d_sl: e.dma_start(
                        d_sl.rearrange("p (ic h) -> p ic h", ic=8),
                        d_wd1[ex, :, :].rearrange("(ic p) h -> p ic h", p=128)[
                            :, :, hc * 128:(hc + 1) * 128]), outs=[d_sl], dma=True)
                    for tb in range(2):
                        py = ps_y[tb]
                        def mmd(e, d_sl=d_sl, py=py, tb=tb):
                            last = None
                            for ic in range(8):
                                last = e.matmul(py.ap, d_sl[:, ic * 128:(ic + 1) * 128],
                                                abuf2[tb][:, ic * 512:(ic + 1) * 512],
                                                start=(ic == 0), stop=(ic == 7))
                            return last
                        p.op("tensor", mmd, ins=[d_sl, abuf2[tb]], outs=[py])
                        ya = slice(hc * 512, (hc + 1) * 512)
                        if ex == 0:
                            p.op("scalar", lambda e, ya=ya, py=py, tb=tb: e.activation(
                                y_acc2[tb][:, ya], py.ap, AF.Copy),
                                ins=[py], outs=[y_acc2[tb]])
                        else:
                            p.op("vector", lambda e, ya=ya, py=py, tb=tb: e.tensor_add(
                                y_acc2[tb][:, ya], y_acc2[tb][:, ya], py.ap),
                                ins=[py, y_acc2[tb]], outs=[y_acc2[tb]])
            for tb in range(2):
                ts = slice(tb * 512, (tb + 1) * 512)
                for hc in range(NC):
                    ya = slice(hc * 512, (hc + 1) * 512)
                    hs = slice(hc * 128, (hc + 1) * 128)
                    xb = xcb[hc % 2]
                    p.op("scalar", lambda e, ya=ya, tb=tb: e.activation(
                        gfin.ap, y_acc2[tb][:, ya], AF.Silu),
                        ins=[y_acc2[tb]], outs=[gfin])
                    p.op("sync", lambda e, hs=hs, ts=ts, xb=xb: e.dma_start(
                        xb.ap, d_x1s[hs, ts]), ins=[bx1s], outs=[xb], dma=True)
                    p.op("vector", lambda e, hc=hc, tb=tb: e.tensor_mul(
                        tmpb.ap, gfin.ap,
                        s_res[:, hc * TPC + tb * 512: hc * TPC + (tb + 1) * 512]),
                        ins=[gfin, s_res], outs=[tmpb])
                    p.op("vector", lambda e, xb=xb: e.tensor_add(xb.ap, xb.ap, tmpb.ap),
                         ins=[tmpb, xb], outs=[xb])
                    out_toks.append(p.op("gpsimd", lambda e, hs=hs, ts=ts, xb=xb: e.dma_start(
                        d_xo[hs, ts], xb.ap), ins=[xb], dma=True))
    _s6()

    p.finish(out_toks)
    p.build()
    return nc


def run_fused(inputs, h):
    nc = build_fused()
    in_maps = []
    for c in range(NCORE):
        ts = slice(c * TPC, (c + 1) * TPC)
        osl = slice(c * OPC, (c + 1) * OPC)
        dsl = slice(c * 256, (c + 1) * 256)
        in_maps.append(dict(
            xT_c=np.ascontiguousarray(h["xT"][:, ts]),
            wqkvTp=h["wqkvTp"],
            cos_q=np.ascontiguousarray(h["cos_q"][:, ts]),
            sinx_q=np.ascontiguousarray(h["sinx_q"][:, ts]),
            cos_k=np.ascontiguousarray(h["cos_k"][:, ts]),
            sinx_k=np.ascontiguousarray(h["sinx_k"][:, ts]),
            ones_col_f=h["ones_col_f"], ones_row_f=h["ones_row_f"],
            ones_col_bf=h["ones_col_bf"], ones_row_bf=h["ones_row_bf"],
            ident_bf=h["ident_bf"], ident_f=h["ident_f"],
            woT=h["woT"], masks_c=h["masks"][c],
            gidx_sl=np.ascontiguousarray(h["gidxT"][:, osl]),
            uidx_sl=np.ascontiguousarray(h["uidxT"][:, osl]),
            didx_sl=np.ascontiguousarray(h["didxT"][:, dsl]),
            lutrep=h["lutrep"],
            wq1=h["wq1"], wk1=h["wk1"], wv1=h["wv1"], wo1=h["wo1"],
            qscale=h["qscale"], kscale=h["kscale"],
            gwT=h["gwT"], wg1=h["wg1"], wu1=h["wu1"], wd1=h["wd1"],
        ))
    res = run_bass_kernel_spmd(nc, in_maps, core_ids=list(range(NCORE)))
    return res.results



def kernel(**inputs):
    h = host_prep(inputs)
    lg = run_fused(inputs, h)
    xo = np.concatenate([np.asarray(r["xoT_c"], np.float32) for r in lg], axis=1)
    return np.ascontiguousarray(xo.T)[None]


def kernel_multilaunch(**inputs):
    h = host_prep(inputs)
    la = run_l0a(h)
    lb = run_l0b(h)
    lc = run_l0c(h, la)
    ld = run_l0bd(h, la, lc, lb)
    le = run_l1a(h, ld)
    lf = run_l1b(h, le)
    lg = run_l1c(h, ld, le, lf)
    xo = np.concatenate([np.asarray(r["xoT_c"], np.float32) for r in lg], axis=1)
    return np.ascontiguousarray(xo.T)[None]


# version 09 (final)
# speedup vs baseline: 1.6804x; 1.0068x over previous
import math
from contextlib import ExitStack

import numpy as np
import ml_dtypes
import concourse.bass as bass
import concourse.mybir as mybir
from concourse.bass_utils import run_bass_kernel_spmd

F32 = mybir.dt.float32
BF16 = mybir.dt.bfloat16
I32 = mybir.dt.int32
AF = mybir.ActivationFunctionType
ALU = mybir.AluOpType
AX = mybir.AxisListType
BF = ml_dtypes.bfloat16

NCORE = 8
D = 2048
SEQ = 8192
TPC = SEQ // NCORE
NH = 16
DH = 128
NC = D // 128
INTER = 5504
OPC = INTER // NCORE
NEXP = 16
IEXP = 1024
NEG = -30000.0


class Buf:
    __slots__ = ("h", "w", "r")

    def __init__(self, h):
        self.h = h
        self.w = None
        self.r = []

    def __getitem__(self, k):
        return self.h[k]

    def rearrange(self, *a, **kw):
        return self.h.rearrange(*a, **kw)

    @property
    def ap(self):
        return self.h[:]


class Prog:
    ENGINES = ("sync", "tensor", "vector", "scalar", "gpsimd")

    NPOOL = 6

    def __init__(self, nc):
        self.nc = nc
        self.es = ExitStack()
        self.sem = {e: self.es.enter_context(nc.semaphore(f"S{e}"))
                    for e in self.ENGINES}
        self.cnt = {e: 0 for e in self.ENGINES}
        self.q = {e: [] for e in self.ENGINES}
        self.hw = {e: {} for e in self.ENGINES}
        self.dpool = {e: [self.es.enter_context(nc.semaphore(f"D{e}{i}"))
                          for i in range(self.NPOOL)]
                      for e in ("sync", "gpsimd", "scalar", "vector")}
        self.dcnt = {e: [0] * self.NPOOL for e in self.dpool}
        self.di = {e: 0 for e in self.dpool}

    def sbuf(self, name, shape, dtype):
        return Buf(self.es.enter_context(self.nc.sbuf_tensor(name, shape, dtype)))

    def psum(self, name, shape):
        return Buf(self.es.enter_context(self.nc.psum_tensor(name, shape, F32)))

    def _wait(self, e, tok):
        sem, val = tok
        key = sem.name
        if self.hw[e].get(key, -1) >= val:
            return
        self.hw[e][key] = val
        self.q[e].append(("w", sem, val))

    def op(self, e, fn, ins=(), outs=(), dma=False, tag=""):
        deps = []
        for b in ins:
            if b.w is not None:
                deps.append(b.w)
        for b in outs:
            if b.w is not None:
                deps.append(b.w)
            deps.extend(b.r)
        for t in deps:
            self._wait(e, t)
        if dma:
            slot = self.di[e] % self.NPOOL
            self.di[e] += 1
            psem = self.dpool[e][slot]
            if self.dcnt[e][slot]:
                self._wait(e, (psem, self.dcnt[e][slot]))
            n_dma = getattr(fn, "n_dma", 1)
            self.dcnt[e][slot] += 16 * n_dma
            tok = (psem, self.dcnt[e][slot])
            self.q[e].append(("om" if n_dma > 1 else "o", fn, (psem, 16), tag))
            for b in ins:
                b.r.append(tok)
            for b in outs:
                b.w = tok
                b.r = []
            return tok
        amt = 1
        self.cnt[e] += amt
        tok = (self.sem[e], self.cnt[e])
        self.q[e].append(("o", fn, (self.sem[e], amt), tag))
        for b in ins:
            b.r.append(tok)
        for b in outs:
            b.w = tok
            b.r = []
        return tok

    def barrier_token(self, e):
        return (self.sem[e], self.cnt[e])

    def barrier(self):
        toks = [(self.sem[e], self.cnt[e]) for e in self.ENGINES
                if self.cnt[e] > 0]
        for e in self.dpool:
            for s, c in zip(self.dpool[e], self.dcnt[e]):
                if c:
                    toks.append((s, c))
        for s, c in getattr(self, "ccs_used", []):
            toks.append((s, c))
        for e in self.ENGINES:
            for t in toks:
                self._wait(e, t)

    from contextlib import contextmanager as _cm

    @_cm
    def scope(self):
        es = ExitStack()
        outer = self.es
        self.es = es
        try:
            yield
        finally:
            self.barrier()
            es.close()
            self.es = outer

    def cc(self, fn, ins=(), outs=()):
        if not hasattr(self, "ccsems"):
            self.ccsems = []
            self.ccs_used = []
        sem = self.nc.semaphore(f"CC{len(self.ccsems)}").__enter__()
        self.ccsems.append(sem)
        deps = []
        for b in ins:
            if b.w is not None:
                deps.append(b.w)
        for b in outs:
            if b.w is not None:
                deps.append(b.w)
            deps.extend(b.r)
        for t in deps:
            self._wait("gpsimd", t)
        tok = (sem, 1)
        self.ccs_used.append(tok)
        self.q["gpsimd"].append(("o", fn, (sem, 1), "cc"))
        for b in ins:
            b.r.append(tok)
        for b in outs:
            b.w = tok
            b.r = []
        return tok

    def finish(self, out_tokens):
        for t in out_tokens:
            self._wait("sync", t)

    def build(self):
        with self.nc.Block() as block:
            for e in self.ENGINES:
                items = self.q[e]

                def run(eng, items=items):
                    for it in items:
                        if it[0] == "w":
                            eng.wait_ge(it[1], it[2])
                        elif it[0] == "om":
                            insts = it[1](eng)
                            sem, amt = it[2]
                            for inst in insts:
                                inst.then_inc(sem, amt)
                        else:
                            last = it[1](eng)
                            if isinstance(last, list):
                                last = last[-1]
                            sem, amt = it[2]
                            last.then_inc(sem, amt)
                getattr(block, e)(run)
        self.es.close()


def _deint_perm():
    p = np.empty(128, np.int64)
    p[:64] = np.arange(64) * 2
    p[64:] = np.arange(64) * 2 + 1
    return p


def host_prep(inputs):
    h = {}
    x = np.asarray(inputs["L0_attn_x"])
    h["xT"] = np.ascontiguousarray(x[0].T)

    perm = _deint_perm()
    wqkvT = np.ascontiguousarray(np.asarray(inputs["L0_attn_W_qkv"]).T)
    wq = wqkvT[:, :2048].reshape(2048, NH, DH)[:, :, perm].reshape(2048, 2048)
    wk = wqkvT[:, 2048:4096].reshape(2048, NH, DH)[:, :, perm].reshape(2048, 2048)
    wv = wqkvT[:, 4096:]
    h["wqkvTp"] = np.ascontiguousarray(np.concatenate([wq, wk, wv], axis=1))
    h["woT"] = np.ascontiguousarray(np.asarray(inputs["L0_attn_W_o"]).T)

    h["lutrep"] = np.ascontiguousarray(
        np.broadcast_to(np.asarray(inputs["L0_ffn_lut"]), (128, 16))).astype(np.float32)
    h["gidxT"] = np.ascontiguousarray(np.asarray(inputs["L0_ffn_gate_idx"]).T)
    h["uidxT"] = np.ascontiguousarray(np.asarray(inputs["L0_ffn_up_idx"]).T)
    h["didxT"] = np.ascontiguousarray(np.asarray(inputs["L0_ffn_down_idx"]).T)

    h["wq1"] = np.ascontiguousarray(np.asarray(inputs["L1_attn_wq"]).reshape(2048, 2048))
    h["wk1"] = np.ascontiguousarray(np.asarray(inputs["L1_attn_wk"]).reshape(2048, 2048))
    h["wv1"] = np.ascontiguousarray(np.asarray(inputs["L1_attn_wv"]).reshape(2048, 2048))
    h["wo1"] = np.ascontiguousarray(np.asarray(inputs["L1_attn_wo"]).reshape(2048, 2048))
    h["qscale"] = np.asarray(inputs["L1_attn_q_scale"]).reshape(128, 1).astype(np.float32)
    h["kscale"] = np.asarray(inputs["L1_attn_k_scale"]).reshape(128, 1).astype(np.float32)
    h["gwT"] = np.ascontiguousarray(np.asarray(inputs["L1_ffn_gate_w"]).T)
    h["wg1"] = np.asarray(inputs["L1_ffn_wg"])
    h["wu1"] = np.asarray(inputs["L1_ffn_wu"])
    h["wd1"] = np.asarray(inputs["L1_ffn_wd"])

    h["ones_row_bf"] = np.ones((1, 128), BF)
    h["ones_col_bf"] = np.ones((128, 1), BF)
    h["ones_col_f"] = np.ones((128, 1), np.float32)
    h["ones_row_f"] = np.ones((1, 128), np.float32)
    h["ident_f"] = np.eye(128, dtype=np.float32)
    h["ident_bf"] = np.eye(128, dtype=BF)

    inv_freq = (10000.0 ** (-np.arange(0, DH, 2) / DH)).astype(np.float64)
    pos = np.arange(SEQ, dtype=np.float64)
    ang = pos[:, None] * inv_freq[None, :]
    cos = np.cos(ang).T
    sin = np.sin(ang).T
    cos128 = np.concatenate([cos, cos], 0)
    sinx = np.concatenate([-sin, sin], 0)
    sc = 1.0 / math.sqrt(DH)
    h["cos_k"] = cos128.astype(np.float32)
    h["sinx_k"] = sinx.astype(np.float32)
    h["cos_q"] = (cos128 * sc).astype(np.float32)
    h["sinx_q"] = (sinx * sc).astype(np.float32)

    f = np.arange(512)
    p = np.arange(128)
    masks = np.zeros((NCORE, 2, 64, 128, 512), BF)
    for c in range(NCORE):
        for qb in range(2):
            QB = c * TPC + qb * 512
            for kt in range(64):
                Dd = 128 * kt - QB
                if Dd + 127 <= 0:
                    continue
                m = np.where(f[None, :] >= (Dd + p[:, None]), 0.0, NEG)
                masks[c, qb, kt] = m.astype(BF)
    h["masks"] = masks
    return h


def build_l0a():
    nc = bass.Bass("TRN2", target_bir_lowering=False)
    d_x = nc.dram_tensor("xT_c", [D, TPC], F32, kind="ExternalInput")
    d_w = nc.dram_tensor("wqkvTp", [D, 3 * D], F32, kind="ExternalInput")
    d_cq = nc.dram_tensor("cos_q", [128, TPC], F32, kind="ExternalInput")
    d_sq = nc.dram_tensor("sinx_q", [128, TPC], F32, kind="ExternalInput")
    d_ck = nc.dram_tensor("cos_k", [128, TPC], F32, kind="ExternalInput")
    d_sk = nc.dram_tensor("sinx_k", [128, TPC], F32, kind="ExternalInput")
    d_onecf = nc.dram_tensor("ones_col_f", [128, 1], F32, kind="ExternalInput")
    d_onerf = nc.dram_tensor("ones_row_f", [1, 128], F32, kind="ExternalInput")
    d_onecb = nc.dram_tensor("ones_col_bf", [128, 1], BF16, kind="ExternalInput")

    d_xn = nc.dram_tensor("xn", [D, TPC], BF16, kind="ExternalOutput")
    d_q = nc.dram_tensor("qf", [D, TPC], BF16, kind="ExternalOutput")
    d_k = nc.dram_tensor("kf", [D, TPC], BF16, kind="ExternalOutput")
    d_v = nc.dram_tensor("vt", [TPC, D], BF16, kind="ExternalOutput")
    d_k2 = nc.dram_tensor("k2m", [1, NH], F32, kind="ExternalOutput")

    p = Prog(nc)
    out_toks = []

    xc = [p.sbuf(f"x{c}", [128, TPC], F32) for c in range(NC)]
    xnc = [p.sbuf(f"xn{c}", [128, TPC], BF16) for c in range(NC)]
    sqc = p.sbuf("sqc", [128, TPC], F32)
    cq = p.sbuf("cq", [128, TPC], F32)
    sq_t = p.sbuf("sq_t", [128, TPC], F32)
    ck = p.sbuf("ck", [128, TPC], F32)
    sk = p.sbuf("sk", [128, TPC], F32)
    onecf = p.sbuf("onecf", [128, 1], F32)
    onerf = p.sbuf("onerf", [1, 128], F32)
    onecb = p.sbuf("onecb", [128, 1], BF16)
    rowA = p.sbuf("rowA", [1, TPC], F32)
    rowB = p.sbuf("rowB", [1, TPC], F32)
    rowC = p.sbuf("rowC", [1, TPC], F32)
    tmpc = p.sbuf("tmpc", [128, TPC], F32)
    wslab = [p.sbuf(f"wslab{i}", [128, NC * 128], BF16) for i in range(2)]
    vslab = [p.sbuf(f"vslab{i}", [128, NC * 512], BF16) for i in range(2)]
    qkh = [p.sbuf(f"qkh{i}", [128, TPC], BF16) for i in range(2)]
    ropetmp = p.sbuf("ropetmp", [128, 512], F32)
    ropecos = p.sbuf("ropecos", [128, 512], F32)
    vtt = [p.sbuf(f"vtt{i}", [128, 512], BF16) for i in range(2)]
    k2sq = p.sbuf("k2sq", [128, TPC], BF16)
    k2row = p.sbuf("k2row", [1, NH], F32)

    T1 = p.psum("T1", [128, TPC])
    T2 = p.psum("T2", [128, TPC])
    T3 = p.psum("T3", [128, 512])
    T4 = p.psum("T4", [128, 512])
    ps_sum, ps_sq = T1, T2
    ps_b, ps_b2 = T1, T2
    ps_qk = [T3, T4]
    ps_v = [T3, T4]
    ps_k2 = T2

    for c in range(NC):
        p.op("sync", lambda e, c=c: e.dma_start(xc[c].ap, d_x[c * 128:(c + 1) * 128, :]),
             outs=[xc[c]], dma=True)
    p.op("sync", lambda e: e.dma_start(cq.ap, d_cq[:]), outs=[cq], dma=True)
    p.op("sync", lambda e: e.dma_start(sq_t.ap, d_sq[:]), outs=[sq_t], dma=True)
    p.op("sync", lambda e: e.dma_start(ck.ap, d_ck[:]), outs=[ck], dma=True)
    p.op("sync", lambda e: e.dma_start(sk.ap, d_sk[:]), outs=[sk], dma=True)
    p.op("sync", lambda e: e.dma_start(onecf.ap, d_onecf[:]), outs=[onecf], dma=True)
    p.op("sync", lambda e: e.dma_start(onerf.ap, d_onerf[:]), outs=[onerf], dma=True)
    p.op("sync", lambda e: e.dma_start(onecb.ap, d_onecb[:]), outs=[onecb], dma=True)

    for c in range(NC):
        p.op("scalar", lambda e, c=c: e.activation(sqc.ap, xc[c].ap, AF.Square),
             ins=[xc[c]], outs=[sqc])
        def mm_sum(e, c=c):
            last = None
            for tb in range(2):
                last = e.matmul(ps_sum[0:1, tb * 512:(tb + 1) * 512], onecf.ap,
                                xc[c][:, tb * 512:(tb + 1) * 512],
                                start=(c == 0), stop=(c == NC - 1))
            return last
        p.op("tensor", mm_sum, ins=[onecf, xc[c]], outs=[ps_sum])
        def mm_sq(e, c=c):
            last = None
            for tb in range(2):
                last = e.matmul(ps_sq[0:1, tb * 512:(tb + 1) * 512], onecf.ap,
                                sqc[:, tb * 512:(tb + 1) * 512],
                                start=(c == 0), stop=(c == NC - 1))
            return last
        p.op("tensor", mm_sq, ins=[onecf, sqc], outs=[ps_sq])

    p.op("scalar", lambda e: e.activation(rowA.ap, ps_sum[0:1, :], AF.Copy, scale=1.0 / D),
         ins=[ps_sum], outs=[rowA])
    p.op("scalar", lambda e: e.activation(rowB.ap, ps_sq[0:1, :], AF.Copy, scale=1.0 / D),
         ins=[ps_sq], outs=[rowB])
    p.op("scalar", lambda e: e.activation(rowC.ap, rowA.ap, AF.Square),
         ins=[rowA], outs=[rowC])
    p.op("vector", lambda e: e.tensor_sub(rowB.ap, rowB.ap, rowC.ap),
         ins=[rowB, rowC], outs=[rowB])
    p.op("vector", lambda e: e.tensor_scalar_add(rowB.ap, rowB.ap, 1e-5),
         ins=[rowB], outs=[rowB])
    p.op("scalar", lambda e: e.activation(rowC.ap, rowB.ap, AF.Sqrt),
         ins=[rowB], outs=[rowC])
    p.op("vector", lambda e: e.reciprocal(rowB.ap, rowC.ap),
         ins=[rowC], outs=[rowB])
    p.op("vector", lambda e: e.tensor_mul(rowC.ap, rowA.ap, rowB.ap),
         ins=[rowA, rowB], outs=[rowC])
    p.op("scalar", lambda e: e.activation(rowC.ap, rowC.ap, AF.Copy, scale=-1.0),
         ins=[rowC], outs=[rowC])
    def bc_r(e):
        last = None
        for tb in range(2):
            last = e.matmul(ps_b[:, tb * 512:(tb + 1) * 512], onerf.ap,
                            rowB[0:1, tb * 512:(tb + 1) * 512], start=True, stop=True)
        return last
    p.op("tensor", bc_r, ins=[onerf, rowB], outs=[ps_b])
    def bc_m(e):
        last = None
        for tb in range(2):
            last = e.matmul(ps_b2[:, tb * 512:(tb + 1) * 512], onerf.ap,
                            rowC[0:1, tb * 512:(tb + 1) * 512], start=True, stop=True)
        return last
    p.op("tensor", bc_m, ins=[onerf, rowC], outs=[ps_b2])
    for c in range(NC):
        p.op("vector", lambda e, c=c: e.tensor_mul(tmpc.ap, xc[c].ap, ps_b.ap),
             ins=[xc[c], ps_b], outs=[tmpc])
        p.op("vector", lambda e, c=c: e.tensor_add(xnc[c].ap, tmpc.ap, ps_b2.ap),
             ins=[tmpc, ps_b2], outs=[xnc[c]])
        out_toks.append(p.op("gpsimd", lambda e, c=c: e.dma_start(
            d_xn[c * 128:(c + 1) * 128, :], xnc[c].ap), ins=[xnc[c]], dma=True))

    def emit_qk(kind, off_e, cos_t, sin_t, d_out, do_k2):
        for h in range(NH):
            sl = wslab[h % 2]
            hb = qkh[h % 2]
            p.op("gpsimd", lambda e, h=h, sl=sl: e.dma_start(
                sl.rearrange("p (dc e) -> p dc e", dc=NC),
                d_w.rearrange("(dc p) e -> p dc e", p=128)[:, :, off_e + h * 128: off_e + (h + 1) * 128]),
                outs=[sl], dma=True)
            for tb in range(2):
                ps = ps_qk[tb]
                def mm(e, sl=sl, tb=tb, ps=ps):
                    last = None
                    for dc in range(NC):
                        last = e.matmul(ps.ap, sl[:, dc * 128:(dc + 1) * 128],
                                        xnc[dc][:, tb * 512:(tb + 1) * 512],
                                        start=(dc == 0), stop=(dc == NC - 1))
                    return last
                p.op("tensor", mm, ins=[sl] + xnc, outs=[ps])
                ts = slice(tb * 512, (tb + 1) * 512)
                p.op("vector", lambda e, ps=ps, ts=ts: e.tensor_mul(
                    ropetmp[0:64, :], ps[64:128, :], sin_t[0:64, ts]),
                    ins=[ps, sin_t], outs=[ropetmp])
                p.op("vector", lambda e, ps=ps, ts=ts: e.tensor_mul(
                    ropetmp[64:128, :], ps[0:64, :], sin_t[64:128, ts]),
                    ins=[ps, sin_t], outs=[ropetmp])
                p.op("vector", lambda e, ps=ps, ts=ts: e.tensor_mul(
                    ropecos.ap, ps.ap, cos_t[:, ts]),
                    ins=[ps, cos_t], outs=[ropecos])
                p.op("vector", lambda e, hb=hb, ts=ts: e.tensor_add(
                    hb[:, ts], ropecos.ap, ropetmp.ap),
                    ins=[ropecos, ropetmp], outs=[hb])
            out_toks.append(p.op("gpsimd", lambda e, h=h, hb=hb: e.dma_start(
                d_out[h * 128:(h + 1) * 128, :], hb.ap), ins=[hb], dma=True))
            if do_k2:
                p.op("scalar", lambda e, hb=hb: e.activation(k2sq.ap, hb.ap, AF.Square),
                     ins=[hb], outs=[k2sq])
                def mmk2(e):
                    last = None
                    for tb in range(2):
                        last = e.matmul(ps_k2[0:1, tb * 512:(tb + 1) * 512], onecb.ap,
                                        k2sq[:, tb * 512:(tb + 1) * 512],
                                        start=True, stop=True)
                    return last
                p.op("tensor", mmk2, ins=[onecb, k2sq], outs=[ps_k2])
                p.op("vector", lambda e, h=h: e.reduce_max(
                    k2row[0:1, h:h + 1], ps_k2[0:1, :], AX.X), ins=[ps_k2], outs=[k2row])

    emit_qk("q", 0, cq, sq_t, d_q, False)
    emit_qk("k", D, ck, sk, d_k, True)
    out_toks.append(p.op("gpsimd", lambda e: e.dma_start(d_k2[:], k2row.ap),
                         ins=[k2row], dma=True))

    for eb in range(4):
        sl = vslab[eb % 2]
        p.op("gpsimd", lambda e, eb=eb, sl=sl: e.dma_start(
            sl.rearrange("p (dc e) -> p dc e", dc=NC),
            d_w.rearrange("(dc p) e -> p dc e", p=128)[
                :, :, 2 * D + eb * 512: 2 * D + (eb + 1) * 512]),
            outs=[sl], dma=True)
        for tt in range(TPC // 128):
            ps = ps_v[tt % 2]
            def mmv(e, tt=tt, sl=sl, ps=ps):
                last = None
                for dc in range(NC):
                    last = e.matmul(ps.ap, xnc[dc][:, tt * 128:(tt + 1) * 128],
                                    sl[:, dc * 512:(dc + 1) * 512],
                                    start=(dc == 0), stop=(dc == NC - 1))
                return last
            p.op("tensor", mmv, ins=[sl] + xnc, outs=[ps])
            vt_b = vtt[tt % 2]
            p.op("scalar", lambda e, ps=ps, vt_b=vt_b: e.activation(
                vt_b.ap, ps.ap, AF.Copy), ins=[ps], outs=[vt_b])
            out_toks.append(p.op("gpsimd", lambda e, tt=tt, eb=eb, vt_b=vt_b: e.dma_start(
                d_v[tt * 128:(tt + 1) * 128, eb * 512:(eb + 1) * 512], vt_b.ap),
                ins=[vt_b], dma=True))

    p.finish(out_toks)
    p.build()
    return nc


def run_l0a(h):
    nc = build_l0a()
    in_maps = []
    for c in range(NCORE):
        ts = slice(c * TPC, (c + 1) * TPC)
        in_maps.append(dict(
            xT_c=np.ascontiguousarray(h["xT"][:, ts]),
            wqkvTp=h["wqkvTp"],
            cos_q=np.ascontiguousarray(h["cos_q"][:, ts]),
            sinx_q=np.ascontiguousarray(h["sinx_q"][:, ts]),
            cos_k=np.ascontiguousarray(h["cos_k"][:, ts]),
            sinx_k=np.ascontiguousarray(h["sinx_k"][:, ts]),
            ones_col_f=h["ones_col_f"],
            ones_row_f=h["ones_row_f"],
            ones_col_bf=h["ones_col_bf"],
        ))
    res = run_bass_kernel_spmd(nc, in_maps, core_ids=list(range(NCORE)))
    return res.results




def build_l0c():
    nc = bass.Bass("TRN2", target_bir_lowering=False)
    d_q = nc.dram_tensor("qf", [D, TPC], BF16, kind="ExternalInput")
    d_kf = nc.dram_tensor("kfull", [D, SEQ], BF16, kind="ExternalInput")
    d_vf = nc.dram_tensor("vfull", [SEQ, D], BF16, kind="ExternalInput")
    d_k2 = nc.dram_tensor("k2all", [NH, NCORE], F32, kind="ExternalInput")
    d_wo = nc.dram_tensor("woT", [D, D], F32, kind="ExternalInput")
    d_msk = nc.dram_tensor("masks_c", [2, 64, 128, 512], BF16, kind="ExternalInput")
    d_onerb = nc.dram_tensor("ones_row_bf", [1, 128], BF16, kind="ExternalInput")
    d_onecb = nc.dram_tensor("ones_col_bf", [128, 1], BF16, kind="ExternalInput")
    d_idb = nc.dram_tensor("ident_bf", [128, 128], BF16, kind="ExternalInput")
    d_scr = nc.dram_tensor("scr_k2", [1, NH], F32, kind="Internal")
    d_s = nc.dram_tensor("s_out", [D, TPC], BF16, kind="ExternalOutput")

    p = Prog(nc)
    out_toks = []

    qsb = p.sbuf("qsb", [128, NH * TPC], BF16)
    ysb = p.sbuf("ysb", [128, NH * TPC], BF16)
    mskslab = p.sbuf("mskslab", [128, 64 * 512], BF16)
    ksb = p.sbuf("ksb", [128, SEQ], BF16)
    vsb = p.sbuf("vsb", [128, SEQ], BF16)
    onerb = p.sbuf("onerb", [1, 128], BF16)
    onecb = p.sbuf("onecb", [128, 1], BF16)
    idb = p.sbuf("idb", [128, 128], BF16)
    k2sb = p.sbuf("k2sb", [NH, NCORE], F32)
    k2col = p.sbuf("k2col", [NH, 1], F32)
    k2row = p.sbuf("k2row", [1, NH], F32)
    sqh = p.sbuf("sqh", [128, TPC], BF16)
    q2row = p.sbuf("q2row", [1, TPC], F32)
    brow = p.sbuf("brow", [1, TPC], F32)
    negb = p.sbuf("negb", [1, TPC], BF16)
    p_sb = [p.sbuf(f"p_sb{i}", [128, 512], BF16) for i in range(2)]
    lrow = p.sbuf("lrow", [1, 512], F32)
    linvb = p.sbuf("linvb", [1, 512], BF16)
    lb_sb = p.sbuf("lb_sb", [128, 512], BF16)
    woslab = [p.sbuf(f"woslab{i}", [128, NC * 128], BF16) for i in range(2)]
    s_sb = [p.sbuf(f"s_sb{i}", [128, TPC], BF16) for i in range(2)]

    ps_s = [p.psum(f"ps_s{i}", [128, 512]) for i in range(2)]
    ps_o = p.psum("ps_o", [128, 512])
    ps_l = p.psum("ps_l", [1, 512])
    ps_q2 = p.psum("ps_q2", [1, TPC])

    p.op("gpsimd", lambda e: e.dma_start(
        qsb.rearrange("p (h t) -> p h t", h=NH),
        d_q.rearrange("(h p) t -> p h t", p=128)), outs=[qsb], dma=True)
    for b, d in ((onerb, d_onerb), (onecb, d_onecb), (idb, d_idb), (k2sb, d_k2)):
        p.op("sync", lambda e, b=b, d=d: e.dma_start(b.ap, d[:]), outs=[b], dma=True)
    p.op("vector", lambda e: e.reduce_max(k2col.ap, k2sb.ap, AX.X),
         ins=[k2sb], outs=[k2col])
    dscr = Buf(d_scr)
    p.op("gpsimd", lambda e: e.dma_start(d_scr.rearrange("a b -> b a"), k2col.ap),
         ins=[k2col], outs=[dscr], dma=True)
    p.op("gpsimd", lambda e: e.dma_start(k2row.ap, d_scr[:]),
         ins=[dscr], outs=[k2row], dma=True)

    for qb in range(2):
        p.op("sync", lambda e, qb=qb: e.dma_start(
            mskslab.rearrange("p (kt f) -> p kt f", kt=64),
            d_msk[qb, :, :, :].rearrange("kt p f -> p kt f")),
            outs=[mskslab], dma=True)
        for h in range(NH):
            p.op("scalar", lambda e, h=h: e.activation(
                sqh.ap, qsb[:, h * TPC:(h + 1) * TPC], AF.Square),
                ins=[qsb], outs=[sqh])
            def mmq2(e):
                last = None
                for tb in range(2):
                    last = e.matmul(ps_q2[0:1, tb * 512:(tb + 1) * 512], onecb.ap,
                                    sqh[:, tb * 512:(tb + 1) * 512], start=True, stop=True)
                return last
            p.op("tensor", mmq2, ins=[onecb, sqh], outs=[ps_q2])
            p.op("vector", lambda e, h=h: e.tensor_scalar(
                q2row.ap, ps_q2[0:1, :], k2row[0:1, h:h + 1], 1.0404,
                ALU.mult, ALU.mult), ins=[ps_q2, k2row], outs=[q2row])
            p.op("scalar", lambda e: e.activation(brow.ap, q2row.ap, AF.Sqrt),
                 ins=[q2row], outs=[brow])
            p.op("scalar", lambda e: e.activation(negb.ap, brow.ap, AF.Copy, scale=-1.0),
                 ins=[brow], outs=[negb])
            p.op("sync", lambda e, h=h: e.dma_start(
                ksb.ap, d_kf[h * 128:(h + 1) * 128, :]), outs=[ksb], dma=True)
            p.op("sync", lambda e, h=h: e.dma_start(
                vsb.rearrange("p (kt d) -> p kt d", kt=64),
                d_vf.rearrange("(kt p) d -> p kt d", p=128)[:, :, h * 128:(h + 1) * 128]),
                outs=[vsb], dma=True)
            for kt in range(64):
                ps = ps_s[kt % 2]
                pb = p_sb[kt % 2]
                def mms(e, h=h, qb=qb, kt=kt, ps=ps):
                    e.matmul(ps.ap, onerb.ap, negb[0:1, qb * 512:(qb + 1) * 512],
                             start=True, stop=False)
                    e.matmul(ps.ap, idb.ap, mskslab[:, kt * 512:(kt + 1) * 512],
                             start=False, stop=False)
                    return e.matmul(ps.ap, ksb[:, kt * 128:(kt + 1) * 128],
                                    qsb[:, h * TPC + qb * 512: h * TPC + (qb + 1) * 512],
                                    start=False, stop=True)
                p.op("tensor", mms, ins=[onerb, negb, idb, mskslab, ksb, qsb], outs=[ps])
                p.op("scalar", lambda e, ps=ps, pb=pb: e.activation(pb.ap, ps.ap, AF.Exp),
                     ins=[ps], outs=[pb])
                p.op("tensor", lambda e, kt=kt, pb=pb: e.matmul(
                    ps_o.ap, vsb[:, kt * 128:(kt + 1) * 128], pb.ap,
                    start=(kt == 0), stop=(kt == 63)),
                    ins=[vsb, pb], outs=[ps_o] if kt == 63 else [ps_o])
                p.op("tensor", lambda e, kt=kt, pb=pb: e.matmul(
                    ps_l[0:1, :], onecb.ap, pb.ap,
                    start=(kt == 0), stop=(kt == 63)),
                    ins=[onecb, pb], outs=[ps_l])
            p.op("scalar", lambda e: e.activation(lrow.ap, ps_l[0:1, :], AF.Copy),
                 ins=[ps_l], outs=[lrow])
            p.op("vector", lambda e: e.reciprocal(lrow.ap, lrow.ap),
                 ins=[lrow], outs=[lrow])
            p.op("scalar", lambda e: e.activation(linvb.ap, lrow.ap, AF.Copy),
                 ins=[lrow], outs=[linvb])
            p.op("tensor", lambda e: e.matmul(ps_s[0].ap, onerb.ap, linvb.ap,
                                              start=True, stop=True),
                 ins=[onerb, linvb], outs=[ps_s[0]])
            p.op("scalar", lambda e: e.activation(lb_sb.ap, ps_s[0].ap, AF.Copy),
                 ins=[ps_s[0]], outs=[lb_sb])
            p.op("vector", lambda e, h=h, qb=qb: e.tensor_mul(
                ysb[:, h * TPC + qb * 512: h * TPC + (qb + 1) * 512],
                ps_o.ap, lb_sb.ap), ins=[ps_o, lb_sb], outs=[ysb])

    for eo in range(NC):
        sl = woslab[eo % 2]
        p.op("gpsimd", lambda e, eo=eo, sl=sl: e.dma_start(
            sl.rearrange("p (dc e) -> p dc e", dc=NC),
            d_wo.rearrange("(dc p) e -> p dc e", p=128)[:, :, eo * 128:(eo + 1) * 128]),
            outs=[sl], dma=True)
        sb = s_sb[eo % 2]
        for tb in range(2):
            ps = ps_s[tb]
            def mmo(e, sl=sl, tb=tb, ps=ps):
                last = None
                for dc in range(NC):
                    last = e.matmul(ps.ap, sl[:, dc * 128:(dc + 1) * 128],
                                    ysb[:, dc * TPC + tb * 512: dc * TPC + (tb + 1) * 512],
                                    start=(dc == 0), stop=(dc == NC - 1))
                return last
            p.op("tensor", mmo, ins=[sl, ysb], outs=[ps])
            p.op("scalar", lambda e, tb=tb, ps=ps, sb=sb: e.activation(
                sb[:, tb * 512:(tb + 1) * 512], ps.ap, AF.Copy), ins=[ps], outs=[sb])
        out_toks.append(p.op("gpsimd", lambda e, eo=eo, sb=sb: e.dma_start(
            d_s[eo * 128:(eo + 1) * 128, :], sb.ap), ins=[sb], dma=True))

    p.finish(out_toks)
    p.build()
    return nc


def run_l0c(h, la):
    nc = build_l0c()
    kfull = np.concatenate([np.asarray(r["kf"]) for r in la], axis=1)
    vfull = np.concatenate([np.asarray(r["vt"]) for r in la], axis=0)
    k2all = np.stack([np.asarray(r["k2m"])[0] for r in la], axis=1)
    in_maps = []
    for c in range(NCORE):
        in_maps.append(dict(
            qf=np.asarray(la[c]["qf"]),
            kfull=kfull, vfull=vfull, k2all=np.ascontiguousarray(k2all),
            woT=h["woT"], masks_c=h["masks"][c],
            ones_row_bf=h["ones_row_bf"], ones_col_bf=h["ones_col_bf"],
            ident_bf=h["ident_bf"],
        ))
    res = run_bass_kernel_spmd(nc, in_maps, core_ids=list(range(NCORE)))
    return res.results




def build_l0b():
    nc = bass.Bass("TRN2", target_bir_lowering=False)
    d_gi = nc.dram_tensor("gidx_sl", [D, OPC], I32, kind="ExternalInput")
    d_ui = nc.dram_tensor("uidx_sl", [D, OPC], I32, kind="ExternalInput")
    d_di = nc.dram_tensor("didx_sl", [INTER, 2 * 128], I32, kind="ExternalInput")
    d_lut = nc.dram_tensor("lutrep", [128, 16], F32, kind="ExternalInput")
    d_wg = nc.dram_tensor("wgT_sl", [D, OPC], BF16, kind="ExternalOutput")
    d_wu = nc.dram_tensor("wuT_sl", [D, OPC], BF16, kind="ExternalOutput")
    d_wd = nc.dram_tensor("wdT_sl", [INTER, 2 * 128], BF16, kind="ExternalOutput")

    p = Prog(nc)
    out_toks = []
    lut = p.sbuf("lut", [128, 16], F32)
    p.op("sync", lambda e: e.dma_start(lut.ap, d_lut[:]), outs=[lut], dma=True)

    idxb = [p.sbuf(f"idxb{i}", [128, OPC], I32) for i in range(2)]
    accb = [p.sbuf(f"accb{i}", [128, OPC], BF16) for i in range(2)]
    term = [p.sbuf(f"term{i}", [128, OPC], BF16) for i in range(2)]

    def construct(d_idx, d_out, nrow, ncol):
        nslab = (nrow + 127) // 128
        for s in range(nslab):
            r0, r1 = s * 128, min((s + 1) * 128, nrow)
            ib, ab, tb = idxb[s % 2], accb[s % 2], term[s % 2]
            p.op("sync", lambda e, d_idx=d_idx, r0=r0, r1=r1, ib=ib, ncol=ncol:
                 e.dma_start(ib[0:r1 - r0, 0:ncol], d_idx[r0:r1, :]),
                 outs=[ib], dma=True)
            np_ = r1 - r0
            for j in range(16):
                if j == 0:
                    p.op("vector", lambda e, ib=ib, ab=ab, np_=np_, ncol=ncol:
                         e.tensor_scalar(ab[0:np_, 0:ncol], ib[0:np_, 0:ncol], 0.0,
                                         lut[0:np_, 0:1], ALU.is_equal, ALU.mult),
                         ins=[ib, lut], outs=[ab])
                else:
                    p.op("vector", lambda e, j=j, ib=ib, tb=tb, np_=np_, ncol=ncol:
                         e.tensor_scalar(tb[0:np_, 0:ncol], ib[0:np_, 0:ncol], float(j),
                                         lut[0:np_, j:j + 1], ALU.is_equal, ALU.mult),
                         ins=[ib, lut], outs=[tb])
                    p.op("vector", lambda e, ab=ab, tb=tb, np_=np_, ncol=ncol:
                         e.tensor_add(ab[0:np_, 0:ncol], ab[0:np_, 0:ncol],
                                      tb[0:np_, 0:ncol]),
                         ins=[tb, ab], outs=[ab])
            out_toks.append(p.op("gpsimd", lambda e, d_out=d_out, r0=r0, r1=r1, ab=ab,
                                 ncol=ncol: e.dma_start(d_out[r0:r1, :],
                                                        ab[0:r1 - r0, 0:ncol]),
                                 ins=[ab], dma=True))

    construct(d_gi, d_wg, D, OPC)
    construct(d_ui, d_wu, D, OPC)
    construct(d_di, d_wd, INTER, 2 * 128)
    p.finish(out_toks)
    p.build()
    return nc


def run_l0b(h):
    nc = build_l0b()
    in_maps = []
    for c in range(NCORE):
        osl = slice(c * OPC, (c + 1) * OPC)
        dsl = slice(c * 2 * 128, (c + 1) * 2 * 128)
        in_maps.append(dict(
            gidx_sl=np.ascontiguousarray(h["gidxT"][:, osl]),
            uidx_sl=np.ascontiguousarray(h["uidxT"][:, osl]),
            didx_sl=np.ascontiguousarray(h["didxT"][:, dsl]),
            lutrep=h["lutrep"],
        ))
    res = run_bass_kernel_spmd(nc, in_maps, core_ids=list(range(NCORE)))
    return res.results


def build_l0d():
    nc = bass.Bass("TRN2", target_bir_lowering=False)
    d_xn = nc.dram_tensor("xn", [D, TPC], BF16, kind="ExternalInput")
    d_x = nc.dram_tensor("xT_c", [D, TPC], F32, kind="ExternalInput")
    d_s = nc.dram_tensor("s_out", [D, TPC], BF16, kind="ExternalInput")
    d_wg = nc.dram_tensor("wgT", [D, INTER], BF16, kind="ExternalInput")
    d_wu = nc.dram_tensor("wuT", [D, INTER], BF16, kind="ExternalInput")
    d_wd = nc.dram_tensor("wdT", [INTER, D], BF16, kind="ExternalInput")
    d_x1 = nc.dram_tensor("x1T_c", [D, TPC], F32, kind="ExternalOutput")

    NOC = INTER // 128
    p = Prog(nc)
    out_toks = []
    xnc = [p.sbuf(f"xn{c}", [128, TPC], BF16) for c in range(NC)]
    hbuf = p.sbuf("hbuf", [128, NOC * 512], BF16)
    gslab = [p.sbuf(f"gslab{i}", [128, NC * 128], BF16) for i in range(2)]
    uslab = [p.sbuf(f"uslab{i}", [128, NC * 128], BF16) for i in range(2)]
    dslab = [p.sbuf(f"dslab{i}", [128, NOC * 128], BF16) for i in range(2)]
    gs = p.sbuf("gs", [128, 512], BF16)
    xcb = [p.sbuf(f"xcb{i}", [128, 512], F32) for i in range(2)]
    scb = [p.sbuf(f"scb{i}", [128, 512], BF16) for i in range(2)]
    gcb = p.sbuf("gcb", [128, 512], BF16)
    tmpb = p.sbuf("tmpb", [128, 512], BF16)

    ps_g = [p.psum(f"ps_g{i}", [128, 512]) for i in range(2)]
    ps_u = [p.psum(f"ps_u{i}", [128, 512]) for i in range(2)]
    ps_d = [p.psum(f"ps_d{i}", [128, 512]) for i in range(2)]

    for c in range(NC):
        p.op("sync", lambda e, c=c: e.dma_start(xnc[c].ap, d_xn[c * 128:(c + 1) * 128, :]),
             outs=[xnc[c]], dma=True)

    for tb in range(2):
        ts = slice(tb * 512, (tb + 1) * 512)
        for oc in range(NOC):
            g_sl, u_sl = gslab[oc % 2], uslab[oc % 2]
            p.op("gpsimd", lambda e, oc=oc, g_sl=g_sl: e.dma_start(
                g_sl.rearrange("p (dc o) -> p dc o", dc=NC),
                d_wg.rearrange("(dc p) o -> p dc o", p=128)[:, :, oc * 128:(oc + 1) * 128]),
                outs=[g_sl], dma=True)
            p.op("gpsimd", lambda e, oc=oc, u_sl=u_sl: e.dma_start(
                u_sl.rearrange("p (dc o) -> p dc o", dc=NC),
                d_wu.rearrange("(dc p) o -> p dc o", p=128)[:, :, oc * 128:(oc + 1) * 128]),
                outs=[u_sl], dma=True)
            pg, pu = ps_g[oc % 2], ps_u[oc % 2]
            def mmg(e, g_sl=g_sl, ts=ts, pg=pg):
                last = None
                for dc in range(NC):
                    last = e.matmul(pg.ap, g_sl[:, dc * 128:(dc + 1) * 128],
                                    xnc[dc][:, ts], start=(dc == 0), stop=(dc == NC - 1))
                return last
            p.op("tensor", mmg, ins=[g_sl] + xnc, outs=[pg])
            def mmu(e, u_sl=u_sl, ts=ts, pu=pu):
                last = None
                for dc in range(NC):
                    last = e.matmul(pu.ap, u_sl[:, dc * 128:(dc + 1) * 128],
                                    xnc[dc][:, ts], start=(dc == 0), stop=(dc == NC - 1))
                return last
            p.op("tensor", mmu, ins=[u_sl] + xnc, outs=[pu])
            p.op("scalar", lambda e, pg=pg: e.activation(gs.ap, pg.ap, AF.Silu),
                 ins=[pg], outs=[gs])
            p.op("vector", lambda e, oc=oc, pu=pu: e.tensor_mul(
                hbuf[:, oc * 512:(oc + 1) * 512], gs.ap, pu.ap),
                ins=[gs, pu], outs=[hbuf])
        for hc in range(NC):
            d_sl = dslab[hc % 2]
            p.op("gpsimd", lambda e, hc=hc, d_sl=d_sl: e.dma_start(
                d_sl.rearrange("p (ic h) -> p ic h", ic=NOC),
                d_wd.rearrange("(ic p) h -> p ic h", p=128)[:, :, hc * 128:(hc + 1) * 128]),
                outs=[d_sl], dma=True)
            pd = ps_d[hc % 2]
            def mmd(e, d_sl=d_sl, pd=pd):
                last = None
                for ic in range(NOC):
                    last = e.matmul(pd.ap, d_sl[:, ic * 128:(ic + 1) * 128],
                                    hbuf[:, ic * 512:(ic + 1) * 512],
                                    start=(ic == 0), stop=(ic == NOC - 1))
                return last
            p.op("tensor", mmd, ins=[d_sl, hbuf], outs=[pd])
            p.op("scalar", lambda e, pd=pd: e.activation(gcb.ap, pd.ap, AF.Silu),
                 ins=[pd], outs=[gcb])
            hs = slice(hc * 128, (hc + 1) * 128)
            xb, sb = xcb[hc % 2], scb[hc % 2]
            p.op("sync", lambda e, hs=hs, ts=ts, xb=xb: e.dma_start(
                xb.ap, d_x[hs, ts]), outs=[xb], dma=True)
            p.op("sync", lambda e, hs=hs, ts=ts, sb=sb: e.dma_start(
                sb.ap, d_s[hs, ts]), outs=[sb], dma=True)
            p.op("vector", lambda e, sb=sb: e.tensor_mul(tmpb.ap, gcb.ap, sb.ap),
                 ins=[gcb, sb], outs=[tmpb])
            p.op("vector", lambda e, xb=xb: e.tensor_add(xb.ap, xb.ap, tmpb.ap),
                 ins=[tmpb, xb], outs=[xb])
            out_toks.append(p.op("gpsimd", lambda e, hs=hs, ts=ts, xb=xb: e.dma_start(
                d_x1[hs, ts], xb.ap), ins=[xb], dma=True))
    p.finish(out_toks)
    p.build()
    return nc


def run_l0bd(h, la, lc, lb):
    nc = build_l0d()
    wgT = np.concatenate([np.asarray(r["wgT_sl"]) for r in lb], axis=1)
    wuT = np.concatenate([np.asarray(r["wuT_sl"]) for r in lb], axis=1)
    wdT = np.concatenate([np.asarray(r["wdT_sl"]) for r in lb], axis=1)
    in_maps = []
    for c in range(NCORE):
        ts = slice(c * TPC, (c + 1) * TPC)
        in_maps.append(dict(
            xn=np.asarray(la[c]["xn"]),
            xT_c=np.ascontiguousarray(h["xT"][:, ts]),
            s_out=np.asarray(lc[c]["s_out"]),
            wgT=wgT, wuT=wuT, wdT=wdT,
        ))
    res = run_bass_kernel_spmd(nc, in_maps, core_ids=list(range(NCORE)))
    return res.results




def build_l1a():
    nc = bass.Bass("TRN2", target_bir_lowering=False)
    d_x = nc.dram_tensor("x1T_c", [D, TPC], F32, kind="ExternalInput")
    d_wq = nc.dram_tensor("wq1", [D, D], F32, kind="ExternalInput")
    d_wk = nc.dram_tensor("wk1", [D, D], F32, kind="ExternalInput")
    d_wv = nc.dram_tensor("wv1", [D, D], F32, kind="ExternalInput")
    d_qs = nc.dram_tensor("qscale", [128, 1], F32, kind="ExternalInput")
    d_ks = nc.dram_tensor("kscale", [128, 1], F32, kind="ExternalInput")
    d_onecf = nc.dram_tensor("ones_col_f", [128, 1], F32, kind="ExternalInput")
    d_onerf = nc.dram_tensor("ones_row_f", [1, 128], F32, kind="ExternalInput")
    d_onecb = nc.dram_tensor("ones_col_bf", [128, 1], BF16, kind="ExternalInput")
    d_onerb = nc.dram_tensor("ones_row_bf", [1, 128], BF16, kind="ExternalInput")

    d_xn = nc.dram_tensor("xn1", [D, TPC], BF16, kind="ExternalOutput")
    d_qn = nc.dram_tensor("qn", [D, TPC], BF16, kind="ExternalOutput")
    d_kn = nc.dram_tensor("kn", [D, TPC], BF16, kind="ExternalOutput")
    d_v = nc.dram_tensor("v1t", [TPC, D], BF16, kind="ExternalOutput")

    p = Prog(nc)
    out_toks = []
    xc = [p.sbuf(f"sx{c}", [128, TPC], F32) for c in range(NC)]
    xnc = [p.sbuf(f"sxn{c}", [128, TPC], BF16) for c in range(NC)]
    sqc = p.sbuf("sqc", [128, TPC], F32)
    onecf = p.sbuf("onecf", [128, 1], F32)
    onerf = p.sbuf("onerf", [1, 128], F32)
    onecb = p.sbuf("onecb", [128, 1], BF16)
    onerb = p.sbuf("onerb", [1, 128], BF16)
    qscol = p.sbuf("qscol", [128, 1], F32)
    kscol = p.sbuf("kscol", [128, 1], F32)
    rowA = p.sbuf("rowA", [1, TPC], F32)
    rowB = p.sbuf("rowB", [1, TPC], F32)
    rowC = p.sbuf("rowC", [1, TPC], F32)
    tmpc = p.sbuf("tmpc", [128, TPC], F32)
    wslab = [p.sbuf(f"wslab{i}", [128, NC * 128], BF16) for i in range(2)]
    vslab = [p.sbuf(f"vslab{i}", [128, NC * 512], BF16) for i in range(2)]
    qkh = [p.sbuf(f"qkh{i}", [128, TPC], BF16) for i in range(2)]
    sqh = p.sbuf("sqh", [128, 512], BF16)
    r512 = p.sbuf("r512", [1, 512], F32)
    rbf = p.sbuf("rbf", [1, 512], BF16)
    rb_sb = p.sbuf("rb_sb", [128, 512], BF16)
    vtt = [p.sbuf(f"vtt{i}", [128, 512], BF16) for i in range(2)]

    T1 = p.psum("T1", [128, TPC])
    T2 = p.psum("T2", [128, TPC])
    T3 = p.psum("T3", [128, 512])
    T4 = p.psum("T4", [128, 512])
    ps_sum, ps_sq = T1, T2
    ps_b, ps_b2 = T1, T2
    ps_qk = [T3, T4]
    ps_q2 = T2
    ps_rb = T1
    ps_v = [T3, T4]

    for c in range(NC):
        p.op("sync", lambda e, c=c: e.dma_start(xc[c].ap, d_x[c * 128:(c + 1) * 128, :]),
             outs=[xc[c]], dma=True)
    for b, d in ((onecf, d_onecf), (onerf, d_onerf), (onecb, d_onecb),
                 (onerb, d_onerb), (qscol, d_qs), (kscol, d_ks)):
        p.op("sync", lambda e, b=b, d=d: e.dma_start(b.ap, d[:]), outs=[b], dma=True)

    for c in range(NC):
        p.op("scalar", lambda e, c=c: e.activation(sqc.ap, xc[c].ap, AF.Square),
             ins=[xc[c]], outs=[sqc])
        def mm_sum(e, c=c):
            last = None
            for tb in range(2):
                last = e.matmul(ps_sum[0:1, tb * 512:(tb + 1) * 512], onecf.ap,
                                xc[c][:, tb * 512:(tb + 1) * 512],
                                start=(c == 0), stop=(c == NC - 1))
            return last
        p.op("tensor", mm_sum, ins=[onecf, xc[c]], outs=[ps_sum])
        def mm_sq(e, c=c):
            last = None
            for tb in range(2):
                last = e.matmul(ps_sq[0:1, tb * 512:(tb + 1) * 512], onecf.ap,
                                sqc[:, tb * 512:(tb + 1) * 512],
                                start=(c == 0), stop=(c == NC - 1))
            return last
        p.op("tensor", mm_sq, ins=[onecf, sqc], outs=[ps_sq])
    p.op("scalar", lambda e: e.activation(rowA.ap, ps_sum[0:1, :], AF.Copy, scale=1.0 / D),
         ins=[ps_sum], outs=[rowA])
    p.op("scalar", lambda e: e.activation(rowB.ap, ps_sq[0:1, :], AF.Copy, scale=1.0 / D),
         ins=[ps_sq], outs=[rowB])
    p.op("scalar", lambda e: e.activation(rowC.ap, rowA.ap, AF.Square),
         ins=[rowA], outs=[rowC])
    p.op("vector", lambda e: e.tensor_sub(rowB.ap, rowB.ap, rowC.ap),
         ins=[rowB, rowC], outs=[rowB])
    p.op("vector", lambda e: e.tensor_scalar_add(rowB.ap, rowB.ap, 1e-5),
         ins=[rowB], outs=[rowB])
    p.op("scalar", lambda e: e.activation(rowC.ap, rowB.ap, AF.Sqrt),
         ins=[rowB], outs=[rowC])
    p.op("vector", lambda e: e.reciprocal(rowB.ap, rowC.ap),
         ins=[rowC], outs=[rowB])
    p.op("vector", lambda e: e.tensor_mul(rowC.ap, rowA.ap, rowB.ap),
         ins=[rowA, rowB], outs=[rowC])
    p.op("scalar", lambda e: e.activation(rowC.ap, rowC.ap, AF.Copy, scale=-1.0),
         ins=[rowC], outs=[rowC])
    def bc_r(e):
        last = None
        for tb in range(2):
            last = e.matmul(ps_b[:, tb * 512:(tb + 1) * 512], onerf.ap,
                            rowB[0:1, tb * 512:(tb + 1) * 512], start=True, stop=True)
        return last
    p.op("tensor", bc_r, ins=[onerf, rowB], outs=[ps_b])
    def bc_m(e):
        last = None
        for tb in range(2):
            last = e.matmul(ps_b2[:, tb * 512:(tb + 1) * 512], onerf.ap,
                            rowC[0:1, tb * 512:(tb + 1) * 512], start=True, stop=True)
        return last
    p.op("tensor", bc_m, ins=[onerf, rowC], outs=[ps_b2])
    for c in range(NC):
        p.op("vector", lambda e, c=c: e.tensor_mul(tmpc.ap, xc[c].ap, ps_b.ap),
             ins=[xc[c], ps_b], outs=[tmpc])
        p.op("vector", lambda e, c=c: e.tensor_add(xnc[c].ap, tmpc.ap, ps_b2.ap),
             ins=[tmpc, ps_b2], outs=[xnc[c]])
        out_toks.append(p.op("gpsimd", lambda e, c=c: e.dma_start(
            d_xn[c * 128:(c + 1) * 128, :], xnc[c].ap), ins=[xnc[c]], dma=True))

    def emit_qk1(d_wgt, scol, d_out):
        for h in range(NH):
            sl = wslab[h % 2]
            hb = qkh[h % 2]
            p.op("gpsimd", lambda e, h=h, sl=sl, d_wgt=d_wgt: e.dma_start(
                sl.rearrange("p (dc e) -> p dc e", dc=NC),
                d_wgt.rearrange("(dc p) e -> p dc e", p=128)[:, :, h * 128:(h + 1) * 128]),
                outs=[sl], dma=True)
            for tb in range(2):
                ps = ps_qk[tb]
                def mm(e, sl=sl, tb=tb, ps=ps):
                    last = None
                    for dc in range(NC):
                        last = e.matmul(ps.ap, sl[:, dc * 128:(dc + 1) * 128],
                                        xnc[dc][:, tb * 512:(tb + 1) * 512],
                                        start=(dc == 0), stop=(dc == NC - 1))
                    return last
                p.op("tensor", mm, ins=[sl] + xnc, outs=[ps])
                p.op("scalar", lambda e, ps=ps: e.activation(sqh.ap, ps.ap, AF.Square),
                     ins=[ps], outs=[sqh])
                p.op("tensor", lambda e: e.matmul(ps_q2[0:1, 0:512], onecb.ap, sqh.ap,
                                                  start=True, stop=True),
                     ins=[onecb, sqh], outs=[ps_q2])
                p.op("scalar", lambda e: e.activation(r512.ap, ps_q2[0:1, 0:512],
                                                      AF.Copy, scale=1.0 / DH),
                     ins=[ps_q2], outs=[r512])
                p.op("vector", lambda e: e.tensor_scalar_add(r512.ap, r512.ap, 1e-6),
                     ins=[r512], outs=[r512])
                p.op("scalar", lambda e: e.activation(r512.ap, r512.ap, AF.Sqrt),
                     ins=[r512], outs=[r512])
                p.op("vector", lambda e: e.reciprocal(r512.ap, r512.ap),
                     ins=[r512], outs=[r512])
                p.op("scalar", lambda e: e.activation(rbf.ap, r512.ap, AF.Copy),
                     ins=[r512], outs=[rbf])
                p.op("tensor", lambda e: e.matmul(ps_rb[:, 0:512], onerb.ap, rbf.ap,
                                                  start=True, stop=True),
                     ins=[onerb, rbf], outs=[ps_rb])
                p.op("scalar", lambda e: e.activation(rb_sb.ap, ps_rb[:, 0:512], AF.Copy),
                     ins=[ps_rb], outs=[rb_sb])
                ts = slice(tb * 512, (tb + 1) * 512)
                p.op("vector", lambda e, hb=hb, ts=ts, ps=ps, scol=scol:
                     e.scalar_tensor_tensor(hb[:, ts], ps.ap, scol.ap, rb_sb.ap,
                                            ALU.mult, ALU.mult),
                     ins=[ps, scol, rb_sb], outs=[hb])
            out_toks.append(p.op("gpsimd", lambda e, h=h, hb=hb, d_out=d_out: e.dma_start(
                d_out[h * 128:(h + 1) * 128, :], hb.ap), ins=[hb], dma=True))

    emit_qk1(d_wq, qscol, d_qn)
    emit_qk1(d_wk, kscol, d_kn)

    for eb in range(4):
        sl = vslab[eb % 2]
        p.op("gpsimd", lambda e, eb=eb, sl=sl: e.dma_start(
            sl.rearrange("p (dc e) -> p dc e", dc=NC),
            d_wv.rearrange("(dc p) e -> p dc e", p=128)[:, :, eb * 512:(eb + 1) * 512]),
            outs=[sl], dma=True)
        for tt in range(TPC // 128):
            ps = ps_v[tt % 2]
            def mmv(e, tt=tt, sl=sl, ps=ps):
                last = None
                for dc in range(NC):
                    last = e.matmul(ps.ap, xnc[dc][:, tt * 128:(tt + 1) * 128],
                                    sl[:, dc * 512:(dc + 1) * 512],
                                    start=(dc == 0), stop=(dc == NC - 1))
                return last
            p.op("tensor", mmv, ins=[sl] + xnc, outs=[ps])
            vt_b = vtt[tt % 2]
            p.op("scalar", lambda e, ps=ps, vt_b=vt_b: e.activation(
                vt_b.ap, ps.ap, AF.Copy), ins=[ps], outs=[vt_b])
            out_toks.append(p.op("gpsimd", lambda e, tt=tt, eb=eb, vt_b=vt_b: e.dma_start(
                d_v[tt * 128:(tt + 1) * 128, eb * 512:(eb + 1) * 512], vt_b.ap),
                ins=[vt_b], dma=True))

    p.finish(out_toks)
    p.build()
    return nc


def run_l1a(h, ld):
    nc = build_l1a()
    in_maps = []
    for c in range(NCORE):
        in_maps.append(dict(
            x1T_c=np.asarray(ld[c]["x1T_c"]),
            wq1=h["wq1"], wk1=h["wk1"], wv1=h["wv1"],
            qscale=h["qscale"], kscale=h["kscale"],
            ones_col_f=h["ones_col_f"], ones_row_f=h["ones_row_f"],
            ones_col_bf=h["ones_col_bf"], ones_row_bf=h["ones_row_bf"],
        ))
    res = run_bass_kernel_spmd(nc, in_maps, core_ids=list(range(NCORE)))
    return res.results


def build_l1b():
    nc = bass.Bass("TRN2", target_bir_lowering=False)
    d_q = nc.dram_tensor("qn", [D, TPC], BF16, kind="ExternalInput")
    d_kf = nc.dram_tensor("knfull", [D, SEQ], BF16, kind="ExternalInput")
    d_vf = nc.dram_tensor("v1full", [SEQ, D], BF16, kind="ExternalInput")
    d_wo = nc.dram_tensor("wo1", [D, D], F32, kind="ExternalInput")
    d_onerb = nc.dram_tensor("ones_row_bf", [1, 128], BF16, kind="ExternalInput")
    d_onecb = nc.dram_tensor("ones_col_bf", [128, 1], BF16, kind="ExternalInput")
    d_idf = nc.dram_tensor("ident_f", [128, 128], F32, kind="ExternalInput")
    d_s = nc.dram_tensor("s1_out", [D, TPC], BF16, kind="ExternalOutput")

    p = Prog(nc)
    out_toks = []
    qsb = p.sbuf("qsb", [128, NH * TPC], BF16)
    ysb = p.sbuf("ysb", [128, NH * TPC], BF16)
    ksb = [p.sbuf(f"ksb{i}", [128, SEQ], BF16) for i in range(2)]
    vsb = [p.sbuf(f"vsb{i}", [128, SEQ], BF16) for i in range(2)]
    onerb = p.sbuf("onerb", [1, 128], BF16)
    onecb = p.sbuf("onecb", [128, 1], BF16)
    idf = p.sbuf("idf", [128, 128], F32)
    m_all = p.sbuf("m_all", [128, 16], F32)
    mcol = p.sbuf("mcol", [128, 1], F32)
    negm = p.sbuf("negm", [1, 512], BF16)
    p_sb = [p.sbuf(f"p_sb{i}", [128, 512], BF16) for i in range(2)]
    lrow = p.sbuf("lrow", [1, 512], F32)
    linvb = p.sbuf("linvb", [1, 512], BF16)
    lb_sb = p.sbuf("lb_sb", [128, 512], BF16)
    woslab = [p.sbuf(f"woslab{i}", [128, NC * 128], BF16) for i in range(2)]
    s_sb = [p.sbuf(f"s_sb{i}", [128, TPC], BF16) for i in range(2)]

    ps_s = [p.psum(f"ps_s{i}", [128, 512]) for i in range(2)]
    ps_o = p.psum("ps_o", [128, 512])
    ps_l = p.psum("ps_l", [1, 512])
    ps_t = p.psum("ps_t", [1, 128])

    p.op("gpsimd", lambda e: e.dma_start(
        qsb.rearrange("p (h t) -> p h t", h=NH),
        d_q.rearrange("(h p) t -> p h t", p=128)), outs=[qsb], dma=True)
    for b, d in ((onerb, d_onerb), (onecb, d_onecb), (idf, d_idf)):
        p.op("sync", lambda e, b=b, d=d: e.dma_start(b.ap, d[:]), outs=[b], dma=True)

    for h in range(NH):
        kb, vb = ksb[h % 2], vsb[h % 2]
        p.op("sync", lambda e, h=h, kb=kb: e.dma_start(
            kb.ap, d_kf[h * 128:(h + 1) * 128, :]), outs=[kb], dma=True)
        p.op("sync", lambda e, h=h, vb=vb: e.dma_start(
            vb.rearrange("p (kt d) -> p kt d", kt=64),
            d_vf.rearrange("(kt p) d -> p kt d", p=128)[:, :, h * 128:(h + 1) * 128]),
            outs=[vb], dma=True)
        for qb in range(2):
            for qt in range(4):
                q0 = h * TPC + qb * 512 + qt * 128
                for kc in range(16):
                    ps = ps_s[kc % 2]
                    p.op("tensor", lambda e, q0=q0, kc=kc, ps=ps, kb=kb: e.matmul(
                        ps.ap, qsb[:, q0:q0 + 128], kb[:, kc * 512:(kc + 1) * 512],
                        start=True, stop=True), ins=[qsb, kb], outs=[ps])
                    p.op("vector", lambda e, kc=kc, ps=ps: e.reduce_max(
                        m_all[:, kc:kc + 1], ps.ap, AX.X), ins=[ps], outs=[m_all])
                p.op("vector", lambda e: e.reduce_max(mcol.ap, m_all.ap, AX.X),
                     ins=[m_all], outs=[mcol])
                p.op("tensor", lambda e: e.transpose(ps_t.ap, mcol.ap, idf.ap),
                     ins=[mcol, idf], outs=[ps_t])
                p.op("scalar", lambda e, qt=qt: e.activation(
                    negm[0:1, qt * 128:(qt + 1) * 128], ps_t.ap, AF.Copy, scale=-1.0),
                    ins=[ps_t], outs=[negm])
            for kt in range(64):
                ps = ps_s[kt % 2]
                pb = p_sb[kt % 2]
                def mms(e, h=h, qb=qb, kt=kt, ps=ps, kb=kb):
                    e.matmul(ps.ap, onerb.ap, negm.ap, start=True, stop=False)
                    return e.matmul(ps.ap, kb[:, kt * 128:(kt + 1) * 128],
                                    qsb[:, h * TPC + qb * 512: h * TPC + (qb + 1) * 512],
                                    start=False, stop=True)
                p.op("tensor", mms, ins=[onerb, negm, kb, qsb], outs=[ps])
                p.op("scalar", lambda e, ps=ps, pb=pb: e.activation(pb.ap, ps.ap, AF.Exp),
                     ins=[ps], outs=[pb])
                p.op("tensor", lambda e, kt=kt, pb=pb, vb=vb: e.matmul(
                    ps_o.ap, vb[:, kt * 128:(kt + 1) * 128], pb.ap,
                    start=(kt == 0), stop=(kt == 63)), ins=[vb, pb], outs=[ps_o])
                p.op("tensor", lambda e, kt=kt, pb=pb: e.matmul(
                    ps_l[0:1, :], onecb.ap, pb.ap,
                    start=(kt == 0), stop=(kt == 63)), ins=[onecb, pb], outs=[ps_l])
            p.op("scalar", lambda e: e.activation(lrow.ap, ps_l[0:1, :], AF.Copy),
                 ins=[ps_l], outs=[lrow])
            p.op("vector", lambda e: e.reciprocal(lrow.ap, lrow.ap),
                 ins=[lrow], outs=[lrow])
            p.op("scalar", lambda e: e.activation(linvb.ap, lrow.ap, AF.Copy),
                 ins=[lrow], outs=[linvb])
            p.op("tensor", lambda e: e.matmul(ps_s[0].ap, onerb.ap, linvb.ap,
                                              start=True, stop=True),
                 ins=[onerb, linvb], outs=[ps_s[0]])
            p.op("scalar", lambda e: e.activation(lb_sb.ap, ps_s[0].ap, AF.Copy),
                 ins=[ps_s[0]], outs=[lb_sb])
            p.op("vector", lambda e, h=h, qb=qb: e.tensor_mul(
                ysb[:, h * TPC + qb * 512: h * TPC + (qb + 1) * 512],
                ps_o.ap, lb_sb.ap), ins=[ps_o, lb_sb], outs=[ysb])

    for eo in range(NC):
        sl = woslab[eo % 2]
        p.op("gpsimd", lambda e, eo=eo, sl=sl: e.dma_start(
            sl.rearrange("p (dc e) -> p dc e", dc=NC),
            d_wo.rearrange("(dc p) e -> p dc e", p=128)[:, :, eo * 128:(eo + 1) * 128]),
            outs=[sl], dma=True)
        sb = s_sb[eo % 2]
        for tb in range(2):
            ps = ps_s[tb]
            def mmo(e, sl=sl, tb=tb, ps=ps):
                last = None
                for dc in range(NC):
                    last = e.matmul(ps.ap, sl[:, dc * 128:(dc + 1) * 128],
                                    ysb[:, dc * TPC + tb * 512: dc * TPC + (tb + 1) * 512],
                                    start=(dc == 0), stop=(dc == NC - 1))
                return last
            p.op("tensor", mmo, ins=[sl, ysb], outs=[ps])
            p.op("scalar", lambda e, tb=tb, ps=ps, sb=sb: e.activation(
                sb[:, tb * 512:(tb + 1) * 512], ps.ap, AF.Copy), ins=[ps], outs=[sb])
        out_toks.append(p.op("gpsimd", lambda e, eo=eo, sb=sb: e.dma_start(
            d_s[eo * 128:(eo + 1) * 128, :], sb.ap), ins=[sb], dma=True))

    p.finish(out_toks)
    p.build()
    return nc


def run_l1b(h, le):
    nc = build_l1b()
    knfull = np.concatenate([np.asarray(r["kn"]) for r in le], axis=1)
    v1full = np.concatenate([np.asarray(r["v1t"]) for r in le], axis=0)
    in_maps = []
    for c in range(NCORE):
        in_maps.append(dict(
            qn=np.asarray(le[c]["qn"]), knfull=knfull, v1full=v1full,
            wo1=h["wo1"], ones_row_bf=h["ones_row_bf"],
            ones_col_bf=h["ones_col_bf"], ident_f=h["ident_f"],
        ))
    res = run_bass_kernel_spmd(nc, in_maps, core_ids=list(range(NCORE)))
    return res.results




def build_l1c():
    nc = bass.Bass("TRN2", target_bir_lowering=False)
    d_xn = nc.dram_tensor("xn1", [D, TPC], BF16, kind="ExternalInput")
    d_x = nc.dram_tensor("x1T_c", [D, TPC], F32, kind="ExternalInput")
    d_s = nc.dram_tensor("s1_out", [D, TPC], BF16, kind="ExternalInput")
    d_gw = nc.dram_tensor("gwT", [D, NEXP], F32, kind="ExternalInput")
    d_wg = nc.dram_tensor("wg1", [NEXP, D, IEXP], F32, kind="ExternalInput")
    d_wu = nc.dram_tensor("wu1", [NEXP, D, IEXP], F32, kind="ExternalInput")
    d_wd = nc.dram_tensor("wd1", [NEXP, IEXP, D], F32, kind="ExternalInput")
    d_onerb = nc.dram_tensor("ones_row_bf", [1, 128], BF16, kind="ExternalInput")
    d_idf = nc.dram_tensor("ident_f", [128, 128], F32, kind="ExternalInput")
    d_scr = nc.dram_tensor("scr_comb", [NEXP, TPC], F32, kind="Internal")
    d_xo = nc.dram_tensor("xoT_c", [D, TPC], F32, kind="ExternalOutput")

    p = Prog(nc)
    out_toks = []
    xnc = [p.sbuf(f"sxn{c}", [128, TPC], BF16) for c in range(NC)]
    y_acc = p.sbuf("y_acc", [128, NC * 512], F32)
    abuf = p.sbuf("abuf", [128, 8 * 512], BF16)
    gslab = [p.sbuf(f"gslab{i}", [128, NC * 128], BF16) for i in range(2)]
    uslab = [p.sbuf(f"uslab{i}", [128, NC * 128], BF16) for i in range(2)]
    dslab = [p.sbuf(f"dslab{i}", [128, 8 * 128], BF16) for i in range(2)]
    gw_sb = p.sbuf("gw_sb", [128, NC * NEXP], BF16)
    onerb = p.sbuf("onerb", [1, 128], BF16)
    idf = p.sbuf("idf", [128, 128], F32)
    prob = p.sbuf("prob", [128, NEXP], F32)
    mask1 = p.sbuf("mask1", [128, NEXP], F32)
    p2 = p.sbuf("p2", [128, NEXP], F32)
    comb = p.sbuf("comb", [128, NEXP], F32)
    mxc = p.sbuf("mxc", [128, 1], F32)
    sumc = p.sbuf("sumc", [128, 1], F32)
    m1c = p.sbuf("m1c", [128, 1], F32)
    m2c = p.sbuf("m2c", [128, 1], F32)
    ct_sb = p.sbuf("ct_sb", [NEXP, 128], F32)
    crow = p.sbuf("crow", [1, TPC], F32)
    crowb = p.sbuf("crowb", [1, TPC], BF16)
    cb_sb = p.sbuf("cb_sb", [128, 512], BF16)
    gs = p.sbuf("gs", [128, 512], BF16)
    gsc = p.sbuf("gsc", [128, 512], BF16)
    gfin = p.sbuf("gfin", [128, 512], BF16)
    tmpb = p.sbuf("tmpb", [128, 512], BF16)
    xcb = [p.sbuf(f"xcb{i}", [128, 512], F32) for i in range(2)]
    scb = [p.sbuf(f"scb{i}", [128, 512], BF16) for i in range(2)]

    ps_misc = p.psum("ps_misc", [128, 512])
    ps_cb = p.psum("ps_cb", [128, 512])
    ps_g = [p.psum(f"ps_g{i}", [128, 512]) for i in range(2)]
    ps_u = [p.psum(f"ps_u{i}", [128, 512]) for i in range(2)]
    ps_y = [p.psum(f"ps_y{i}", [128, 512]) for i in range(2)]

    for c in range(NC):
        p.op("sync", lambda e, c=c: e.dma_start(xnc[c].ap, d_xn[c * 128:(c + 1) * 128, :]),
             outs=[xnc[c]], dma=True)
    p.op("gpsimd", lambda e: e.dma_start(
        gw_sb.rearrange("p (dc e) -> p dc e", dc=NC),
        d_gw.rearrange("(dc p) e -> p dc e", p=128)), outs=[gw_sb], dma=True)
    for b, d in ((onerb, d_onerb), (idf, d_idf)):
        p.op("sync", lambda e, b=b, d=d: e.dma_start(b.ap, d[:]), outs=[b], dma=True)

    dscr = Buf(d_scr)
    for tt in range(TPC // 128):
        def mmsc(e, tt=tt):
            last = None
            for dc in range(NC):
                last = e.matmul(ps_misc[:, 0:NEXP], xnc[dc][:, tt * 128:(tt + 1) * 128],
                                gw_sb[:, dc * NEXP:(dc + 1) * NEXP],
                                start=(dc == 0), stop=(dc == NC - 1))
            return last
        p.op("tensor", mmsc, ins=[gw_sb] + xnc, outs=[ps_misc])
        p.op("vector", lambda e: e.reduce_max(mxc.ap, ps_misc[:, 0:NEXP], AX.X),
             ins=[ps_misc], outs=[mxc])
        p.op("vector", lambda e: e.tensor_scalar_mul(mxc.ap, mxc.ap, -1.0),
             ins=[mxc], outs=[mxc])
        p.op("scalar", lambda e: e.activation(prob.ap, ps_misc[:, 0:NEXP], AF.Exp,
                                              bias=mxc.ap, accum_out=sumc.ap),
             ins=[ps_misc, mxc], outs=[prob, sumc])
        p.op("vector", lambda e: e.reciprocal(sumc.ap, sumc.ap), ins=[sumc], outs=[sumc])
        p.op("vector", lambda e: e.tensor_scalar_mul(prob.ap, prob.ap, sumc.ap),
             ins=[prob, sumc], outs=[prob])
        p.op("vector", lambda e: e.reduce_max(m1c.ap, prob.ap, AX.X),
             ins=[prob], outs=[m1c])
        p.op("vector", lambda e: e.tensor_scalar(mask1.ap, prob.ap, m1c.ap, None,
                                                 ALU.is_equal), ins=[prob, m1c], outs=[mask1])
        p.op("vector", lambda e: e.tensor_mul(p2.ap, prob.ap, mask1.ap),
             ins=[prob, mask1], outs=[p2])
        p.op("vector", lambda e: e.tensor_sub(p2.ap, prob.ap, p2.ap),
             ins=[prob, p2], outs=[p2])
        p.op("vector", lambda e: e.reduce_max(m2c.ap, p2.ap, AX.X), ins=[p2], outs=[m2c])
        p.op("vector", lambda e: e.tensor_scalar(p2.ap, p2.ap, m2c.ap, None,
                                                 ALU.is_equal), ins=[p2, m2c], outs=[p2])
        p.op("vector", lambda e: e.tensor_add(mask1.ap, mask1.ap, p2.ap),
             ins=[mask1, p2], outs=[mask1])
        p.op("vector", lambda e: e.tensor_mul(comb.ap, prob.ap, mask1.ap),
             ins=[prob, mask1], outs=[comb])
        p.op("vector", lambda e: e.tensor_add(m1c.ap, m1c.ap, m2c.ap),
             ins=[m1c, m2c], outs=[m1c])
        p.op("vector", lambda e: e.tensor_scalar_add(m1c.ap, m1c.ap, 1e-20),
             ins=[m1c], outs=[m1c])
        p.op("vector", lambda e: e.reciprocal(m1c.ap, m1c.ap), ins=[m1c], outs=[m1c])
        p.op("vector", lambda e: e.tensor_scalar_mul(comb.ap, comb.ap, m1c.ap),
             ins=[comb, m1c], outs=[comb])
        p.op("tensor", lambda e: e.transpose(ps_misc[0:NEXP, 0:128], comb.ap, idf.ap),
             ins=[comb, idf], outs=[ps_misc])
        p.op("scalar", lambda e: e.activation(ct_sb.ap, ps_misc[0:NEXP, 0:128], AF.Copy),
             ins=[ps_misc], outs=[ct_sb])
        out_toks_dummy = p.op("gpsimd", lambda e, tt=tt: e.dma_start(
            d_scr[:, tt * 128:(tt + 1) * 128], ct_sb.ap), ins=[ct_sb], outs=[dscr], dma=True)

    for tb in range(2):
        ts = slice(tb * 512, (tb + 1) * 512)
        for ex in range(NEXP):
            p.op("sync", lambda e, ex=ex: e.dma_start(crow.ap, d_scr[ex:ex + 1, :]),
                 ins=[dscr], outs=[crow], dma=True)
            p.op("scalar", lambda e: e.activation(crowb.ap, crow.ap, AF.Copy),
                 ins=[crow], outs=[crowb])
            p.op("tensor", lambda e, ts=ts: e.matmul(ps_cb.ap, onerb.ap, crowb[0:1, ts],
                                                     start=True, stop=True),
                 ins=[onerb, crowb], outs=[ps_cb])
            p.op("scalar", lambda e: e.activation(cb_sb.ap, ps_cb.ap, AF.Copy),
                 ins=[ps_cb], outs=[cb_sb])
            for ic in range(8):
                g_sl, u_sl = gslab[ic % 2], uslab[ic % 2]
                p.op("gpsimd", lambda e, ex=ex, ic=ic, g_sl=g_sl: e.dma_start(
                    g_sl.rearrange("p (dc i) -> p dc i", dc=NC),
                    d_wg[ex, :, :].rearrange("(dc p) i -> p dc i", p=128)[
                        :, :, ic * 128:(ic + 1) * 128]), outs=[g_sl], dma=True)
                p.op("gpsimd", lambda e, ex=ex, ic=ic, u_sl=u_sl: e.dma_start(
                    u_sl.rearrange("p (dc i) -> p dc i", dc=NC),
                    d_wu[ex, :, :].rearrange("(dc p) i -> p dc i", p=128)[
                        :, :, ic * 128:(ic + 1) * 128]), outs=[u_sl], dma=True)
                pg, pu = ps_g[ic % 2], ps_u[ic % 2]
                def mmg(e, g_sl=g_sl, ts=ts, pg=pg):
                    last = None
                    for dc in range(NC):
                        last = e.matmul(pg.ap, g_sl[:, dc * 128:(dc + 1) * 128],
                                        xnc[dc][:, ts], start=(dc == 0), stop=(dc == NC - 1))
                    return last
                p.op("tensor", mmg, ins=[g_sl] + xnc, outs=[pg])
                def mmu(e, u_sl=u_sl, ts=ts, pu=pu):
                    last = None
                    for dc in range(NC):
                        last = e.matmul(pu.ap, u_sl[:, dc * 128:(dc + 1) * 128],
                                        xnc[dc][:, ts], start=(dc == 0), stop=(dc == NC - 1))
                    return last
                p.op("tensor", mmu, ins=[u_sl] + xnc, outs=[pu])
                p.op("scalar", lambda e, pg=pg: e.activation(gs.ap, pg.ap, AF.Silu),
                     ins=[pg], outs=[gs])
                p.op("vector", lambda e: e.tensor_mul(gsc.ap, gs.ap, cb_sb.ap),
                     ins=[gs, cb_sb], outs=[gsc])
                p.op("vector", lambda e, ic=ic, pu=pu: e.tensor_mul(
                    abuf[:, ic * 512:(ic + 1) * 512], gsc.ap, pu.ap),
                    ins=[gsc, pu], outs=[abuf])
            for hc in range(NC):
                d_sl = dslab[hc % 2]
                p.op("gpsimd", lambda e, ex=ex, hc=hc, d_sl=d_sl: e.dma_start(
                    d_sl.rearrange("p (ic h) -> p ic h", ic=8),
                    d_wd[ex, :, :].rearrange("(ic p) h -> p ic h", p=128)[
                        :, :, hc * 128:(hc + 1) * 128]), outs=[d_sl], dma=True)
                py = ps_y[hc % 2]
                def mmd(e, d_sl=d_sl, py=py):
                    last = None
                    for ic in range(8):
                        last = e.matmul(py.ap, d_sl[:, ic * 128:(ic + 1) * 128],
                                        abuf[:, ic * 512:(ic + 1) * 512],
                                        start=(ic == 0), stop=(ic == 7))
                    return last
                p.op("tensor", mmd, ins=[d_sl, abuf], outs=[py])
                ya = slice(hc * 512, (hc + 1) * 512)
                if ex == 0:
                    p.op("scalar", lambda e, ya=ya, py=py: e.activation(
                        y_acc[:, ya], py.ap, AF.Copy), ins=[py], outs=[y_acc])
                else:
                    p.op("vector", lambda e, ya=ya, py=py: e.tensor_add(
                        y_acc[:, ya], y_acc[:, ya], py.ap), ins=[py, y_acc], outs=[y_acc])
        for hc in range(NC):
            ya = slice(hc * 512, (hc + 1) * 512)
            hs = slice(hc * 128, (hc + 1) * 128)
            xb, sb = xcb[hc % 2], scb[hc % 2]
            p.op("scalar", lambda e, ya=ya: e.activation(gfin.ap, y_acc[:, ya], AF.Silu),
                 ins=[y_acc], outs=[gfin])
            p.op("sync", lambda e, hs=hs, ts=ts, xb=xb: e.dma_start(xb.ap, d_x[hs, ts]),
                 outs=[xb], dma=True)
            p.op("sync", lambda e, hs=hs, ts=ts, sb=sb: e.dma_start(sb.ap, d_s[hs, ts]),
                 outs=[sb], dma=True)
            p.op("vector", lambda e, sb=sb: e.tensor_mul(tmpb.ap, gfin.ap, sb.ap),
                 ins=[gfin, sb], outs=[tmpb])
            p.op("vector", lambda e, xb=xb: e.tensor_add(xb.ap, xb.ap, tmpb.ap),
                 ins=[tmpb, xb], outs=[xb])
            out_toks.append(p.op("gpsimd", lambda e, hs=hs, ts=ts, xb=xb: e.dma_start(
                d_xo[hs, ts], xb.ap), ins=[xb], dma=True))

    p.finish(out_toks)
    p.build()
    return nc


def run_l1c(h, ld, le, lf):
    nc = build_l1c()
    in_maps = []
    for c in range(NCORE):
        in_maps.append(dict(
            xn1=np.asarray(le[c]["xn1"]),
            x1T_c=np.asarray(ld[c]["x1T_c"]),
            s1_out=np.asarray(lf[c]["s1_out"]),
            gwT=h["gwT"], wg1=h["wg1"], wu1=h["wu1"], wd1=h["wd1"],
            ones_row_bf=h["ones_row_bf"], ident_f=h["ident_f"],
        ))
    res = run_bass_kernel_spmd(nc, in_maps, core_ids=list(range(NCORE)))
    return res.results




def build_fused():
    nc = bass.Bass("TRN2", target_bir_lowering=False)
    NOC = INTER // 128

    d_x = nc.dram_tensor("xT_c", [D, TPC], F32, kind="ExternalInput")
    d_w = nc.dram_tensor("wqkvTp", [D, 3 * D], F32, kind="ExternalInput")
    d_cq = nc.dram_tensor("cos_q", [128, TPC], F32, kind="ExternalInput")
    d_sq = nc.dram_tensor("sinx_q", [128, TPC], F32, kind="ExternalInput")
    d_ck = nc.dram_tensor("cos_k", [128, TPC], F32, kind="ExternalInput")
    d_sk = nc.dram_tensor("sinx_k", [128, TPC], F32, kind="ExternalInput")
    d_onecf = nc.dram_tensor("ones_col_f", [128, 1], F32, kind="ExternalInput")
    d_onerf = nc.dram_tensor("ones_row_f", [1, 128], F32, kind="ExternalInput")
    d_onecb = nc.dram_tensor("ones_col_bf", [128, 1], BF16, kind="ExternalInput")
    d_onerb = nc.dram_tensor("ones_row_bf", [1, 128], BF16, kind="ExternalInput")
    d_idb = nc.dram_tensor("ident_bf", [128, 128], BF16, kind="ExternalInput")
    d_idf = nc.dram_tensor("ident_f", [128, 128], F32, kind="ExternalInput")
    d_wo = nc.dram_tensor("woT", [D, D], F32, kind="ExternalInput")
    d_msk = nc.dram_tensor("masks_c", [2, 64, 128, 512], BF16, kind="ExternalInput")
    d_gi = nc.dram_tensor("gidx_sl", [D, OPC], I32, kind="ExternalInput")
    d_ui = nc.dram_tensor("uidx_sl", [D, OPC], I32, kind="ExternalInput")
    d_di = nc.dram_tensor("didx_sl", [INTER, 256], I32, kind="ExternalInput")
    d_lut = nc.dram_tensor("lutrep", [128, 16], F32, kind="ExternalInput")
    d_wq1 = nc.dram_tensor("wq1", [D, D], F32, kind="ExternalInput")
    d_wk1 = nc.dram_tensor("wk1", [D, D], F32, kind="ExternalInput")
    d_wv1 = nc.dram_tensor("wv1", [D, D], F32, kind="ExternalInput")
    d_wo1 = nc.dram_tensor("wo1", [D, D], F32, kind="ExternalInput")
    d_qs = nc.dram_tensor("qscale", [128, 1], F32, kind="ExternalInput")
    d_ks = nc.dram_tensor("kscale", [128, 1], F32, kind="ExternalInput")
    d_gw = nc.dram_tensor("gwT", [D, NEXP], F32, kind="ExternalInput")
    d_wg1 = nc.dram_tensor("wg1", [NEXP, D, IEXP], F32, kind="ExternalInput")
    d_wu1 = nc.dram_tensor("wu1", [NEXP, D, IEXP], F32, kind="ExternalInput")
    d_wd1 = nc.dram_tensor("wd1", [NEXP, IEXP, D], F32, kind="ExternalInput")

    d_qf = nc.dram_tensor("i_qf", [D, TPC], BF16)
    d_kfl = nc.dram_tensor("i_kfl", [D, TPC], BF16)
    d_kfa = nc.dram_tensor("i_kfa", [NCORE * D, TPC], BF16)
    d_vtl = nc.dram_tensor("i_vtl", [TPC, D], BF16)
    d_vta = nc.dram_tensor("i_vta", [SEQ, D], BF16)
    d_k2l = nc.dram_tensor("i_k2l", [NH, 1], F32)
    d_k2a = nc.dram_tensor("i_k2a", [NCORE * NH, 1], F32)
    d_scrk2 = nc.dram_tensor("i_scrk2", [1, NH], F32)
    d_wgl = nc.dram_tensor("i_wgl", [D, OPC], BF16)
    d_wga = nc.dram_tensor("i_wga", [NCORE * D, OPC], BF16)
    d_wul = nc.dram_tensor("i_wul", [D, OPC], BF16)
    d_wua = nc.dram_tensor("i_wua", [NCORE * D, OPC], BF16)
    d_wdl = nc.dram_tensor("i_wdl", [INTER, 256], BF16)
    d_wda = nc.dram_tensor("i_wda", [NCORE * INTER, 256], BF16)
    d_qn = nc.dram_tensor("i_qn", [D, TPC], BF16)
    d_knl = nc.dram_tensor("i_knl", [D, TPC], BF16)
    d_kna = nc.dram_tensor("i_kna", [NCORE * D, TPC], BF16)
    d_v1l = nc.dram_tensor("i_v1l", [TPC, D], BF16)
    d_v1a = nc.dram_tensor("i_v1a", [SEQ, D], BF16)
    d_scrc = nc.dram_tensor("i_scrc", [NEXP, TPC], F32)
    d_x1s = nc.dram_tensor("i_x1s", [D, TPC], F32)

    d_xo = nc.dram_tensor("xoT_c", [D, TPC], F32, kind="ExternalOutput")

    p = Prog(nc)
    out_toks = []

    xnc = [p.sbuf(f"P_xn{c}", [128, TPC], BF16) for c in range(NC)]
    s_res = p.sbuf("P_s", [128, NC * TPC], BF16)
    onecf = p.sbuf("P_onecf", [128, 1], F32)
    onerf = p.sbuf("P_onerf", [1, 128], F32)
    onecb = p.sbuf("P_onecb", [128, 1], BF16)
    onerb = p.sbuf("P_onerb", [1, 128], BF16)
    idb = p.sbuf("P_idb", [128, 128], BF16)
    idf = p.sbuf("P_idf", [128, 128], F32)
    qscol = p.sbuf("P_qscol", [128, 1], F32)
    kscol = p.sbuf("P_kscol", [128, 1], F32)
    lut = p.sbuf("P_lut", [128, 16], F32)
    for b, d in ((onecf, d_onecf), (onerf, d_onerf), (onecb, d_onecb),
                 (onerb, d_onerb), (idb, d_idb), (idf, d_idf),
                 (qscol, d_qs), (kscol, d_ks), (lut, d_lut)):
        p.op("sync", lambda e, b=b, d=d: e.dma_start(b.ap, d[:]), outs=[b], dma=True)

    bkfl, bvtl, bk2l = Buf(d_kfl), Buf(d_vtl), Buf(d_k2l)
    bkfa, bvta, bk2a = Buf(d_kfa), Buf(d_vta), Buf(d_k2a)
    bqf = Buf(d_qf)
    bwgl, bwul, bwdl = Buf(d_wgl), Buf(d_wul), Buf(d_wdl)
    bwga, bwua, bwda = Buf(d_wga), Buf(d_wua), Buf(d_wda)
    bqn, bknl, bv1l = Buf(d_qn), Buf(d_knl), Buf(d_v1l)
    bkna, bv1a = Buf(d_kna), Buf(d_v1a)
    bscrk2, bscrc = Buf(d_scrk2), Buf(d_scrc)
    bx1s = Buf(d_x1s)


    cops = []
    cbufs = {}

    def make_construct():
        idxb = [p.sbuf(f"C_idx{i}", [128, OPC], I32) for i in range(2)]
        accb = [p.sbuf(f"C_acc{i}", [128, OPC], BF16) for i in range(2)]
        term = [p.sbuf(f"C_term{i}", [128, OPC], BF16) for i in range(2)]
        cbufs.update(idxb=idxb, accb=accb, term=term)

        def construct(d_idx, bout, d_out, nrow, ncol):
            nslab = (nrow + 127) // 128
            for s in range(nslab):
                r0, r1 = s * 128, min((s + 1) * 128, nrow)
                ib, ab, tb = idxb[s % 2], accb[s % 2], term[s % 2]
                np_ = r1 - r0
                cops.append(lambda d_idx=d_idx, r0=r0, r1=r1, ib=ib, ncol=ncol:
                            p.op("gpsimd", lambda e: e.dma_start(
                                ib[0:r1 - r0, 0:ncol], d_idx[r0:r1, :]),
                                outs=[ib], dma=True))
                for j in range(16):
                    if j == 0:
                        cops.append(lambda ib=ib, ab=ab, np_=np_, ncol=ncol:
                                    p.op("vector", lambda e: e.tensor_scalar(
                                        ab[0:np_, 0:ncol], ib[0:np_, 0:ncol], 0.0,
                                        lut[0:np_, 0:1], ALU.is_equal, ALU.mult),
                                        ins=[ib, lut], outs=[ab]))
                    else:
                        cops.append(lambda j=j, ib=ib, tb=tb, np_=np_, ncol=ncol:
                                    p.op("vector", lambda e: e.tensor_scalar(
                                        tb[0:np_, 0:ncol], ib[0:np_, 0:ncol], float(j),
                                        lut[0:np_, j:j + 1], ALU.is_equal, ALU.mult),
                                        ins=[ib, lut], outs=[tb]))
                        cops.append(lambda ab=ab, tb=tb, np_=np_, ncol=ncol:
                                    p.op("vector", lambda e: e.tensor_add(
                                        ab[0:np_, 0:ncol], ab[0:np_, 0:ncol],
                                        tb[0:np_, 0:ncol]), ins=[tb, ab], outs=[ab]))
                cops.append(lambda d_out=d_out, bout=bout, r0=r0, r1=r1, ab=ab,
                            ncol=ncol: p.op("gpsimd", lambda e: e.dma_start(
                                d_out[r0:r1, :], ab[0:r1 - r0, 0:ncol]),
                                ins=[ab], outs=[bout], dma=True))

        construct(d_gi, bwgl, d_wgl, D, OPC)
        construct(d_ui, bwul, d_wul, D, OPC)
        construct(d_di, bwdl, d_wdl, INTER, 256)

    def drain_cops(n):
        k = 0
        while cops and k < n:
            cops.pop(0)()
            k += 1

    def _s1():
        with p.scope():
            xc = [p.sbuf(f"A_x{c}", [128, TPC], F32) for c in range(NC)]
            cq = p.sbuf("A_cq", [128, TPC], F32)
            sq_t = p.sbuf("A_sq", [128, TPC], F32)
            ck = p.sbuf("A_ck", [128, TPC], F32)
            sk = p.sbuf("A_sk", [128, TPC], F32)
            rowA = p.sbuf("A_rowA", [1, TPC], F32)
            rowB = p.sbuf("A_rowB", [1, TPC], F32)
            rowC = p.sbuf("A_rowC", [1, TPC], F32)
            tmpc = p.sbuf("A_tmpc", [128, TPC], F32)
            wslab = [p.sbuf(f"A_wslab{i}", [128, NC * 128], BF16) for i in range(2)]
            vslab = [p.sbuf(f"A_vslab{i}", [128, NC * 256], BF16) for i in range(2)]
            qkh = [p.sbuf(f"A_qkh{i}", [128, TPC], BF16) for i in range(2)]
            ropetmp = p.sbuf("A_ropetmp", [128, 512], F32)
            ropecos = p.sbuf("A_ropecos", [128, 512], F32)
            vtt = [p.sbuf(f"A_vtt{i}", [128, 512], BF16) for i in range(2)]
            k2sq = p.sbuf("A_k2sq", [128, TPC], BF16)
            k2rowA = p.sbuf("A_k2row", [1, NH], F32)

            T1 = p.psum("A_T1", [128, TPC])
            T2 = p.psum("A_T2", [128, TPC])
            T3 = p.psum("A_T3", [128, 512])
            T4 = p.psum("A_T4", [128, 512])
            ps_qk = [T3, T4]
            ps_v = [T3, T4]

            for c in range(NC):
                p.op("sync", lambda e, c=c: e.dma_start(xc[c].ap, d_x[c * 128:(c + 1) * 128, :]),
                     outs=[xc[c]], dma=True)
            for b, d in ((cq, d_cq), (sq_t, d_sq), (ck, d_ck), (sk, d_sk)):
                p.op("sync", lambda e, b=b, d=d: e.dma_start(b.ap, d[:]), outs=[b], dma=True)

            sqc = tmpc
            for c in range(NC):
                p.op("scalar", lambda e, c=c: e.activation(sqc.ap, xc[c].ap, AF.Square),
                     ins=[xc[c]], outs=[sqc])
                def mm_sum(e, c=c):
                    last = None
                    for tb in range(2):
                        last = e.matmul(T1[0:1, tb * 512:(tb + 1) * 512], onecf.ap,
                                        xc[c][:, tb * 512:(tb + 1) * 512],
                                        start=(c == 0), stop=(c == NC - 1))
                    return last
                p.op("tensor", mm_sum, ins=[onecf, xc[c]], outs=[T1])
                def mm_sq(e, c=c):
                    last = None
                    for tb in range(2):
                        last = e.matmul(T2[0:1, tb * 512:(tb + 1) * 512], onecf.ap,
                                        sqc[:, tb * 512:(tb + 1) * 512],
                                        start=(c == 0), stop=(c == NC - 1))
                    return last
                p.op("tensor", mm_sq, ins=[onecf, sqc], outs=[T2])
            p.op("scalar", lambda e: e.activation(rowA.ap, T1[0:1, :], AF.Copy, scale=1.0 / D),
                 ins=[T1], outs=[rowA])
            p.op("scalar", lambda e: e.activation(rowB.ap, T2[0:1, :], AF.Copy, scale=1.0 / D),
                 ins=[T2], outs=[rowB])
            p.op("scalar", lambda e: e.activation(rowC.ap, rowA.ap, AF.Square),
                 ins=[rowA], outs=[rowC])
            p.op("vector", lambda e: e.tensor_sub(rowB.ap, rowB.ap, rowC.ap),
                 ins=[rowB, rowC], outs=[rowB])
            p.op("vector", lambda e: e.tensor_scalar_add(rowB.ap, rowB.ap, 1e-5),
                 ins=[rowB], outs=[rowB])
            p.op("scalar", lambda e: e.activation(rowC.ap, rowB.ap, AF.Sqrt),
                 ins=[rowB], outs=[rowC])
            p.op("vector", lambda e: e.reciprocal(rowB.ap, rowC.ap), ins=[rowC], outs=[rowB])
            p.op("vector", lambda e: e.tensor_mul(rowC.ap, rowA.ap, rowB.ap),
                 ins=[rowA, rowB], outs=[rowC])
            p.op("scalar", lambda e: e.activation(rowC.ap, rowC.ap, AF.Copy, scale=-1.0),
                 ins=[rowC], outs=[rowC])
            def bc_r(e):
                last = None
                for tb in range(2):
                    last = e.matmul(T1[:, tb * 512:(tb + 1) * 512], onerf.ap,
                                    rowB[0:1, tb * 512:(tb + 1) * 512], start=True, stop=True)
                return last
            p.op("tensor", bc_r, ins=[onerf, rowB], outs=[T1])
            def bc_m(e):
                last = None
                for tb in range(2):
                    last = e.matmul(T2[:, tb * 512:(tb + 1) * 512], onerf.ap,
                                    rowC[0:1, tb * 512:(tb + 1) * 512], start=True, stop=True)
                return last
            p.op("tensor", bc_m, ins=[onerf, rowC], outs=[T2])
            for c in range(NC):
                p.op("vector", lambda e, c=c: e.tensor_mul(tmpc.ap, xc[c].ap, T1.ap),
                     ins=[xc[c], T1], outs=[tmpc])
                p.op("vector", lambda e, c=c: e.tensor_add(xnc[c].ap, tmpc.ap, T2.ap),
                     ins=[tmpc, T2], outs=[xnc[c]])

            def emit_qk(off_e, cos_t, sin_t, d_out, bout, do_k2):
                for h in range(NH):
                    sl = wslab[h % 2]
                    hb = qkh[h % 2]
                    p.op("gpsimd", lambda e, h=h, sl=sl: e.dma_start(
                        sl.rearrange("p (dc e) -> p dc e", dc=NC),
                        d_w.rearrange("(dc p) e -> p dc e", p=128)[
                            :, :, off_e + h * 128: off_e + (h + 1) * 128]),
                        outs=[sl], dma=True)
                    for tb in range(2):
                        ps = ps_qk[tb]
                        def mm(e, sl=sl, tb=tb, ps=ps):
                            last = None
                            for dc in range(NC):
                                last = e.matmul(ps.ap, sl[:, dc * 128:(dc + 1) * 128],
                                                xnc[dc][:, tb * 512:(tb + 1) * 512],
                                                start=(dc == 0), stop=(dc == NC - 1))
                            return last
                        p.op("tensor", mm, ins=[sl] + xnc, outs=[ps])
                        ts = slice(tb * 512, (tb + 1) * 512)
                        p.op("vector", lambda e, ps=ps, ts=ts: e.tensor_mul(
                            ropetmp[0:64, :], ps[64:128, :], sin_t[0:64, ts]),
                            ins=[ps, sin_t], outs=[ropetmp])
                        p.op("vector", lambda e, ps=ps, ts=ts: e.tensor_mul(
                            ropetmp[64:128, :], ps[0:64, :], sin_t[64:128, ts]),
                            ins=[ps, sin_t], outs=[ropetmp])
                        p.op("vector", lambda e, ps=ps, ts=ts: e.tensor_mul(
                            ropecos.ap, ps.ap, cos_t[:, ts]),
                            ins=[ps, cos_t], outs=[ropecos])
                        p.op("vector", lambda e, hb=hb, ts=ts: e.tensor_add(
                            hb[:, ts], ropecos.ap, ropetmp.ap),
                            ins=[ropecos, ropetmp], outs=[hb])
                    p.op("gpsimd", lambda e, h=h, hb=hb, d_out=d_out: e.dma_start(
                        d_out[h * 128:(h + 1) * 128, :], hb.ap),
                        ins=[hb], outs=[bout], dma=True)
                    if do_k2:
                        p.op("scalar", lambda e, hb=hb: e.activation(k2sq.ap, hb.ap, AF.Square),
                             ins=[hb], outs=[k2sq])
                        def mmk2(e):
                            last = None
                            for tb in range(2):
                                last = e.matmul(T2[0:1, tb * 512:(tb + 1) * 512], onecb.ap,
                                                k2sq[:, tb * 512:(tb + 1) * 512],
                                                start=True, stop=True)
                            return last
                        p.op("tensor", mmk2, ins=[onecb, k2sq], outs=[T2])
                        p.op("vector", lambda e, h=h: e.reduce_max(
                            k2rowA[0:1, h:h + 1], T2[0:1, :], AX.X),
                            ins=[T2], outs=[k2rowA])

            emit_qk(0, cq, sq_t, d_qf, bqf, False)
            emit_qk(D, ck, sk, d_kfl, bkfl, True)
            p.op("gpsimd", lambda e: e.dma_start(d_k2l.rearrange("a b -> b a"), k2rowA.ap),
                 ins=[k2rowA], outs=[bk2l], dma=True)

            for eb in range(8):
                sl = vslab[eb % 2]
                p.op("gpsimd", lambda e, eb=eb, sl=sl: e.dma_start(
                    sl.rearrange("p (dc e) -> p dc e", dc=NC),
                    d_w.rearrange("(dc p) e -> p dc e", p=128)[
                        :, :, 2 * D + eb * 256: 2 * D + (eb + 1) * 256]),
                    outs=[sl], dma=True)
                for tt in range(TPC // 128):
                    ps = ps_v[tt % 2]
                    def mmv(e, tt=tt, sl=sl, ps=ps):
                        last = None
                        for dc in range(NC):
                            last = e.matmul(ps[:, 0:256], xnc[dc][:, tt * 128:(tt + 1) * 128],
                                            sl[:, dc * 256:(dc + 1) * 256],
                                            start=(dc == 0), stop=(dc == NC - 1))
                        return last
                    p.op("tensor", mmv, ins=[sl] + xnc, outs=[ps])
                    vt_b = vtt[tt % 2]
                    p.op("scalar", lambda e, ps=ps, vt_b=vt_b: e.activation(
                        vt_b[:, 0:256], ps[:, 0:256], AF.Copy), ins=[ps], outs=[vt_b])
                    p.op("gpsimd", lambda e, tt=tt, eb=eb, vt_b=vt_b: e.dma_start(
                        d_vtl[tt * 128:(tt + 1) * 128, eb * 256:(eb + 1) * 256],
                        vt_b[:, 0:256]), ins=[vt_b], outs=[bvtl], dma=True)


    _s1()

    def ag(din, dout, bin_, bout):
        p.cc(lambda e, din=din, dout=dout: e.collective_compute(
            "AllGather", ALU.bypass, replica_groups=[list(range(NCORE))],
            ins=[din.ap()], outs=[dout.ap()]), ins=[bin_], outs=[bout])

    ag(d_kfl, d_kfa, bkfl, bkfa)
    ag(d_vtl, d_vta, bvtl, bvta)
    ag(d_k2l, d_k2a, bk2l, bk2a)

    def _s2():
        with p.scope():
            make_construct()
            mskslab = p.sbuf("B_msk", [128, 64 * 512], BF16)
            ksb = p.sbuf("B_ksb", [128, SEQ], BF16)
            vsb = p.sbuf("B_vsb", [128, SEQ], BF16)
            qh_sb = [p.sbuf(f"B_qh{i}", [128, TPC], BF16) for i in range(2)]
            ysb = p.sbuf("B_ysb", [128, NH * 512], BF16)
            k2sb = p.sbuf("B_k2sb", [NH, NCORE], F32)
            k2colB = p.sbuf("B_k2col", [NH, 1], F32)
            k2row = p.sbuf("B_k2row", [1, NH], F32)
            sqh = p.sbuf("B_sqh", [128, 512], BF16)
            b2sc = p.sbuf("B_b2sc", [1, 1], F32)
            bscb = p.sbuf("B_bscb", [1, 1], BF16)
            negb = p.sbuf("B_negbc", [128, 1], F32)
            p_sb = [p.sbuf(f"B_p{i}", [128, 1024], BF16) for i in range(2)]
            l_accW = p.sbuf("B_laccW", [128, 1024], F32)
            lrow = p.sbuf("B_lrow", [1, 512], F32)
            linvb = p.sbuf("B_linvb", [1, 512], BF16)
            lb_sb = p.sbuf("B_lb", [128, 512], BF16)

            ps_s = [p.psum(f"B_ps{i}", [128, 1024]) for i in range(2)]
            ps_nb = p.psum("B_pnb", [128, 1])
            ps_o = p.psum("B_po", [128, 512])
            ps_q2 = p.psum("B_pq2", [1, TPC])
            ps_l = ps_q2

            p.op("sync", lambda e: e.dma_start(
                k2sb.ap, d_k2a.rearrange("(r h) c -> h (r c)", r=NCORE),
                allow_slow_non_contiguous=True), ins=[bk2a], outs=[k2sb], dma=True)
            p.op("vector", lambda e: e.reduce_max(k2colB.ap, k2sb.ap, AX.X),
                 ins=[k2sb], outs=[k2colB])
            p.op("gpsimd", lambda e: e.dma_start(d_scrk2.rearrange("a b -> b a"), k2colB.ap),
                 ins=[k2colB], outs=[bscrk2], dma=True)
            p.op("gpsimd", lambda e: e.dma_start(k2row.ap, d_scrk2[:]),
                 ins=[bscrk2], outs=[k2row], dma=True)

            kfa_v = d_kfa.rearrange("(r c) t -> c r t", r=NCORE)
            for qb in range(2):
                p.op("scalar", lambda e, qb=qb: e.dma_start(
                    mskslab.rearrange("p (kt f) -> p kt f", kt=64),
                    d_msk[qb, :, :, :].rearrange("kt p f -> p kt f")),
                    outs=[mskslab], dma=True)
                for h in range(NH):
                    qh = qh_sb[h % 2]
                    p.op("gpsimd", lambda e, h=h, qh=qh: e.dma_start(
                        qh.ap, d_qf[h * 128:(h + 1) * 128, :]), ins=[bqf], outs=[qh], dma=True)
                    for tb2 in range(2):
                        p.op("scalar", lambda e, qh=qh, tb2=tb2: e.activation(
                            sqh.ap, qh[:, tb2 * 512:(tb2 + 1) * 512], AF.Square),
                            ins=[qh], outs=[sqh])
                        p.op("tensor", lambda e, tb2=tb2: e.matmul(
                            ps_q2[0:1, tb2 * 512:(tb2 + 1) * 512], onecb.ap, sqh.ap,
                            start=True, stop=True), ins=[onecb, sqh], outs=[ps_q2])
                    p.op("vector", lambda e: e.reduce_max(b2sc.ap, ps_q2[0:1, :], AX.X),
                         ins=[ps_q2], outs=[b2sc])
                    p.op("vector", lambda e, h=h: e.tensor_scalar(
                        b2sc.ap, b2sc.ap, k2row[0:1, h:h + 1], 1.0404,
                        ALU.mult, ALU.mult), ins=[b2sc, k2row], outs=[b2sc])
                    p.op("scalar", lambda e: e.activation(bscb.ap, b2sc.ap, AF.Sqrt),
                         ins=[b2sc], outs=[bscb])
                    p.op("tensor", lambda e: e.matmul(ps_nb.ap, onerb.ap, bscb.ap,
                                                      start=True, stop=True),
                         ins=[onerb, bscb], outs=[ps_nb])
                    p.op("scalar", lambda e: e.activation(negb.ap, ps_nb.ap, AF.Copy,
                                                          scale=-1.0), ins=[ps_nb], outs=[negb])
                    p.op("sync", lambda e, h=h: e.dma_start(
                        ksb.rearrange("p (r t) -> p r t", r=NCORE),
                        kfa_v[h * 128:(h + 1) * 128, :, :]), ins=[bkfa], outs=[ksb], dma=True)
                    p.op("sync", lambda e, h=h: e.dma_start(
                        vsb.rearrange("p (kt d) -> p kt d", kt=64),
                        d_vta.rearrange("(kt p) d -> p kt d", p=128)[
                            :, :, h * 128:(h + 1) * 128]), ins=[bvta], outs=[vsb], dma=True)
                    def emit_pv(ktp, pb):
                        for half in range(2):
                            kt = 2 * ktp + half
                            p.op("tensor", lambda e, kt=kt, pb=pb, half=half: e.matmul(
                                ps_o.ap, vsb[:, kt * 128:(kt + 1) * 128],
                                pb[:, half * 512:(half + 1) * 512],
                                start=(kt == 0), stop=(kt == 63)), ins=[vsb, pb],
                                outs=[ps_o])
                        if ktp == 0:
                            p.op("vector", lambda e, pb=pb: e.tensor_copy(l_accW.ap, pb.ap),
                                 ins=[pb], outs=[l_accW])
                        else:
                            p.op("vector", lambda e, pb=pb: e.tensor_add(
                                l_accW.ap, l_accW.ap, pb.ap), ins=[pb], outs=[l_accW])
                    pend_exp = None
                    pend_pv = []
                    for ktp in range(32):
                        ps = ps_s[ktp % 2]
                        pb = p_sb[ktp % 2]
                        def mms(e, qb=qb, ktp=ktp, ps=ps, qh=qh):
                            for half in range(2):
                                kt = 2 * ktp + half
                                e.matmul(ps[:, half * 512:(half + 1) * 512], idb.ap,
                                         mskslab[:, kt * 512:(kt + 1) * 512],
                                         start=True, stop=False)
                                last = e.matmul(ps[:, half * 512:(half + 1) * 512],
                                                ksb[:, kt * 128:(kt + 1) * 128],
                                                qh[:, qb * 512:(qb + 1) * 512],
                                                start=False, stop=True)
                            return last
                        p.op("tensor", mms, ins=[idb, mskslab, ksb, qh],
                             outs=[ps])
                        if pend_exp is not None:
                            eps, epb, ektp_prev = pend_exp
                            p.op("scalar", lambda e, eps=eps, epb=epb: e.activation(
                                epb.ap, eps.ap, AF.Exp, bias=negb.ap),
                                ins=[eps, negb], outs=[epb])
                            pend_pv.append((ektp_prev, epb))
                        if len(pend_pv) > 1:
                            emit_pv(*pend_pv.pop(0))
                        pend_exp = (ps, pb, ktp)
                    eps, epb, ektp = pend_exp
                    p.op("scalar", lambda e, eps=eps, epb=epb: e.activation(
                        epb.ap, eps.ap, AF.Exp, bias=negb.ap), ins=[eps, negb], outs=[epb])
                    pend_pv.append((ektp, epb))
                    for it in pend_pv:
                        emit_pv(*it)
                    p.op("vector", lambda e: e.tensor_add(
                        l_accW[:, 0:512], l_accW[:, 0:512], l_accW[:, 512:1024]),
                        ins=[l_accW], outs=[l_accW])
                    p.op("tensor", lambda e: e.matmul(ps_l[0:1, 0:512], onecf.ap,
                                                      l_accW[:, 0:512],
                                                      start=True, stop=True),
                         ins=[onecf, l_accW], outs=[ps_l])
                    p.op("scalar", lambda e: e.activation(lrow.ap, ps_l[0:1, 0:512], AF.Copy),
                         ins=[ps_l], outs=[lrow])
                    p.op("vector", lambda e: e.reciprocal(lrow.ap, lrow.ap),
                         ins=[lrow], outs=[lrow])
                    p.op("scalar", lambda e: e.activation(linvb.ap, lrow.ap, AF.Copy),
                         ins=[lrow], outs=[linvb])
                    p.op("tensor", lambda e: e.matmul(ps_s[0][:, 0:512], onerb.ap, linvb.ap,
                                                      start=True, stop=True),
                         ins=[onerb, linvb], outs=[ps_s[0]])
                    p.op("scalar", lambda e: e.activation(lb_sb.ap, ps_s[0][:, 0:512], AF.Copy),
                         ins=[ps_s[0]], outs=[lb_sb])
                    p.op("vector", lambda e, h=h: e.tensor_mul(
                        ysb[:, h * 512:(h + 1) * 512], ps_o.ap, lb_sb.ap),
                        ins=[ps_o, lb_sb], outs=[ysb])
                    drain_cops(75)
                for eo in range(NC):
                    off = (eo % 2) * 2048
                    p.op("gpsimd", lambda e, eo=eo, off=off: e.dma_start(
                        ksb[:, off:off + 2048].rearrange("p (dc e) -> p dc e", dc=NC),
                        d_wo.rearrange("(dc p) e -> p dc e", p=128)[
                            :, :, eo * 128:(eo + 1) * 128]), outs=[ksb], dma=True)
                    ps = ps_s[eo % 2]
                    def mmo(e, off=off, ps=ps):
                        last = None
                        for dc in range(NC):
                            last = e.matmul(ps[:, 0:512],
                                            ksb[:, off + dc * 128:off + (dc + 1) * 128],
                                            ysb[:, dc * 512:(dc + 1) * 512],
                                            start=(dc == 0), stop=(dc == NC - 1))
                        return last
                    p.op("tensor", mmo, ins=[ksb, ysb], outs=[ps])
                    p.op("scalar", lambda e, eo=eo, qb=qb, ps=ps: e.activation(
                        s_res[:, eo * TPC + qb * 512: eo * TPC + (qb + 1) * 512],
                        ps[:, 0:512], AF.Copy), ins=[ps], outs=[s_res])
            drain_cops(10 ** 9)


    _s2()

    ag(d_wgl, d_wga, bwgl, bwga)
    ag(d_wul, d_wua, bwul, bwua)
    ag(d_wdl, d_wda, bwdl, bwda)

    x1_scope = p.scope()
    x1_scope.__enter__()
    x1 = p.sbuf("P_x1", [128, NC * TPC], F32)

    def _s3():
        with p.scope():
            hbuf = p.sbuf("G_h", [128, NOC * 512], BF16)
            gslab = [p.sbuf(f"G_gs{i}", [128, NC * 128], BF16) for i in range(2)]
            uslab = [p.sbuf(f"G_us{i}", [128, NC * 128], BF16) for i in range(2)]
            dslab = [p.sbuf("G_ds0", [128, NOC * 128], BF16)] * 2
            gsil = p.sbuf("G_gsil", [128, 512], BF16)
            gcb = p.sbuf("G_gcb", [128, 512], BF16)
            tmpb = p.sbuf("G_tmpb", [128, 512], BF16)
            xcb = [p.sbuf(f"G_xcb{i}", [128, 512], F32) for i in range(2)]
            ps_g = [p.psum(f"G_pg{i}", [128, 512]) for i in range(2)]
            ps_u = [p.psum(f"G_pu{i}", [128, 512]) for i in range(2)]
            ps_d = [p.psum(f"G_pd{i}", [128, 512]) for i in range(2)]

            def ochunk_pieces(oc):
                lo, hi = oc * 128, (oc + 1) * 128
                out = []
                r0, r1 = lo // OPC, (hi - 1) // OPC
                for r in range(r0, r1 + 1):
                    a = max(lo, r * OPC) - r * OPC
                    b = min(hi, (r + 1) * OPC) - r * OPC
                    out.append((r, a, b, max(lo, r * OPC) - lo))
                return out

            wga_v = d_wga.rearrange("(r dc p) o -> p r dc o", r=NCORE, p=128)
            wua_v = d_wua.rearrange("(r dc p) o -> p r dc o", r=NCORE, p=128)
            wda_v = d_wda.rearrange("(r ic p) h -> p r ic h", r=NCORE, p=128)

            for tb in range(2):
                ts = slice(tb * 512, (tb + 1) * 512)
                for oc in range(NOC):
                    g_sl, u_sl = gslab[oc % 2], uslab[oc % 2]
                    pcs = ochunk_pieces(oc)

                    def mk_ld(sl, src_v, pcs=pcs):
                        def f(e):
                            out = []
                            for (r, a, b, off) in pcs:
                                out.append(e.dma_start(
                                    sl.rearrange("p (dc o) -> p dc o", dc=NC)[
                                        :, :, off:off + b - a],
                                    src_v[:, r, :, a:b]))
                            return out
                        f.n_dma = len(pcs)
                        return f
                    p.op("sync", mk_ld(g_sl, wga_v), ins=[bwga], outs=[g_sl], dma=True)
                    p.op("gpsimd", mk_ld(u_sl, wua_v), ins=[bwua], outs=[u_sl], dma=True)
                    pg, pu = ps_g[oc % 2], ps_u[oc % 2]
                    def mmg(e, g_sl=g_sl, ts=ts, pg=pg):
                        last = None
                        for dc in range(NC):
                            last = e.matmul(pg.ap, g_sl[:, dc * 128:(dc + 1) * 128],
                                            xnc[dc][:, ts], start=(dc == 0), stop=(dc == NC - 1))
                        return last
                    p.op("tensor", mmg, ins=[g_sl] + xnc, outs=[pg])
                    def mmu(e, u_sl=u_sl, ts=ts, pu=pu):
                        last = None
                        for dc in range(NC):
                            last = e.matmul(pu.ap, u_sl[:, dc * 128:(dc + 1) * 128],
                                            xnc[dc][:, ts], start=(dc == 0), stop=(dc == NC - 1))
                        return last
                    p.op("tensor", mmu, ins=[u_sl] + xnc, outs=[pu])
                    p.op("scalar", lambda e, pg=pg: e.activation(gsil.ap, pg.ap, AF.Silu),
                         ins=[pg], outs=[gsil])
                    p.op("vector", lambda e, oc=oc, pu=pu: e.tensor_mul(
                        hbuf[:, oc * 512:(oc + 1) * 512], gsil.ap, pu.ap),
                        ins=[gsil, pu], outs=[hbuf])
                for hc in range(NC):
                    d_sl = dslab[hc % 2]
                    rr, h0 = hc // 2, (hc % 2) * 128
                    p.op("sync", lambda e, d_sl=d_sl, rr=rr, h0=h0: e.dma_start(
                        d_sl.rearrange("p (ic h) -> p ic h", ic=NOC),
                        wda_v[:, rr, :, h0:h0 + 128]), ins=[bwda], outs=[d_sl], dma=True)
                    pd = ps_d[hc % 2]
                    def mmd(e, d_sl=d_sl, pd=pd):
                        last = None
                        for ic in range(NOC):
                            last = e.matmul(pd.ap, d_sl[:, ic * 128:(ic + 1) * 128],
                                            hbuf[:, ic * 512:(ic + 1) * 512],
                                            start=(ic == 0), stop=(ic == NOC - 1))
                        return last
                    p.op("tensor", mmd, ins=[d_sl, hbuf], outs=[pd])
                    p.op("scalar", lambda e, pd=pd: e.activation(gcb.ap, pd.ap, AF.Silu),
                         ins=[pd], outs=[gcb])
                    hs = slice(hc * 128, (hc + 1) * 128)
                    xb = xcb[hc % 2]
                    p.op("sync", lambda e, hs=hs, ts=ts, xb=xb: e.dma_start(
                        xb.ap, d_x[hs, ts]), outs=[xb], dma=True)
                    p.op("vector", lambda e, hc=hc, ts=ts: e.tensor_mul(
                        tmpb.ap, gcb.ap, s_res[:, hc * TPC + tb * 512:hc * TPC + (tb + 1) * 512]),
                        ins=[gcb, s_res], outs=[tmpb])
                    p.op("vector", lambda e, hc=hc, tb=tb, xb=xb: e.tensor_add(
                        x1[:, hc * TPC + tb * 512: hc * TPC + (tb + 1) * 512],
                        xb.ap, tmpb.ap), ins=[tmpb, xb], outs=[x1])
                    p.op("sync", lambda e, hc=hc, tb=tb, hs=hs, ts=ts: e.dma_start(
                        d_x1s[hs, ts],
                        x1[:, hc * TPC + tb * 512: hc * TPC + (tb + 1) * 512]),
                        ins=[x1], outs=[bx1s], dma=True)


    _s3()

    def _s4():
        with p.scope():
            rowA = p.sbuf("E_rowA", [1, TPC], F32)
            rowB = p.sbuf("E_rowB", [1, TPC], F32)
            rowC = p.sbuf("E_rowC", [1, TPC], F32)
            tmpc = p.sbuf("E_tmpc", [128, TPC], F32)
            wslab = [p.sbuf(f"E_wslab{i}", [128, NC * 128], BF16) for i in range(2)]
            vslab = [p.sbuf(f"E_vslab{i}", [128, NC * 512], BF16) for i in range(2)]
            qkh = [p.sbuf(f"E_qkh{i}", [128, TPC], BF16) for i in range(2)]
            sqh = p.sbuf("E_sqh", [128, 512], BF16)
            r512 = p.sbuf("E_r512", [1, 512], F32)
            rbf = p.sbuf("E_rbf", [1, 512], BF16)
            rb_sb = p.sbuf("E_rb", [128, 512], BF16)
            vtt = [p.sbuf(f"E_vtt{i}", [128, 512], BF16) for i in range(2)]
            T1 = p.psum("E_T1", [128, TPC])
            T2 = p.psum("E_T2", [128, TPC])
            T3 = p.psum("E_T3", [128, 512])
            T4 = p.psum("E_T4", [128, 512])
            ps_qk = [T3, T4]
            ps_v = [T3, T4]

            def x1s(c):
                return x1[:, c * TPC:(c + 1) * TPC]

            sqc = tmpc
            for c in range(NC):
                p.op("scalar", lambda e, c=c: e.activation(sqc.ap, x1s(c), AF.Square),
                     ins=[x1], outs=[sqc])
                def mm_sum(e, c=c):
                    last = None
                    for tb in range(2):
                        last = e.matmul(T1[0:1, tb * 512:(tb + 1) * 512], onecf.ap,
                                        x1[:, c * TPC + tb * 512: c * TPC + (tb + 1) * 512],
                                        start=(c == 0), stop=(c == NC - 1))
                    return last
                p.op("tensor", mm_sum, ins=[onecf, x1], outs=[T1])
                def mm_sq(e, c=c):
                    last = None
                    for tb in range(2):
                        last = e.matmul(T2[0:1, tb * 512:(tb + 1) * 512], onecf.ap,
                                        sqc[:, tb * 512:(tb + 1) * 512],
                                        start=(c == 0), stop=(c == NC - 1))
                    return last
                p.op("tensor", mm_sq, ins=[onecf, sqc], outs=[T2])
            p.op("scalar", lambda e: e.activation(rowA.ap, T1[0:1, :], AF.Copy, scale=1.0 / D),
                 ins=[T1], outs=[rowA])
            p.op("scalar", lambda e: e.activation(rowB.ap, T2[0:1, :], AF.Copy, scale=1.0 / D),
                 ins=[T2], outs=[rowB])
            p.op("scalar", lambda e: e.activation(rowC.ap, rowA.ap, AF.Square),
                 ins=[rowA], outs=[rowC])
            p.op("vector", lambda e: e.tensor_sub(rowB.ap, rowB.ap, rowC.ap),
                 ins=[rowB, rowC], outs=[rowB])
            p.op("vector", lambda e: e.tensor_scalar_add(rowB.ap, rowB.ap, 1e-5),
                 ins=[rowB], outs=[rowB])
            p.op("scalar", lambda e: e.activation(rowC.ap, rowB.ap, AF.Sqrt),
                 ins=[rowB], outs=[rowC])
            p.op("vector", lambda e: e.reciprocal(rowB.ap, rowC.ap), ins=[rowC], outs=[rowB])
            p.op("vector", lambda e: e.tensor_mul(rowC.ap, rowA.ap, rowB.ap),
                 ins=[rowA, rowB], outs=[rowC])
            p.op("scalar", lambda e: e.activation(rowC.ap, rowC.ap, AF.Copy, scale=-1.0),
                 ins=[rowC], outs=[rowC])
            def bc_r(e):
                last = None
                for tb in range(2):
                    last = e.matmul(T1[:, tb * 512:(tb + 1) * 512], onerf.ap,
                                    rowB[0:1, tb * 512:(tb + 1) * 512], start=True, stop=True)
                return last
            p.op("tensor", bc_r, ins=[onerf, rowB], outs=[T1])
            def bc_m(e):
                last = None
                for tb in range(2):
                    last = e.matmul(T2[:, tb * 512:(tb + 1) * 512], onerf.ap,
                                    rowC[0:1, tb * 512:(tb + 1) * 512], start=True, stop=True)
                return last
            p.op("tensor", bc_m, ins=[onerf, rowC], outs=[T2])
            for c in range(NC):
                p.op("vector", lambda e, c=c: e.tensor_mul(tmpc.ap, x1s(c), T1.ap),
                     ins=[x1, T1], outs=[tmpc])
                p.op("vector", lambda e, c=c: e.tensor_add(xnc[c].ap, tmpc.ap, T2.ap),
                     ins=[tmpc, T2], outs=[xnc[c]])

            def emit_qk1(d_wgt, scol, d_out, bout):
                for h in range(NH):
                    sl = wslab[h % 2]
                    hb = qkh[h % 2]
                    p.op("gpsimd", lambda e, h=h, sl=sl, d_wgt=d_wgt: e.dma_start(
                        sl.rearrange("p (dc e) -> p dc e", dc=NC),
                        d_wgt.rearrange("(dc p) e -> p dc e", p=128)[
                            :, :, h * 128:(h + 1) * 128]), outs=[sl], dma=True)
                    for tb in range(2):
                        ps = ps_qk[tb]
                        def mm(e, sl=sl, tb=tb, ps=ps):
                            last = None
                            for dc in range(NC):
                                last = e.matmul(ps.ap, sl[:, dc * 128:(dc + 1) * 128],
                                                xnc[dc][:, tb * 512:(tb + 1) * 512],
                                                start=(dc == 0), stop=(dc == NC - 1))
                            return last
                        p.op("tensor", mm, ins=[sl] + xnc, outs=[ps])
                        p.op("scalar", lambda e, ps=ps: e.activation(sqh.ap, ps.ap, AF.Square),
                             ins=[ps], outs=[sqh])
                        p.op("tensor", lambda e: e.matmul(T2[0:1, 0:512], onecb.ap, sqh.ap,
                                                          start=True, stop=True),
                             ins=[onecb, sqh], outs=[T2])
                        p.op("scalar", lambda e: e.activation(r512.ap, T2[0:1, 0:512],
                                                              AF.Copy, scale=1.0 / DH),
                             ins=[T2], outs=[r512])
                        p.op("vector", lambda e: e.tensor_scalar_add(r512.ap, r512.ap, 1e-6),
                             ins=[r512], outs=[r512])
                        p.op("scalar", lambda e: e.activation(r512.ap, r512.ap, AF.Sqrt),
                             ins=[r512], outs=[r512])
                        p.op("vector", lambda e: e.reciprocal(r512.ap, r512.ap),
                             ins=[r512], outs=[r512])
                        p.op("scalar", lambda e: e.activation(rbf.ap, r512.ap, AF.Copy),
                             ins=[r512], outs=[rbf])
                        p.op("tensor", lambda e: e.matmul(T1[:, 0:512], onerb.ap, rbf.ap,
                                                          start=True, stop=True),
                             ins=[onerb, rbf], outs=[T1])
                        p.op("scalar", lambda e: e.activation(rb_sb.ap, T1[:, 0:512], AF.Copy),
                             ins=[T1], outs=[rb_sb])
                        ts = slice(tb * 512, (tb + 1) * 512)
                        p.op("vector", lambda e, hb=hb, ts=ts, ps=ps, scol=scol:
                             e.scalar_tensor_tensor(hb[:, ts], ps.ap, scol.ap, rb_sb.ap,
                                                    ALU.mult, ALU.mult),
                             ins=[ps, scol, rb_sb], outs=[hb])
                    p.op("gpsimd", lambda e, h=h, hb=hb, d_out=d_out, bout=bout: e.dma_start(
                        d_out[h * 128:(h + 1) * 128, :], hb.ap), ins=[hb], outs=[bout], dma=True)

            emit_qk1(d_wq1, qscol, d_qn, bqn)
            emit_qk1(d_wk1, kscol, d_knl, bknl)

            for eb in range(4):
                sl = vslab[eb % 2]
                p.op("gpsimd", lambda e, eb=eb, sl=sl: e.dma_start(
                    sl.rearrange("p (dc e) -> p dc e", dc=NC),
                    d_wv1.rearrange("(dc p) e -> p dc e", p=128)[
                        :, :, eb * 512:(eb + 1) * 512]), outs=[sl], dma=True)
                for tt in range(TPC // 128):
                    ps = ps_v[tt % 2]
                    def mmv(e, tt=tt, sl=sl, ps=ps):
                        last = None
                        for dc in range(NC):
                            last = e.matmul(ps.ap, xnc[dc][:, tt * 128:(tt + 1) * 128],
                                            sl[:, dc * 512:(dc + 1) * 512],
                                            start=(dc == 0), stop=(dc == NC - 1))
                        return last
                    p.op("tensor", mmv, ins=[sl] + xnc, outs=[ps])
                    vt_b = vtt[tt % 2]
                    p.op("scalar", lambda e, ps=ps, vt_b=vt_b: e.activation(
                        vt_b.ap, ps.ap, AF.Copy), ins=[ps], outs=[vt_b])
                    p.op("gpsimd", lambda e, tt=tt, eb=eb, vt_b=vt_b: e.dma_start(
                        d_v1l[tt * 128:(tt + 1) * 128, eb * 512:(eb + 1) * 512], vt_b.ap),
                        ins=[vt_b], outs=[bv1l], dma=True)


    _s4()

    ag(d_knl, d_kna, bknl, bkna)
    ag(d_v1l, d_v1a, bv1l, bv1a)

    def _s5():
        with p.scope():
            ksb = p.sbuf("F_ksb", [128, SEQ], BF16)
            vsb = p.sbuf("F_vsb", [128, SEQ], BF16)
            qh_sb = [p.sbuf("F_qh0", [128, TPC], BF16)] * 2
            ysb = p.sbuf("F_ysb", [128, NH * TPC], BF16)
            m_all = p.sbuf("F_mall", [128, 16], F32)
            mcol = p.sbuf("F_mcol", [128, 1], F32)
            mrun = p.sbuf("F_mrun", [128, 1], F32)
            msc = p.sbuf("F_msc", [1, 1], F32)
            mscb = p.sbuf("F_mscb", [1, 1], BF16)
            negm = p.sbuf("F_negmc", [128, 1], F32)
            p_sb = [p.sbuf(f"F_p{i}", [128, 1024], BF16) for i in range(2)]
            l_accW = p.sbuf("F_laccW", [128, 1024], F32)
            l_acc = [Buf(l_accW.h), Buf(l_accW.h)]
            lrow = p.sbuf("F_lrow", [1, 512], F32)
            linvb = p.sbuf("F_linvb", [1, 512], BF16)
            ps_s = [p.psum(f"F_ps{i}", [128, 1024]) for i in range(2)]
            ps_o = [p.psum(f"F_po{i}", [128, 512]) for i in range(2)]
            ps_nb = p.psum("F_pnb", [128, 1])
            ps_lt = p.psum("F_plt", [1, 512])

            kna_v = d_kna.rearrange("(r c) t -> c r t", r=NCORE)
            for h in range(NH):
                qh = qh_sb[h % 2]
                p.op("gpsimd", lambda e, h=h, qh=qh: e.dma_start(
                    qh.ap, d_qn[h * 128:(h + 1) * 128, :]), ins=[bqn], outs=[qh], dma=True)
                p.op("sync", lambda e, h=h: e.dma_start(
                    ksb.rearrange("p (r t) -> p r t", r=NCORE),
                    kna_v[h * 128:(h + 1) * 128, :, :]), ins=[bkna], outs=[ksb], dma=True)
                p.op("sync", lambda e, h=h: e.dma_start(
                    vsb.rearrange("p (kt d) -> p kt d", kt=64),
                    d_v1a.rearrange("(kt p) d -> p kt d", p=128)[
                        :, :, h * 128:(h + 1) * 128]), ins=[bv1a], outs=[vsb], dma=True)
                for qt in range(8):
                    q0 = qt * 128
                    for kc in range(16):
                        ps = ps_s[kc % 2]
                        p.op("tensor", lambda e, q0=q0, kc=kc, ps=ps, qh=qh:
                             e.matmul(ps[:, 0:512], qh[:, q0:q0 + 128],
                                      ksb[:, kc * 512:(kc + 1) * 512],
                                      start=True, stop=True), ins=[qh, ksb], outs=[ps])
                        p.op("vector", lambda e, kc=kc, ps=ps: e.reduce_max(
                            m_all[:, kc:kc + 1], ps[:, 0:512], AX.X), ins=[ps], outs=[m_all])
                    p.op("vector", lambda e: e.reduce_max(mcol.ap, m_all.ap, AX.X),
                         ins=[m_all], outs=[mcol])
                    if qt == 0:
                        p.op("vector", lambda e: e.tensor_copy(mrun.ap, mcol.ap),
                             ins=[mcol], outs=[mrun])
                    else:
                        p.op("vector", lambda e: e.tensor_max(mrun.ap, mrun.ap, mcol.ap),
                             ins=[mcol, mrun], outs=[mrun])
                p.op("tensor", lambda e: e.transpose(ps_lt[0:1, 0:128], mrun.ap, idf.ap),
                     ins=[mrun, idf], outs=[ps_lt])
                p.op("scalar", lambda e: e.activation(lrow[0:1, 0:128], ps_lt[0:1, 0:128],
                                                      AF.Copy), ins=[ps_lt], outs=[lrow])
                p.op("vector", lambda e: e.reduce_max(msc.ap, lrow[0:1, 0:128], AX.X),
                     ins=[lrow], outs=[msc])
                p.op("scalar", lambda e: e.activation(mscb.ap, msc.ap, AF.Copy),
                     ins=[msc], outs=[mscb])
                p.op("tensor", lambda e: e.matmul(ps_nb.ap, onerb.ap, mscb.ap,
                                                  start=True, stop=True),
                     ins=[onerb, mscb], outs=[ps_nb])
                p.op("scalar", lambda e: e.activation(negm.ap, ps_nb.ap, AF.Copy,
                                                      scale=-1.0), ins=[ps_nb], outs=[negm])
                def emit_pv(kt, pb):
                    for qb in range(2):
                        p.op("tensor", lambda e, kt=kt, pb=pb, qb=qb: e.matmul(
                            ps_o[qb].ap, vsb[:, kt * 128:(kt + 1) * 128],
                            pb[:, qb * 512:(qb + 1) * 512],
                            start=(kt == 0), stop=(kt == 63)), ins=[vsb, pb],
                            outs=[ps_o[qb]])
                    if kt == 0:
                        p.op("vector", lambda e, pb=pb: e.tensor_copy(l_accW.ap, pb.ap),
                             ins=[pb], outs=[l_accW])
                    else:
                        p.op("vector", lambda e, pb=pb: e.tensor_add(
                            l_accW.ap, l_accW.ap, pb.ap), ins=[pb], outs=[l_accW])
                pend_exp = None
                pend_pv = []
                for kt in range(64):
                    ps = ps_s[kt % 2]
                    pb = p_sb[kt % 2]
                    def mms(e, kt=kt, ps=ps, qh=qh):
                        e.matmul(ps[:, 0:512], ksb[:, kt * 128:(kt + 1) * 128],
                                 qh[:, 0:512], start=True, stop=True)
                        return e.matmul(ps[:, 512:1024], ksb[:, kt * 128:(kt + 1) * 128],
                                        qh[:, 512:1024], start=True, stop=True)
                    p.op("tensor", mms, ins=[ksb, qh], outs=[ps])
                    if pend_exp is not None:
                        eps, epb, ektp_prev = pend_exp
                        p.op("scalar", lambda e, eps=eps, epb=epb: e.activation(
                            epb.ap, eps.ap, AF.Exp, bias=negm.ap),
                            ins=[eps, negm], outs=[epb])
                        pend_pv.append((ektp_prev, epb))
                    if len(pend_pv) > 1:
                        emit_pv(*pend_pv.pop(0))
                    pend_exp = (ps, pb, kt)
                eps, epb, ektp = pend_exp
                p.op("scalar", lambda e, eps=eps, epb=epb: e.activation(
                    epb.ap, eps.ap, AF.Exp, bias=negm.ap), ins=[eps, negm], outs=[epb])
                pend_pv.append((ektp, epb))
                for it in pend_pv:
                    emit_pv(*it)
                for qb in range(2):
                    p.op("tensor", lambda e, qb=qb: e.matmul(
                        ps_lt[0:1, :], onecf.ap, l_accW[:, qb * 512:(qb + 1) * 512],
                        start=True, stop=True), ins=[onecf, l_accW], outs=[ps_lt])
                    p.op("scalar", lambda e: e.activation(lrow.ap, ps_lt[0:1, :], AF.Copy),
                         ins=[ps_lt], outs=[lrow])
                    p.op("vector", lambda e: e.reciprocal(lrow.ap, lrow.ap),
                         ins=[lrow], outs=[lrow])
                    p.op("scalar", lambda e: e.activation(linvb.ap, lrow.ap, AF.Copy),
                         ins=[lrow], outs=[linvb])
                    p.op("tensor", lambda e, qb=qb: e.matmul(
                        ps_s[0][:, 0:512], onerb.ap, linvb.ap, start=True, stop=True),
                         ins=[onerb, linvb], outs=[ps_s[0]])
                    p.op("scalar", lambda e, qb=qb: e.activation(
                        l_accW[:, qb * 512:(qb + 1) * 512], ps_s[0][:, 0:512], AF.Copy),
                        ins=[ps_s[0]], outs=[l_accW])
                    p.op("vector", lambda e, h=h, qb=qb: e.tensor_mul(
                        ysb[:, h * TPC + qb * 512: h * TPC + (qb + 1) * 512],
                        ps_o[qb].ap, l_accW[:, qb * 512:(qb + 1) * 512]),
                        ins=[ps_o[qb], l_accW], outs=[ysb])
            for qb in range(2):
                for eo in range(NC):
                    off = (eo % 2) * 2048
                    p.op("gpsimd", lambda e, eo=eo, off=off: e.dma_start(
                        ksb[:, off:off + 2048].rearrange("p (dc e) -> p dc e", dc=NC),
                        d_wo1.rearrange("(dc p) e -> p dc e", p=128)[
                            :, :, eo * 128:(eo + 1) * 128]), outs=[ksb], dma=True)
                    ps = ps_s[eo % 2]
                    def mmo(e, off=off, ps=ps, qb=qb):
                        last = None
                        for dc in range(NC):
                            last = e.matmul(ps[:, 0:512],
                                            ksb[:, off + dc * 128:off + (dc + 1) * 128],
                                            ysb[:, dc * TPC + qb * 512: dc * TPC + (qb + 1) * 512],
                                            start=(dc == 0), stop=(dc == NC - 1))
                        return last
                    p.op("tensor", mmo, ins=[ksb, ysb], outs=[ps])
                    p.op("scalar", lambda e, eo=eo, qb=qb, ps=ps: e.activation(
                        s_res[:, eo * TPC + qb * 512: eo * TPC + (qb + 1) * 512],
                        ps[:, 0:512], AF.Copy), ins=[ps], outs=[s_res])
    _s5()

    x1_scope.__exit__(None, None, None)

    def _s6():
        with p.scope():
            y_acc2 = [p.sbuf(f"M_yacc{t}", [128, NC * 512], F32) for t in range(2)]
            abuf2 = [p.sbuf(f"M_abuf{t}", [128, 8 * 512], BF16) for t in range(2)]
            gslab = [p.sbuf(f"M_gs{i}", [128, NC * 128], BF16) for i in range(2)]
            uslab = [p.sbuf(f"M_us{i}", [128, NC * 128], BF16) for i in range(2)]
            dslab = [p.sbuf(f"M_ds{i}", [128, 8 * 128], BF16) for i in range(2)]
            gw_sb = p.sbuf("M_gw", [128, NC * NEXP], BF16)
            prob = p.sbuf("M_prob", [128, NEXP], F32)
            mask1 = p.sbuf("M_mask1", [128, NEXP], F32)
            p2 = p.sbuf("M_p2", [128, NEXP], F32)
            comb = p.sbuf("M_comb", [128, NEXP], F32)
            mxc = p.sbuf("M_mxc", [128, 1], F32)
            sumc = p.sbuf("M_sumc", [128, 1], F32)
            m1c = p.sbuf("M_m1c", [128, 1], F32)
            m2c = p.sbuf("M_m2c", [128, 1], F32)
            ct_sb = p.sbuf("M_ct", [NEXP, 128], F32)
            crow = p.sbuf("M_crow", [1, TPC], F32)
            crowb = p.sbuf("M_crowb", [1, TPC], BF16)
            cb_sb2 = [p.sbuf(f"M_cb{t}", [128, 512], BF16) for t in range(2)]
            gsl_ = p.sbuf("M_gsl", [128, 512], BF16)
            gsc = p.sbuf("M_gsc", [128, 512], BF16)
            gfin = p.sbuf("M_gfin", [128, 512], BF16)
            tmpb = p.sbuf("M_tmpb", [128, 512], BF16)
            xcb = [p.sbuf(f"M_xcb{i}", [128, 512], F32) for i in range(2)]
            ps_misc = p.psum("M_pm", [128, 512])
            ps_cb = p.psum("M_pcb", [128, 512])
            ps_g = [p.psum(f"M_pg{i}", [128, 512]) for i in range(2)]
            ps_u = [p.psum(f"M_pu{i}", [128, 512]) for i in range(2)]
            ps_y = [p.psum(f"M_py{i}", [128, 512]) for i in range(2)]

            p.op("gpsimd", lambda e: e.dma_start(
                gw_sb.rearrange("p (dc e) -> p dc e", dc=NC),
                d_gw.rearrange("(dc p) e -> p dc e", p=128)), outs=[gw_sb], dma=True)

            for tt in range(TPC // 128):
                def mmsc(e, tt=tt):
                    last = None
                    for dc in range(NC):
                        last = e.matmul(ps_misc[:, 0:NEXP],
                                        xnc[dc][:, tt * 128:(tt + 1) * 128],
                                        gw_sb[:, dc * NEXP:(dc + 1) * NEXP],
                                        start=(dc == 0), stop=(dc == NC - 1))
                    return last
                p.op("tensor", mmsc, ins=[gw_sb] + xnc, outs=[ps_misc])
                p.op("vector", lambda e: e.reduce_max(mxc.ap, ps_misc[:, 0:NEXP], AX.X),
                     ins=[ps_misc], outs=[mxc])
                p.op("vector", lambda e: e.tensor_scalar_mul(mxc.ap, mxc.ap, -1.0),
                     ins=[mxc], outs=[mxc])
                p.op("scalar", lambda e: e.activation(prob.ap, ps_misc[:, 0:NEXP], AF.Exp,
                                                      bias=mxc.ap, accum_out=sumc.ap),
                     ins=[ps_misc, mxc], outs=[prob, sumc])
                p.op("vector", lambda e: e.reciprocal(sumc.ap, sumc.ap), ins=[sumc], outs=[sumc])
                p.op("vector", lambda e: e.tensor_scalar_mul(prob.ap, prob.ap, sumc.ap),
                     ins=[prob, sumc], outs=[prob])
                p.op("vector", lambda e: e.reduce_max(m1c.ap, prob.ap, AX.X),
                     ins=[prob], outs=[m1c])
                p.op("vector", lambda e: e.tensor_scalar(mask1.ap, prob.ap, m1c.ap, None,
                                                         ALU.is_equal),
                     ins=[prob, m1c], outs=[mask1])
                p.op("vector", lambda e: e.tensor_mul(p2.ap, prob.ap, mask1.ap),
                     ins=[prob, mask1], outs=[p2])
                p.op("vector", lambda e: e.tensor_sub(p2.ap, prob.ap, p2.ap),
                     ins=[prob, p2], outs=[p2])
                p.op("vector", lambda e: e.reduce_max(m2c.ap, p2.ap, AX.X), ins=[p2], outs=[m2c])
                p.op("vector", lambda e: e.tensor_scalar(p2.ap, p2.ap, m2c.ap, None,
                                                         ALU.is_equal), ins=[p2, m2c], outs=[p2])
                p.op("vector", lambda e: e.tensor_add(mask1.ap, mask1.ap, p2.ap),
                     ins=[mask1, p2], outs=[mask1])
                p.op("vector", lambda e: e.tensor_mul(comb.ap, prob.ap, mask1.ap),
                     ins=[prob, mask1], outs=[comb])
                p.op("vector", lambda e: e.tensor_add(m1c.ap, m1c.ap, m2c.ap),
                     ins=[m1c, m2c], outs=[m1c])
                p.op("vector", lambda e: e.tensor_scalar_add(m1c.ap, m1c.ap, 1e-20),
                     ins=[m1c], outs=[m1c])
                p.op("vector", lambda e: e.reciprocal(m1c.ap, m1c.ap), ins=[m1c], outs=[m1c])
                p.op("vector", lambda e: e.tensor_scalar_mul(comb.ap, comb.ap, m1c.ap),
                     ins=[comb, m1c], outs=[comb])
                p.op("tensor", lambda e: e.transpose(ps_misc[0:NEXP, 0:128], comb.ap, idf.ap),
                     ins=[comb, idf], outs=[ps_misc])
                p.op("scalar", lambda e: e.activation(ct_sb.ap, ps_misc[0:NEXP, 0:128],
                                                      AF.Copy), ins=[ps_misc], outs=[ct_sb])
                p.op("gpsimd", lambda e, tt=tt: e.dma_start(
                    d_scrc[:, tt * 128:(tt + 1) * 128], ct_sb.ap),
                    ins=[ct_sb], outs=[bscrc], dma=True)

            for ex in range(NEXP):
                p.op("sync", lambda e, ex=ex: e.dma_start(crow.ap, d_scrc[ex:ex + 1, :]),
                     ins=[bscrc], outs=[crow], dma=True)
                p.op("scalar", lambda e: e.activation(crowb.ap, crow.ap, AF.Copy),
                     ins=[crow], outs=[crowb])
                for tb in range(2):
                    ts = slice(tb * 512, (tb + 1) * 512)
                    p.op("tensor", lambda e, ts=ts: e.matmul(ps_cb.ap, onerb.ap,
                                                             crowb[0:1, ts],
                                                             start=True, stop=True),
                         ins=[onerb, crowb], outs=[ps_cb])
                    p.op("scalar", lambda e, tb=tb: e.activation(cb_sb2[tb].ap, ps_cb.ap,
                                                                 AF.Copy),
                         ins=[ps_cb], outs=[cb_sb2[tb]])
                for ic in range(8):
                    g_sl, u_sl = gslab[ic % 2], uslab[ic % 2]
                    p.op("gpsimd", lambda e, ex=ex, ic=ic, g_sl=g_sl: e.dma_start(
                        g_sl.rearrange("p (dc i) -> p dc i", dc=NC),
                        d_wg1[ex, :, :].rearrange("(dc p) i -> p dc i", p=128)[
                            :, :, ic * 128:(ic + 1) * 128]), outs=[g_sl], dma=True)
                    p.op("gpsimd", lambda e, ex=ex, ic=ic, u_sl=u_sl: e.dma_start(
                        u_sl.rearrange("p (dc i) -> p dc i", dc=NC),
                        d_wu1[ex, :, :].rearrange("(dc p) i -> p dc i", p=128)[
                            :, :, ic * 128:(ic + 1) * 128]), outs=[u_sl], dma=True)
                    for tb in range(2):
                        ts = slice(tb * 512, (tb + 1) * 512)
                        pg, pu = ps_g[tb], ps_u[tb]
                        def mmg(e, g_sl=g_sl, ts=ts, pg=pg):
                            last = None
                            for dc in range(NC):
                                last = e.matmul(pg.ap, g_sl[:, dc * 128:(dc + 1) * 128],
                                                xnc[dc][:, ts], start=(dc == 0),
                                                stop=(dc == NC - 1))
                            return last
                        p.op("tensor", mmg, ins=[g_sl] + xnc, outs=[pg])
                        def mmu(e, u_sl=u_sl, ts=ts, pu=pu):
                            last = None
                            for dc in range(NC):
                                last = e.matmul(pu.ap, u_sl[:, dc * 128:(dc + 1) * 128],
                                                xnc[dc][:, ts], start=(dc == 0),
                                                stop=(dc == NC - 1))
                            return last
                        p.op("tensor", mmu, ins=[u_sl] + xnc, outs=[pu])
                        p.op("scalar", lambda e, pg=pg: e.activation(gsl_.ap, pg.ap, AF.Silu),
                             ins=[pg], outs=[gsl_])
                        p.op("vector", lambda e, tb=tb: e.tensor_mul(gsc.ap, gsl_.ap,
                                                                     cb_sb2[tb].ap),
                             ins=[gsl_, cb_sb2[tb]], outs=[gsc])
                        p.op("vector", lambda e, ic=ic, pu=pu, tb=tb: e.tensor_mul(
                            abuf2[tb][:, ic * 512:(ic + 1) * 512], gsc.ap, pu.ap),
                            ins=[gsc, pu], outs=[abuf2[tb]])
                for hc in range(NC):
                    d_sl = dslab[hc % 2]
                    p.op("gpsimd", lambda e, ex=ex, hc=hc, d_sl=d_sl: e.dma_start(
                        d_sl.rearrange("p (ic h) -> p ic h", ic=8),
                        d_wd1[ex, :, :].rearrange("(ic p) h -> p ic h", p=128)[
                            :, :, hc * 128:(hc + 1) * 128]), outs=[d_sl], dma=True)
                    for tb in range(2):
                        py = ps_y[tb]
                        def mmd(e, d_sl=d_sl, py=py, tb=tb):
                            last = None
                            for ic in range(8):
                                last = e.matmul(py.ap, d_sl[:, ic * 128:(ic + 1) * 128],
                                                abuf2[tb][:, ic * 512:(ic + 1) * 512],
                                                start=(ic == 0), stop=(ic == 7))
                            return last
                        p.op("tensor", mmd, ins=[d_sl, abuf2[tb]], outs=[py])
                        ya = slice(hc * 512, (hc + 1) * 512)
                        if ex == 0:
                            p.op("scalar", lambda e, ya=ya, py=py, tb=tb: e.activation(
                                y_acc2[tb][:, ya], py.ap, AF.Copy),
                                ins=[py], outs=[y_acc2[tb]])
                        else:
                            p.op("vector", lambda e, ya=ya, py=py, tb=tb: e.tensor_add(
                                y_acc2[tb][:, ya], y_acc2[tb][:, ya], py.ap),
                                ins=[py, y_acc2[tb]], outs=[y_acc2[tb]])
            for tb in range(2):
                ts = slice(tb * 512, (tb + 1) * 512)
                for hc in range(NC):
                    ya = slice(hc * 512, (hc + 1) * 512)
                    hs = slice(hc * 128, (hc + 1) * 128)
                    xb = xcb[hc % 2]
                    p.op("scalar", lambda e, ya=ya, tb=tb: e.activation(
                        gfin.ap, y_acc2[tb][:, ya], AF.Silu),
                        ins=[y_acc2[tb]], outs=[gfin])
                    p.op("sync", lambda e, hs=hs, ts=ts, xb=xb: e.dma_start(
                        xb.ap, d_x1s[hs, ts]), ins=[bx1s], outs=[xb], dma=True)
                    p.op("vector", lambda e, hc=hc, tb=tb: e.tensor_mul(
                        tmpb.ap, gfin.ap,
                        s_res[:, hc * TPC + tb * 512: hc * TPC + (tb + 1) * 512]),
                        ins=[gfin, s_res], outs=[tmpb])
                    p.op("vector", lambda e, xb=xb: e.tensor_add(xb.ap, xb.ap, tmpb.ap),
                         ins=[tmpb, xb], outs=[xb])
                    out_toks.append(p.op("gpsimd", lambda e, hs=hs, ts=ts, xb=xb: e.dma_start(
                        d_xo[hs, ts], xb.ap), ins=[xb], dma=True))
    _s6()

    p.finish(out_toks)
    p.build()
    return nc


def run_fused(inputs, h):
    nc = build_fused()
    in_maps = []
    for c in range(NCORE):
        ts = slice(c * TPC, (c + 1) * TPC)
        osl = slice(c * OPC, (c + 1) * OPC)
        dsl = slice(c * 256, (c + 1) * 256)
        in_maps.append(dict(
            xT_c=np.ascontiguousarray(h["xT"][:, ts]),
            wqkvTp=h["wqkvTp"],
            cos_q=np.ascontiguousarray(h["cos_q"][:, ts]),
            sinx_q=np.ascontiguousarray(h["sinx_q"][:, ts]),
            cos_k=np.ascontiguousarray(h["cos_k"][:, ts]),
            sinx_k=np.ascontiguousarray(h["sinx_k"][:, ts]),
            ones_col_f=h["ones_col_f"], ones_row_f=h["ones_row_f"],
            ones_col_bf=h["ones_col_bf"], ones_row_bf=h["ones_row_bf"],
            ident_bf=h["ident_bf"], ident_f=h["ident_f"],
            woT=h["woT"], masks_c=h["masks"][c],
            gidx_sl=np.ascontiguousarray(h["gidxT"][:, osl]),
            uidx_sl=np.ascontiguousarray(h["uidxT"][:, osl]),
            didx_sl=np.ascontiguousarray(h["didxT"][:, dsl]),
            lutrep=h["lutrep"],
            wq1=h["wq1"], wk1=h["wk1"], wv1=h["wv1"], wo1=h["wo1"],
            qscale=h["qscale"], kscale=h["kscale"],
            gwT=h["gwT"], wg1=h["wg1"], wu1=h["wu1"], wd1=h["wd1"],
        ))
    res = run_bass_kernel_spmd(nc, in_maps, core_ids=list(range(NCORE)))
    return res.results



def kernel(**inputs):
    h = host_prep(inputs)
    lg = run_fused(inputs, h)
    xo = np.concatenate([np.asarray(r["xoT_c"], np.float32) for r in lg], axis=1)
    return np.ascontiguousarray(xo.T)[None]


def kernel_multilaunch(**inputs):
    h = host_prep(inputs)
    la = run_l0a(h)
    lb = run_l0b(h)
    lc = run_l0c(h, la)
    ld = run_l0bd(h, la, lc, lb)
    le = run_l1a(h, ld)
    lf = run_l1b(h, le)
    lg = run_l1c(h, ld, le, lf)
    xo = np.concatenate([np.asarray(r["xoT_c"], np.float32) for r in lg], axis=1)
    return np.ascontiguousarray(xo.T)[None]
